# version 04
# baseline (speedup 1.0000x reference)
import math
from contextlib import ExitStack

import numpy as np
import ml_dtypes

import concourse.bass as bass
import concourse.mybir as mybir
from concourse.bass_utils import run_bass_kernel_spmd

F32 = mybir.dt.float32
BF16 = mybir.dt.bfloat16
AF = mybir.ActivationFunctionType
ALU = mybir.AluOpType
AX = mybir.AxisListType
BF = ml_dtypes.bfloat16

D = 1024
H = 16
DH = 64
S = 2048
NEXP = 8
FF = 4096
SEQS = 2
T = SEQS * S
NT = T // 128
NEGBIG = -3.0e38

ENGINES = ("tensor", "vector", "scalar", "gpsimd", "sync")


class Rec:
    __slots__ = ("eng", "idx", "fn", "edges", "signal", "amount", "sig_val", "n")

    def __init__(self, eng, idx, fn, amount, n):
        self.eng = eng
        self.idx = idx
        self.fn = fn
        self.edges = []
        self.signal = False
        self.amount = amount
        self.sig_val = None
        self.n = n


class Builder:
    def __init__(self, nc, stack):
        self.nc = nc
        self.stack = stack
        self.plans = {e: [] for e in ENGINES}
        self.sig = {e: stack.enter_context(nc.semaphore("sig_" + e)) for e in ENGINES}
        self.writer = {}
        self.reader = {}
        self.covered = {e: {f: -1 for f in ENGINES} for e in ENGINES}
        self.last_dma = {}

    def op(self, eng, fn, reads=(), writes=(), after=(), dma=False, name=""):
        plan = self.plans[eng]
        rec = Rec(eng, len(plan), fn, 16 if dma else 1, name)
        if dma:
            rec.signal = True
            prev = self.last_dma.get(eng)
            if prev is not None:
                after = tuple(after) + (prev,)
            self.last_dma[eng] = rec
        raw = []
        other = []
        for t in reads:
            w = self.writer.get(t)
            if w:
                raw.extend(w.values())
        for t in writes:
            w = self.writer.get(t)
            if w:
                other.extend(w.values())
            r = self.reader.get(t)
            if r:
                other.extend(r.values())
        raw.extend(after)
        cov = self.covered[eng]
        for d, is_raw in [(d, True) for d in raw] + [(d, False) for d in other]:
            if d is rec:
                continue
            if d.eng == eng and eng == "tensor" and not dma and d.amount == 1:
                continue
            if d.eng == eng and not is_raw and not dma and d.amount == 1:
                continue
            if d.idx <= cov[d.eng]:
                continue
            d.signal = True
            rec.edges.append(d)
            cov[d.eng] = max(cov[d.eng], d.idx)
        plan.append(rec)
        for t in reads:
            self.reader.setdefault(t, {})[eng] = rec
        for t in writes:
            self.writer.setdefault(t, {})[eng] = rec
            self.reader[t] = {eng: rec}
        return rec

    def barrier(self):
        lasts = [self.plans[e][-1] for e in ENGINES if self.plans[e]]
        for e in ENGINES:
            if self.plans[e]:
                self.op(e, lambda eng: eng.nop(), after=tuple(lasts), name="barrier")

    def finalize_and_emit(self):
        for e in ENGINES:
            tot = 0
            for rec in self.plans[e]:
                if rec.signal:
                    tot += rec.amount
                    rec.sig_val = tot
        nc = self.nc
        sig = self.sig
        with nc.Block() as blk:
            for ename in ENGINES:
                plan = self.plans[ename]
                if not plan:
                    continue

                def make_runner(plan=plan):
                    def run(eng):
                        for rec in plan:
                            if rec.edges:
                                waits = {}
                                for d in rec.edges:
                                    waits[d.eng] = max(waits.get(d.eng, 0), d.sig_val)
                                for e2, v in waits.items():
                                    eng.wait_ge(sig[e2], v)
                            inst = rec.fn(eng)
                            if rec.signal:
                                assert inst is not None, f"signaling rec {rec.n} has no inst"
                                inst.then_inc(sig[rec.eng], rec.amount)
                    return run

                getattr(blk, ename)(make_runner())


def _alibi_slopes(n_heads):
    start = 2 ** (-(2 ** -(math.log2(n_heads) - 3)))
    return [start * start ** i for i in range(n_heads)]


def _make_consts():
    id16 = np.eye(128, dtype=np.float32).astype(BF)
    id32 = np.eye(128, dtype=np.float32)
    j = np.arange(128)
    mneg = np.where(j[None, :] <= j[:, None], 0.0, NEGBIG).astype(np.float32)

    slopes = _alibi_slopes(H)
    ktmpl = np.zeros((128, S), dtype=np.float32)
    qtmpl = np.zeros((128, S), dtype=np.float32)
    khi = (np.arange(S) >> 8).astype(np.float32)
    klo = (np.arange(S) & 255).astype(np.float32)
    qth = (np.arange(S) >> 7).astype(np.float32)
    qtl = (np.arange(S) & 127).astype(np.float32)
    for h in range(H):
        sl = np.float64(slopes[h])
        sa = np.float32(BF(np.float32(sl)))
        sb = np.float32(BF(np.float32(sl) - sa))
        base = 8 * h
        ktmpl[base + 0] = khi
        ktmpl[base + 1] = khi
        ktmpl[base + 2] = klo
        ktmpl[base + 3] = klo
        ktmpl[base + 4] = -sa * 128.0
        ktmpl[base + 5] = -sb * 128.0
        ktmpl[base + 6] = -sa
        ktmpl[base + 7] = -sb
        qtmpl[base + 0] = sa * 256.0
        qtmpl[base + 1] = sb * 256.0
        qtmpl[base + 2] = sa
        qtmpl[base + 3] = sb
        qtmpl[base + 4] = qth
        qtmpl[base + 5] = qth
        qtmpl[base + 6] = qtl
        qtmpl[base + 7] = qtl
    return id16, id32, mneg, ktmpl.astype(BF), qtmpl.astype(BF)


def build_program(debug=False):
    stack = ExitStack()
    nc = bass.Bass("TRN2", target_bir_lowering=False)

    dx = nc.dram_tensor("x", [T, D], F32, kind="ExternalInput")
    dwq = nc.dram_tensor("wqt", [D, D], F32, kind="ExternalInput")
    dwk = nc.dram_tensor("wkt", [D, D], F32, kind="ExternalInput")
    dwv = nc.dram_tensor("wvt", [D, D], F32, kind="ExternalInput")
    dwo = nc.dram_tensor("wot", [D, D], F32, kind="ExternalInput")
    dbq = nc.dram_tensor("bq_cols", [128, 8], F32, kind="ExternalInput")
    dbk = nc.dram_tensor("bk_cols", [128, 8], F32, kind="ExternalInput")
    dbvb = nc.dram_tensor("bv_bc", [128, D], F32, kind="ExternalInput")
    dbob = nc.dram_tensor("bo_bc", [128, D], F32, kind="ExternalInput")
    dgw = nc.dram_tensor("gw_cols", [128, 64], F32, kind="ExternalInput")
    dgbb = nc.dram_tensor("gb_bc", [128, 8], F32, kind="ExternalInput")
    dw1 = nc.dram_tensor("w1", [NEXP, D, FF], F32, kind="ExternalInput")
    db1 = nc.dram_tensor("b1_cols", [128, 256], F32, kind="ExternalInput")
    dw2 = nc.dram_tensor("w2", [NEXP, FF, D], F32, kind="ExternalInput")
    db2 = nc.dram_tensor("b2", [NEXP, D], F32, kind="ExternalInput")
    did16 = nc.dram_tensor("c_id16", [128, 128], BF16, kind="ExternalInput")
    did32 = nc.dram_tensor("c_id32", [128, 128], F32, kind="ExternalInput")
    dmneg = nc.dram_tensor("c_mneg", [128, 128], F32, kind="ExternalInput")
    dktm = nc.dram_tensor("c_ktmpl", [128, S], BF16, kind="ExternalInput")
    dqtm = nc.dram_tensor("c_qtmpl", [128, S], BF16, kind="ExternalInput")

    dout = nc.dram_tensor("out", [T, D], F32, kind="ExternalOutput")
    g_scr = nc.dram_tensor("g_scr", [T, D], F32, kind="ExternalOutput" if debug else "Internal")
    xn_scr = nc.dram_tensor("xn_scr", [8, 128, T], BF16, kind="ExternalOutput" if debug else "Internal")

    if debug:
        dbg_k = nc.dram_tensor("dbg_k", [128, 8 * S], BF16, kind="ExternalOutput")
        dbg_q = nc.dram_tensor("dbg_q", [128, 8 * S], BF16, kind="ExternalOutput")
        dbg_v = nc.dram_tensor("dbg_v", [128, 16 * 1024], BF16, kind="ExternalOutput")
        dbg_y = nc.dram_tensor("dbg_y", [128, 8 * 1024], BF16, kind="ExternalOutput")

    b = Builder(nc, stack)

    id16 = stack.enter_context(nc.sbuf_tensor("id16", [128, 128], BF16))
    id32 = stack.enter_context(nc.sbuf_tensor("id32", [128, 128], F32))
    mneg = stack.enter_context(nc.sbuf_tensor("mneg", [128, 128], F32))
    ktm = stack.enter_context(nc.sbuf_tensor("ktm", [128, S], BF16))
    qtm = stack.enter_context(nc.sbuf_tensor("qtm", [128, S], BF16))
    bq8 = stack.enter_context(nc.sbuf_tensor("bq8", [128, 8], F32))
    bkc = stack.enter_context(nc.sbuf_tensor("bkc", [128, 8], F32))
    comb = stack.enter_context(nc.sbuf_tensor("comb", [128, NT * 8], F32))
    epsv = stack.enter_context(nc.sbuf_tensor("epsv", [128, 1], F32))
    b.op("vector", lambda e: e.memset(epsv[:, :], 1e-5), writes=("epsv",), name="eps")

    b.op("sync", lambda e: e.dma_start(out=id16[:, :], in_=did16[:, :]), writes=("id16",), dma=True, name="l_id16")
    b.op("sync", lambda e: e.dma_start(out=id32[:, :], in_=did32[:, :]), writes=("id32",), dma=True, name="l_id32")
    b.op("sync", lambda e: e.dma_start(out=mneg[:, :], in_=dmneg[:, :]), writes=("mneg",), dma=True, name="l_mneg")
    b.op("sync", lambda e: e.dma_start(out=ktm[:, :], in_=dktm[:, :]), writes=("ktm",), dma=True, name="l_ktm")
    b.op("sync", lambda e: e.dma_start(out=qtm[:, :], in_=dqtm[:, :]), writes=("qtm",), dma=True, name="l_qtm")
    b.op("sync", lambda e: e.dma_start(out=bkc[:, :], in_=dbk[:, :]), writes=("bkc",), dma=True, name="l_bk")
    b.op("sync", lambda e: e.dma_start(out=bq8[:, :], in_=dbq[:, :]), writes=("bq8",), dma=True, name="l_bq")
    b.op("vector", lambda e: e.tensor_scalar(bq8[:, :], bq8[:, :], 0.125, None, op0=ALU.mult),
         reads=("bq8",), writes=("bq8",), name="bq8s")

    with ExitStack() as scA:
        combT = scA.enter_context(nc.sbuf_tensor("combT", [8, NT * 128], F32))
        w1b0_pre = scA.enter_context(nc.sbuf_tensor("w1b0", [128, 4 * 4096], BF16))

        with ExitStack() as sc:
            sb0 = lambda n, s, d: sc.enter_context(nc.sbuf_tensor(n, s, d))
            ps0 = lambda n, s, d: sc.enter_context(nc.psum_tensor(n, s, d))
            xnT = sb0("xnT", [128, 8 * T], BF16)
            gw = sb0("gw", [128, 64], F32)
            gbb = sb0("gbb", [128, 8], F32)
            xb = [sb0(f"xb{i}", [128, D], F32) for i in range(2)]
            xn32 = [sb0(f"xn32_{i}", [128, D], F32) for i in range(2)]
            xn16 = [sb0(f"xn16_{i}", [128, D], BF16) for i in range(2)]
            xT32 = [sb0(f"xT32_{i}", [128, D], F32) for i in range(2)]
            junk = sb0("junk", [128, D], BF16)
            nsum = sb0("nsum", [128, 1], F32)
            negmean = sb0("negmean", [128, 1], F32)
            msq = sb0("msq", [128, 1], F32)
            var = sb0("var", [128, 1], F32)
            lnv = sb0("lnv", [128, 1], F32)
            rstd = sb0("rstd", [128, 1], F32)
            ssq = sb0("ssq", [128, 1], F32)
            logit = sb0("logit", [128, 8], F32)
            m1 = sb0("m1", [128, 1], F32)
            nm1 = sb0("nm1", [128, 1], F32)
            ew = sb0("ew", [128, 8], F32)
            ind1 = sb0("ind1", [128, 8], F32)
            wm = sb0("wm", [128, 8], F32)
            m2 = sb0("m2", [128, 1], F32)
            indt = sb0("indt", [128, 8], F32)
            cu = sb0("cu", [128, 8], F32)
            dsum = sb0("dsum", [128, 1], F32)
            rec_ = sb0("rec_", [128, 1], F32)
            pp16 = [ps0(f"pp16_{i}", [128, 1024], BF16) for i in range(2)]
            pp32 = [ps0(f"pp32_{i}", [128, 1024], F32) for i in range(2)]
            psg = ps0("psg", [128, 8], F32)
            psct = ps0("psct", [8, 512], F32)

            b.op("sync", lambda e: e.dma_start(out=gw[:, :], in_=dgw[:, :]), writes=("gw",), dma=True, name="l_gw")
            b.op("sync", lambda e: e.dma_start(out=gbb[:, :], in_=dgbb[:, :]), writes=("gbb",), dma=True, name="l_gbb")

            for i in range(NT):
                par = i % 2
                t0 = i * 128
                xbp, xn32p, xn16p, xT32p = xb[par], xn32[par], xn16[par], xT32[par]
                nx, n32, n16, nT32 = f"xb{par}", f"xn32_{par}", f"xn16_{par}", f"xT32_{par}"
                pp16p, pp32p = pp16[par], pp32[par]
                n_pp16, n_pp32 = f"pp16_{par}", f"pp32_{par}"

                b.op("sync", lambda e, d=xbp, s=t0: e.dma_start(out=d[:, :], in_=dx[s:s + 128, :]),
                     writes=(nx,), dma=True, name=f"ldx{i}")
                b.op("vector", lambda e, x=xbp: e.reduce_sum(nsum[:, :], x[:, :], axis=AX.X, negate=True),
                     reads=(nx,), writes=("nsum",), name=f"su{i}")
                b.op("scalar", lambda e, x=xbp: e.activation(junk[:, :], x[:, :], AF.Square, accum_out=ssq[:, :]),
                     reads=(nx,), writes=("junk", "ssq"), name=f"sq{i}")
                b.op("vector", lambda e: e.tensor_scalar(negmean[:, :], nsum[:, :], 1.0 / D, None, op0=ALU.mult),
                     reads=("nsum",), writes=("negmean",), name=f"nm{i}")
                b.op("vector", lambda e: e.tensor_mul(msq[:, :], negmean[:, :], negmean[:, :]),
                     reads=("negmean",), writes=("msq",), name=f"ms{i}")
                b.op("vector", lambda e: e.tensor_scalar(var[:, :], ssq[:, :], 1.0 / D, msq[:, :],
                                                         op0=ALU.mult, op1=ALU.subtract),
                     reads=("ssq", "msq"), writes=("var",), name=f"va{i}")
                b.op("scalar", lambda e: e.activation(lnv[:, :], var[:, :], AF.Ln, bias=epsv[:, :]),
                     reads=("var", "epsv"), writes=("lnv",), name=f"ln{i}")
                b.op("scalar", lambda e: e.activation(rstd[:, :], lnv[:, :], AF.Exp, scale=-0.5),
                     reads=("lnv",), writes=("rstd",), name=f"rs{i}")
                b.op("vector", lambda e, x=xbp, o=xn32p: e.tensor_scalar(
                        o[:, :], x[:, :], negmean[:, :], rstd[:, :], op0=ALU.add, op1=ALU.mult),
                     reads=(nx, "negmean", "rstd"), writes=(n32,), name=f"xn{i}")
                b.op("vector", lambda e, s=xn32p, o=xn16p: e.tensor_copy(o[:, :], s[:, :]),
                     reads=(n32,), writes=(n16,), name=f"cst{i}")

                def tp16(e, src=xn16p, dst=pp16p):
                    inst = None
                    for c in range(8):
                        inst = e.matmul(dst[:, c * 128:(c + 1) * 128], src[:, c * 128:(c + 1) * 128],
                                        id16[:, :], start=(c == 0), stop=(c == 7), is_transpose=True)
                    return inst
                b.op("tensor", tp16, reads=(n16, "id16"), writes=(n_pp16,), name=f"t16_{i}")

                def tp32(e, src=xn32p, dst=pp32p):
                    inst = None
                    for c in range(8):
                        inst = e.matmul(dst[:, c * 128:(c + 1) * 128], src[:, c * 128:(c + 1) * 128],
                                        id32[:, :], start=(c in (0, 4)), stop=(c in (3, 7)), is_transpose=True)
                    return inst
                b.op("tensor", tp32, reads=(n32, "id32"), writes=(n_pp32,), name=f"t32_{i}")

                def ev16(e, src=pp16p, t0=t0):
                    dst = bass.AP(xnT, t0, [[8 * T, 128], [T, 8], [1, 128]])
                    return e.tensor_copy(dst, src[:, :])
                b.op("vector", ev16, reads=(n_pp16,), writes=("xnT",), name=f"e16_{i}")
                b.op("vector", lambda e, src=pp32p, dst=xT32p: e.tensor_copy(dst[:, :], src[:, :]),
                     reads=(n_pp32,), writes=(nT32,), name=f"e32_{i}")

                def gmm(e, src=xT32p):
                    inst = None
                    for c in range(8):
                        inst = e.matmul(psg[:, :], src[:, c * 128:(c + 1) * 128],
                                        gw[:, c * 8:(c + 1) * 8], start=(c == 0), stop=(c == 7))
                    return inst
                b.op("tensor", gmm, reads=(nT32, "gw"), writes=("psg",), name=f"gm{i}")
                b.op("vector", lambda e: e.tensor_add(logit[:, :], psg[:, :], gbb[:, :]),
                     reads=("psg", "gbb"), writes=("logit",), name=f"lg{i}")
                b.op("vector", lambda e: e.reduce_max(m1[:, :], logit[:, :], axis=AX.X),
                     reads=("logit",), writes=("m1",), name=f"m1_{i}")
                b.op("vector", lambda e: e.tensor_scalar(nm1[:, :], m1[:, :], -1.0, None, op0=ALU.mult),
                     reads=("m1",), writes=("nm1",), name=f"nm1_{i}")
                b.op("scalar", lambda e: e.activation(ew[:, :], logit[:, :], AF.Exp, bias=nm1[:, :]),
                     reads=("logit", "nm1"), writes=("ew",), name=f"ew{i}")
                b.op("vector", lambda e: e.tensor_scalar(ind1[:, :], logit[:, :], m1[:, :], None, op0=ALU.is_ge),
                     reads=("logit", "m1"), writes=("ind1",), name=f"i1_{i}")
                b.op("vector", lambda e: e.tensor_scalar(ind1[:, :], ind1[:, :], -1.0, 1.0,
                                                         op0=ALU.mult, op1=ALU.add),
                     reads=("ind1",), writes=("ind1",), name=f"ni{i}")
                b.op("vector", lambda e: e.tensor_mul(wm[:, :], ew[:, :], ind1[:, :]),
                     reads=("ew", "ind1"), writes=("wm",), name=f"wm{i}")
                b.op("vector", lambda e: e.reduce_max(m2[:, :], wm[:, :], axis=AX.X),
                     reads=("wm",), writes=("m2",), name=f"m2_{i}")
                b.op("vector", lambda e: e.tensor_scalar(indt[:, :], ew[:, :], m2[:, :], None, op0=ALU.is_ge),
                     reads=("ew", "m2"), writes=("indt",), name=f"it{i}")
                b.op("vector", lambda e: e.tensor_mul(cu[:, :], ew[:, :], indt[:, :]),
                     reads=("ew", "indt"), writes=("cu",), name=f"cu{i}")
                b.op("vector", lambda e: e.reduce_sum(dsum[:, :], cu[:, :], axis=AX.X),
                     reads=("cu",), writes=("dsum",), name=f"ds{i}")
                b.op("vector", lambda e: e.tensor_scalar(dsum[:, :], dsum[:, :], 1e-6, None, op0=ALU.max),
                     reads=("dsum",), writes=("dsum",), name=f"dm{i}")
                b.op("vector", lambda e: e.reciprocal(rec_[:, :], dsum[:, :]),
                     reads=("dsum",), writes=("rec_",), name=f"rc{i}")
                b.op("vector", lambda e, i=i: e.tensor_scalar(comb[:, i * 8:(i + 1) * 8], cu[:, :],
                                                              rec_[:, :], None, op0=ALU.mult),
                     reads=("cu", "rec_"), writes=("comb",), name=f"cb{i}")

                def ctp(e, i=i):
                    return e.matmul(psct[0:8, (i % 4) * 128:(i % 4) * 128 + 128],
                                    comb[:, i * 8:(i + 1) * 8], id32[:, :],
                                    start=(i % 4 == 0), stop=(i % 4 == 3), is_transpose=True)
                b.op("tensor", ctp, reads=("comb", "id32"), writes=("psct",), name=f"ct{i}")
                if i % 4 == 3:
                    b.op("vector", lambda e, i=i: e.tensor_copy(combT[0:8, (i - 3) * 128:(i + 1) * 128],
                                                                psct[0:8, :]),
                         reads=("psct",), writes=("combT",), name=f"ce{i}")

            for c in range(8):
                b.op("sync", lambda e, c=c: e.dma_start(out=xn_scr[c, :, :], in_=xnT[:, c * T:(c + 1) * T]),
                     reads=("xnT",), writes=("xn_scr",), dma=True, name=f"sp{c}")

            for fc in range(4):
                w1src = dw1[0, :, fc * 512:(fc + 1) * 512].rearrange("(c p) f -> p c f", p=128)
                w1dst = bass.AP(w1b0_pre, fc * 4096, [[4 * 4096, 128], [512, 8], [1, 512]])
                b.op("gpsimd", lambda e, d=w1dst, s=w1src: e.dma_start(out=d, in_=s),
                     writes=("w1b0",), dma=True, name=f"w1l0_{fc}")

        b.barrier()

        with ExitStack() as sc:
            sb1 = lambda n, s, d: sc.enter_context(nc.sbuf_tensor(n, s, d))
            ps1 = lambda n, s, d: sc.enter_context(nc.psum_tensor(n, s, d))
            w1b = [w1b0_pre, sb1("w1b1", [128, 4 * 4096], BF16)]
            w2h = sb1("w2h", [128, 16 * 1024], BF16)
            hdnT = [sb1(f"hdnT{i}", [128, 16 * 512], BF16) for i in range(2)]
            xnb1 = [sb1(f"xnb1_{i}", [128, 8 * 512], BF16) for i in range(2)]
            b1c = sb1("b1c", [128, 256], F32)
            b2sb = sb1("b2sb", [8, D], F32)
            gout = [sb1(f"gout{i}", [128, D], F32) for i in range(2)]
            psa = [ps1(f"psa{i}", [128, 512], F32) for i in range(2)]
            psb = [ps1(f"psb{i}", [128, 1024], F32) for i in range(2)]

            b.op("sync", lambda e: e.dma_start(out=b1c[:, :], in_=db1[:, :]), writes=("b1c",), dma=True, name="l_b1")
            b.op("sync", lambda e: e.dma_start(out=b2sb[:, :], in_=db2[:, :]), writes=("b2sb",), dma=True, name="l_b2")

            for gt in range(NT):
                go, ngo = gout[gt % 2], f"gout{gt % 2}"
                pb, npb = psb[gt % 2], f"psb{gt % 2}"

                def b2mm(e, gt=gt, pb=pb):
                    e.matmul(pb[:, 0:512], combT[0:8, gt * 128:(gt + 1) * 128],
                             b2sb[0:8, 0:512], start=True, stop=True)
                    return e.matmul(pb[:, 512:1024], combT[0:8, gt * 128:(gt + 1) * 128],
                                    b2sb[0:8, 512:1024], start=True, stop=True)
                b.op("tensor", b2mm, reads=("combT", "b2sb"), writes=(npb,), name=f"b2m{gt}")
                b.op("vector", lambda e, pb=pb, go=go: e.tensor_copy(go[:, :], pb[:, :]),
                     reads=(npb,), writes=(ngo,), name=f"b2c{gt}")
                b.op("gpsimd", lambda e, go=go, gt=gt: e.dma_start(
                        out=g_scr[gt * 128:(gt + 1) * 128, :], in_=go[:, :]),
                     reads=(ngo,), writes=(f"g{gt}",), dma=True, name=f"b2s{gt}")

            def load_w1(rnd):
                ex, fh = rnd // 2, rnd % 2
                wp, nwp = w1b[rnd % 2], f"w1b{rnd % 2}"
                for fc in range(4):
                    src = dw1[ex, :, fh * 2048 + fc * 512: fh * 2048 + (fc + 1) * 512] \
                        .rearrange("(c p) f -> p c f", p=128)
                    dst = bass.AP(wp, fc * 4096, [[4 * 4096, 128], [512, 8], [1, 512]])
                    b.op("gpsimd", lambda e, d=dst, s=src: e.dma_start(out=d, in_=s),
                         writes=(nwp,), dma=True, name=f"w1l{rnd}_{fc}")

            for rnd in range(16):
                ex, fh = rnd // 2, rnd % 2
                w1p, nw1 = w1b[rnd % 2], f"w1b{rnd % 2}"
                for fc in range(4):
                    src = dw2[ex, fh * 2048 + fc * 512: fh * 2048 + (fc + 1) * 512, :] \
                        .rearrange("(c p) h -> p c h", p=128)
                    dst = bass.AP(w2h, fc * 4 * 1024, [[16 * 1024, 128], [1024, 4], [1, 1024]])
                    b.op("gpsimd", lambda e, d=dst, s=src: e.dma_start(out=d, in_=s),
                         writes=("w2h",), dma=True, name=f"w2l{rnd}_{fc}")
                if rnd + 1 < 16:
                    load_w1(rnd + 1)

                for tg in range(8):
                    tpos = tg * 512
                    xp, nxp = xnb1[tg % 2], f"xnb1_{tg % 2}"
                    src = xn_scr[:, :, tpos:tpos + 512].rearrange("c p t -> p c t")
                    b.op("sync", lambda e, d=xp, s=src: e.dma_start(
                            out=d.ap().rearrange("p (c t) -> p c t", c=8), in_=s),
                         reads=("xn_scr",), writes=(nxp,), dma=True, name=f"xl{rnd}_{tg}")

                    for fb in range(16):
                        fc, fs = fb // 4, fb % 4
                        pa, npa = psa[fb % 2], f"psa{fb % 2}"

                        def amm(e, w1p=w1p, fc=fc, fs=fs, pa=pa, xp=xp):
                            inst = None
                            for dc in range(8):
                                inst = e.matmul(
                                    pa[:, :],
                                    w1p[:, fc * 4096 + dc * 512 + fs * 128: fc * 4096 + dc * 512 + fs * 128 + 128],
                                    xp[:, dc * 512:(dc + 1) * 512],
                                    start=(dc == 0), stop=(dc == 7))
                            return inst
                        b.op("tensor", amm, reads=(nw1, nxp), writes=(npa,), name=f"am{rnd}_{tg}_{fb}")

                        bcol = ex * 32 + fh * 16 + fb
                        hb, nhb = hdnT[tg % 2], f"hdnT{tg % 2}"
                        b.op("scalar", lambda e, pa=pa, fb=fb, bcol=bcol, hb=hb: e.activation(
                                hb[:, fb * 512:(fb + 1) * 512], pa[:, :], AF.Gelu,
                                bias=b1c[:, bcol:bcol + 1]),
                             reads=(npa, "b1c"), writes=(nhb,), name=f"ge{rnd}_{tg}_{fb}")

                    for tl in range(4):
                        gt = tg * 4 + tl
                        pb, npb = psb[tl % 2], f"psb{tl % 2}"
                        go, ngo = gout[tl % 2], f"gout{tl % 2}"

                        hb, nhb = hdnT[tg % 2], f"hdnT{tg % 2}"
                        def bmm(e, tl=tl, pb=pb, hb=hb):
                            inst = None
                            for fb in range(16):
                                lhs = hb[:, fb * 512 + tl * 128: fb * 512 + tl * 128 + 128]
                                inst = e.matmul(pb[:, 0:512], lhs, w2h[:, fb * 1024: fb * 1024 + 512],
                                                start=(fb == 0), stop=(fb == 15))
                                inst = e.matmul(pb[:, 512:1024], lhs, w2h[:, fb * 1024 + 512: fb * 1024 + 1024],
                                                start=(fb == 0), stop=(fb == 15))
                            return inst
                        b.op("tensor", bmm, reads=(nhb, "w2h"), writes=(npb,), name=f"bm{rnd}_{gt}")
                        b.op("vector", lambda e, pb=pb, go=go, gt=gt, ex=ex: e.tensor_scalar(
                                go[:, :], pb[:, :], comb[:, gt * 8 + ex: gt * 8 + ex + 1], None, op0=ALU.mult),
                             reads=(npb, "comb"), writes=(ngo,), name=f"be{rnd}_{gt}")
                        b.op("gpsimd", lambda e, go=go, gt=gt: e.dma_start(
                                out=g_scr[gt * 128:(gt + 1) * 128, :], in_=go[:, :], accum_op=ALU.add),
                             reads=(ngo,), writes=(f"g{gt}",), dma=True, name=f"ga{rnd}_{gt}")

        b.barrier()

    with ExitStack() as sc:
        sb2 = lambda n, s, d: sc.enter_context(nc.sbuf_tensor(n, s, d))
        kT = sb2("kT", [128, 8 * S], BF16)
        qT2 = sb2("qT2", [128, 8 * S], BF16)
        v_tm = sb2("v_tm", [128, 16 * 1024], BF16)
        yT = sb2("yT", [128, 8 * 1024], BF16)
        bvb = sb2("bvb", [128, D], F32)
        bob = sb2("bob", [128, D], F32)
        lcur = sb2("lcur", [128, 1], F32)
        rl2 = sb2("rl2", [128, 1], F32)
        mstat = sb2("mstat", [128, 2], F32)
        lc2 = sb2("lc2", [128, 2], F32)
        negm = sb2("negm", [128, 1], F32)
        stmp = sb2("stmp", [128, D], F32)
        sgb = sb2("sgb", [128, D], F32)

        b.op("sync", lambda e: e.dma_start(out=bvb[:, :], in_=dbvb[:, :]), writes=("bvb",), dma=True, name="l_bvb")
        b.op("sync", lambda e: e.dma_start(out=bob[:, :], in_=dbob[:, :]), writes=("bob",), dma=True, name="l_bob")

        for sq in range(SEQS):
            soff = sq * S
            with ExitStack() as sp:
                sbp = lambda n, s, d: sp.enter_context(nc.sbuf_tensor(n, s, d))
                psp = lambda n, s, d: sp.enter_context(nc.psum_tensor(n, s, d))
                wkb = sbp(f"wkb{sq}", [128, 8 * 1024], BF16)
                wqb = sbp(f"wqb{sq}", [128, 8 * 1024], BF16)
                wvb = sbp(f"wvb{sq}", [128, 8 * 1024], BF16)
                xnb = [sbp(f"xnb2_{sq}_{i}", [128, 8 * 512], BF16) for i in range(2)]
                pko = [psp(f"pko{sq}_{i}", [128, 512], F32) for i in range(2)]
                psv = psp(f"psv{sq}", [128, 1024], F32)

                for wdst, wsrc, wname in ((wkb, dwk, f"wkb{sq}"), (wqb, dwq, f"wqb{sq}"), (wvb, dwv, f"wvb{sq}")):
                    src = wsrc[:, :].rearrange("(c p) i -> p c i", p=128)
                    b.op("gpsimd", lambda e, dd=wdst, s=src: e.dma_start(
                            out=dd.ap().rearrange("p (c i) -> p c i", c=8), in_=s),
                         writes=(wname,), dma=True, name=f"lw_{wname}")

                for tc in range(4):
                    xp, nxp = xnb[tc % 2], f"xnb2_{sq}_{tc % 2}"
                    src = xn_scr[:, :, soff + tc * 512: soff + (tc + 1) * 512].rearrange("c p t -> p c t")
                    b.op("sync", lambda e, d=xp, s=src: e.dma_start(
                            out=d.ap().rearrange("p (c t) -> p c t", c=8), in_=s),
                         reads=("xn_scr",), writes=(nxp,), dma=True, name=f"x2{sq}_{tc}")

                    for proj, wb, nwb, dstT, ndT, bias, scale in (
                            ("k", wkb, f"wkb{sq}", kT, "kT", bkc, 1.0),
                            ("q", wqb, f"wqb{sq}", qT2, "qT2", bq8, 0.125)):
                        for ic in range(8):
                            pk, npk = pko[ic % 2], f"pko{sq}_{ic % 2}"

                            def pmm(e, xp=xp, ic=ic, pk=pk, wb=wb):
                                inst = None
                                for dc in range(8):
                                    inst = e.matmul(pk[:, :],
                                                    wb[:, dc * 1024 + ic * 128: dc * 1024 + ic * 128 + 128],
                                                    xp[:, dc * 512:(dc + 1) * 512],
                                                    start=(dc == 0), stop=(dc == 7))
                                return inst
                            b.op("tensor", pmm, reads=(nwb, nxp), writes=(npk,), name=f"{proj}m{sq}_{tc}_{ic}")
                            b.op("scalar", lambda e, pk=pk, ic=ic, tc=tc, dstT=dstT, bias=bias, scale=scale:
                                    e.activation(dstT[:, ic * S + tc * 512: ic * S + (tc + 1) * 512], pk[:, :],
                                                 AF.Identity, bias=bias[:, ic:ic + 1], scale=scale),
                                 reads=(npk, "bkc", "bq8"), writes=(ndT,), name=f"{proj}e{sq}_{tc}_{ic}")

                    for tl in range(4):
                        tj = tc * 4 + tl

                        def vmm(e, xp=xp, tl=tl, wb=wvb):
                            inst = None
                            for dc in range(8):
                                lhs = xp[:, dc * 512 + tl * 128: dc * 512 + tl * 128 + 128]
                                inst = e.matmul(psv[:, 0:512], lhs, wb[:, dc * 1024: dc * 1024 + 512],
                                                start=(dc == 0), stop=(dc == 7))
                                inst = e.matmul(psv[:, 512:1024], lhs, wb[:, dc * 1024 + 512: dc * 1024 + 1024],
                                                start=(dc == 0), stop=(dc == 7))
                            return inst
                        b.op("tensor", vmm, reads=(f"wvb{sq}", nxp), writes=(f"psv{sq}",), name=f"vm{sq}_{tc}_{tl}")
                        b.op("vector", lambda e, tj=tj: e.scalar_tensor_tensor(
                                v_tm[:, tj * 1024:(tj + 1) * 1024], psv[:, :], 1.0, bvb[:, :],
                                op0=ALU.mult, op1=ALU.add),
                             reads=(f"psv{sq}", "bvb"), writes=("v_tm",), name=f"ve{sq}_{tc}_{tl}")

            if debug and sq == 0:
                b.op("sync", lambda e: e.dma_start(out=dbg_k[:, :], in_=kT[:, :]),
                     reads=("kT",), dma=True, name="dbgk")
                b.op("sync", lambda e: e.dma_start(out=dbg_q[:, :], in_=qT2[:, :]),
                     reads=("qT2",), dma=True, name="dbgq")
                b.op("sync", lambda e: e.dma_start(out=dbg_v[:, :], in_=v_tm[:, :]),
                     reads=("v_tm",), dma=True, name="dbgv")
            b.barrier()

            with ExitStack() as sp:
                sbp = lambda n, s, d: sp.enter_context(nc.sbuf_tensor(n, s, d))
                psp = lambda n, s, d: sp.enter_context(nc.psum_tensor(n, s, d))
                wob = sbp(f"wob{sq}", [128, 8 * 1024], BF16)
                kqa = [sbp(f"kqa{sq}_{i}", [72, S], BF16) for i in range(2)]
                qaug = [sbp(f"qaug{sq}_{i}", [72, 1024], BF16) for i in range(2)]
                Ebuf = [sbp(f"Eb{sq}_{i}", [128, S], BF16) for i in range(2)]
                ETb = [sbp(f"ET{sq}_{i}", [128, 16 * 128], BF16) for i in range(2)]
                gti = [sbp(f"gti{sq}_{i}", [128, D], F32) for i in range(2)]
                xti = [sbp(f"xti{sq}_{i}", [128, D], F32) for i in range(2)]
                outb = [sbp(f"outb{sq}_{i}", [128, D], F32) for i in range(2)]
                pssp = [psp(f"pss{sq}_{i}", [128, 1024], F32) for i in range(2)]
                pset = psp(f"pset{sq}", [128, 1024], BF16)
                psy = psp(f"psy{sq}", [64, 128], F32)
                pso = psp(f"pso{sq}", [128, 1024], F32)

                src = dwo[:, :].rearrange("(c p) i -> p c i", p=128)
                b.op("gpsimd", lambda e, s=src: e.dma_start(
                        out=wob.ap().rearrange("p (c i) -> p c i", c=8), in_=s),
                     writes=(f"wob{sq}",), dma=True, name=f"lwo{sq}")

                for half in range(2):
                    pending = None
                    for h in range(H):
                        hp = h % 2
                        kq, nkq = kqa[hp], f"kqa{sq}_{hp}"
                        qa, nqa = qaug[hp], f"qaug{sq}_{hp}"
                        b.op("sync", lambda e, kq=kq, h=h: e.dma_start(
                                out=kq[0:64, :],
                                in_=kT[(h % 2) * 64:(h % 2) * 64 + 64, (h // 2) * S:(h // 2 + 1) * S]),
                             reads=("kT",), writes=(nkq,), dma=True, name=f"bk{sq}_{half}_{h}")
                        b.op("sync", lambda e, kq=kq, h=h: e.dma_start(
                                out=kq[64:72, :], in_=ktm[8 * h:8 * h + 8, :]),
                             reads=("ktm",), writes=(nkq,), dma=True, name=f"bkt{sq}_{half}_{h}")
                        b.op("sync", lambda e, qa=qa, h=h, half=half: e.dma_start(
                                out=qa[0:64, :],
                                in_=qT2[(h % 2) * 64:(h % 2) * 64 + 64,
                                        (h // 2) * S + half * 1024:(h // 2) * S + half * 1024 + 1024]),
                             reads=("qT2",), writes=(nqa,), dma=True, name=f"bq{sq}_{half}_{h}")
                        b.op("sync", lambda e, qa=qa, h=h, half=half: e.dma_start(
                                out=qa[64:72, :], in_=qtm[8 * h:8 * h + 8, half * 1024: half * 1024 + 1024]),
                             reads=("qtm",), writes=(nqa,), dma=True, name=f"bqt{sq}_{half}_{h}")

                        for A in range(8):
                            Ag = half * 8 + A
                            W = (Ag + 1) * 128
                            Ep, nEp = Ebuf[A % 2], f"Eb{sq}_{A % 2}"
                            ETp, nETp = ETb[A % 2], f"ET{sq}_{A % 2}"
                            if Ag <= 7:
                                pieces = [(A % 2, 0, W)]
                            else:
                                pieces = [(0, 0, 1024), (1, 1024, W)]
                            tag = f"{sq}_{half}_{h}_{A}"

                            n512 = Ag // 4
                            rem = Ag % 4
                            chunks = [(c * 512, (c + 1) * 512, True, True) for c in range(n512)]
                            if rem:
                                chunks.append((n512 * 512, n512 * 512 + rem * 128, True, False))
                                chunks.append((W - 128, W, False, True))
                            else:
                                chunks.append((W - 128, W, True, True))

                            for pi, (pb, lo, hi) in enumerate(pieces):
                                ps_ = pssp[pb]
                                nps = f"pss{sq}_{pb}"

                                def smm(e, qa=qa, kq=kq, A=A, ps_=ps_, lo=lo, hi=hi, chunks=tuple(chunks)):
                                    inst = None
                                    lhs = qa[0:72, A * 128:(A + 1) * 128]
                                    for gl, gh, st, sp in chunks:
                                        if gl < lo or gl >= hi:
                                            continue
                                        inst = e.matmul(ps_[:, gl - lo:gh - lo], lhs,
                                                        kq[0:72, gl:gh], start=st, stop=sp)
                                    return inst
                                b.op("tensor", smm, reads=(nqa, nkq), writes=(nps,), name=f"s{tag}_{pi}")

                            if pending is not None:
                                pending()
                                pending = None

                            lastpb, lastlo, lasthi = pieces[-1]
                            nlast = f"pss{sq}_{lastpb}"
                            if len(pieces) == 1:
                                ps_, lo = pssp[pieces[0][0]], pieces[0][1]
                                b.op("vector", lambda e, ps_=ps_, W=W: e.tensor_add(
                                        ps_[:, W - 128:W], ps_[:, W - 128:W], mneg[:, :]),
                                     reads=(nlast, "mneg"), writes=(nlast,), name=f"k{tag}")
                                b.op("vector", lambda e, ps_=ps_, W=W: e.reduce_max(
                                        negm[:, :], ps_[:, 0:W], axis=AX.X, negate=True),
                                     reads=(nlast,), writes=("negm",), name=f"r{tag}")
                                b.op("scalar", lambda e, Ep=Ep, ps_=ps_, W=W: e.activation(
                                        Ep[:, 0:W], ps_[:, 0:W], AF.Exp, bias=negm[:, :],
                                        accum_out=lcur[:, :]),
                                     reads=(nlast, "negm"), writes=(nEp, "lcur"), name=f"x{tag}")
                            else:
                                p0n, p1n = f"pss{sq}_0", f"pss{sq}_1"
                                b.op("vector", lambda e: e.reduce_max(
                                        mstat[:, 0:1], pssp[0][:, :], axis=AX.X),
                                     reads=(p0n,), writes=("mstat",), name=f"r0{tag}")
                                b.op("vector", lambda e, W=W: e.tensor_add(
                                        pssp[1][:, W - 1152:W - 1024], pssp[1][:, W - 1152:W - 1024], mneg[:, :]),
                                     reads=(p1n, "mneg"), writes=(p1n,), name=f"k{tag}")
                                b.op("vector", lambda e, W=W: e.reduce_max(
                                        mstat[:, 1:2], pssp[1][:, 0:W - 1024], axis=AX.X),
                                     reads=(p1n,), writes=("mstat",), name=f"r1{tag}")
                                b.op("vector", lambda e: e.reduce_max(
                                        negm[:, :], mstat[:, 0:2], axis=AX.X, negate=True),
                                     reads=("mstat",), writes=("negm",), name=f"rm{tag}")
                                b.op("scalar", lambda e, Ep=Ep: e.activation(
                                        Ep[:, 0:1024], pssp[0][:, :], AF.Exp, bias=negm[:, :],
                                        accum_out=lc2[:, 0:1]),
                                     reads=(p0n, "negm"), writes=(nEp, "lc2"), name=f"x0{tag}")
                                b.op("scalar", lambda e, Ep=Ep, W=W: e.activation(
                                        Ep[:, 1024:W], pssp[1][:, 0:W - 1024], AF.Exp, bias=negm[:, :],
                                        accum_out=lc2[:, 1:2]),
                                     reads=(p1n, "negm"), writes=(nEp, "lc2"), name=f"x1{tag}")
                                b.op("vector", lambda e: e.tensor_add(lcur[:, :], lc2[:, 0:1], lc2[:, 1:2]),
                                     reads=("lc2",), writes=("lcur",), name=f"la{tag}")
                            b.op("vector", lambda e: e.reciprocal(rl2[:, :], lcur[:, :]),
                                 reads=("lcur",), writes=("rl2",), name=f"rl{tag}")
                            b.op("gpsimd", lambda e, Ep=Ep, W=W: e.tensor_scalar(
                                    Ep[:, 0:W], Ep[:, 0:W], rl2[:, :], None, op0=ALU.mult),
                                 reads=(nEp, "rl2"), writes=(nEp,), name=f"en{tag}")

                            def make_tail(Ep=Ep, nEp=nEp, ETp=ETp, nETp=nETp, h=h, A=A, Ag=Ag, tag=tag):
                                def tail():
                                    ngroups = (Ag + 8) // 8
                                    for g in range(ngroups):
                                        j0, j1 = g * 8, min(g * 8 + 8, Ag + 1)

                                        def tpe(e, Ep=Ep, j0=j0, j1=j1):
                                            inst = None
                                            for j in range(j0, j1):
                                                inst = e.matmul(pset[:, (j - j0) * 128:(j - j0 + 1) * 128],
                                                                Ep[:, j * 128:(j + 1) * 128], id16[:, :],
                                                                start=(j == j0), stop=(j == j1 - 1),
                                                                is_transpose=True)
                                            return inst
                                        b.op("tensor", tpe, reads=(nEp, "id16"), writes=("pset",),
                                             name=f"t{tag}_{g}")
                                        b.op("scalar", lambda e, ETp=ETp, j0=j0, j1=j1: e.copy(
                                                ETp[:, j0 * 128:j1 * 128], pset[:, 0:(j1 - j0) * 128]),
                                             reads=("pset",), writes=(nETp,), name=f"u{tag}_{g}")

                                    def avm(e, ETp=ETp, h=h, Ag=Ag):
                                        inst = None
                                        for jj in range(Ag + 1):
                                            inst = e.matmul(psy[0:64, :],
                                                            v_tm[:, jj * 1024 + h * 64: jj * 1024 + h * 64 + 64],
                                                            ETp[:, jj * 128:(jj + 1) * 128],
                                                            start=(jj == 0), stop=(jj == Ag))
                                        return inst
                                    b.op("tensor", avm, reads=(nETp, "v_tm"), writes=("psy",), name=f"a{tag}")
                                    b.op("vector", lambda e, h=h, A=A: e.tensor_copy(
                                            yT[(h % 2) * 64:(h % 2) * 64 + 64,
                                               (h // 2) * 1024 + A * 128:(h // 2) * 1024 + A * 128 + 128],
                                            psy[0:64, :]),
                                         reads=("psy",), writes=("yT",), name=f"y{tag}")
                                return tail
                            pending = make_tail()
                    if pending is not None:
                        pending()
                        pending = None

                    if debug and sq == 0 and half == 0:
                        b.op("sync", lambda e: e.dma_start(out=dbg_y[:, :], in_=yT[:, :]),
                             reads=("yT",), dma=True, name="dbgy")
                    for tl in range(8):
                        t0 = soff + half * 1024 + tl * 128
                        gp, ngp = gti[tl % 2], f"gti{sq}_{tl % 2}"
                        xp2, nxp2 = xti[tl % 2], f"xti{sq}_{tl % 2}"
                        ob, nob = outb[tl % 2], f"outb{sq}_{tl % 2}"
                        gt_glob = t0 // 128
                        b.op("sync", lambda e, gp=gp, t0=t0: e.dma_start(out=gp[:, :], in_=g_scr[t0:t0 + 128, :]),
                             reads=(f"g{gt_glob}",), writes=(ngp,), dma=True, name=f"lg{sq}_{half}_{tl}")
                        b.op("sync", lambda e, xp2=xp2, t0=t0: e.dma_start(out=xp2[:, :], in_=dx[t0:t0 + 128, :]),
                             writes=(nxp2,), dma=True, name=f"lx{sq}_{half}_{tl}")

                        def omm(e, tl=tl):
                            inst = None
                            for ic in range(8):
                                lhs = yT[:, ic * 1024 + tl * 128: ic * 1024 + tl * 128 + 128]
                                inst = e.matmul(pso[:, 0:512], lhs, wob[:, ic * 1024: ic * 1024 + 512],
                                                start=(ic == 0), stop=(ic == 7))
                                inst = e.matmul(pso[:, 512:1024], lhs, wob[:, ic * 1024 + 512: ic * 1024 + 1024],
                                                start=(ic == 0), stop=(ic == 7))
                            return inst
                        b.op("tensor", omm, reads=("yT", f"wob{sq}"), writes=(f"pso{sq}",), name=f"o{sq}_{half}_{tl}")
                        b.op("vector", lambda e: e.scalar_tensor_tensor(
                                stmp[:, :], pso[:, :], 1.0, bob[:, :], op0=ALU.mult, op1=ALU.add),
                             reads=(f"pso{sq}", "bob"), writes=("stmp",), name=f"se{sq}_{half}_{tl}")
                        b.op("scalar", lambda e, gp=gp: e.activation(gp[:, :], gp[:, :], AF.Silu),
                             reads=(ngp,), writes=(ngp,), name=f"si{sq}_{half}_{tl}")
                        b.op("vector", lambda e, gp=gp: e.tensor_mul(sgb[:, :], stmp[:, :], gp[:, :]),
                             reads=("stmp", ngp), writes=("sgb",), name=f"sg{sq}_{half}_{tl}")
                        b.op("vector", lambda e, xp2=xp2, ob=ob: e.tensor_add(ob[:, :], sgb[:, :], xp2[:, :]),
                             reads=("sgb", nxp2), writes=(nob,), name=f"oa{sq}_{half}_{tl}")
                        b.op("sync", lambda e, ob=ob, t0=t0: e.dma_start(out=dout[t0:t0 + 128, :], in_=ob[:, :]),
                             reads=(nob,), writes=("out",), dma=True, name=f"so{sq}_{half}_{tl}")

            if sq + 1 < SEQS:
                b.barrier()

        final_out = b.writer["out"]["sync"]
        b.op("sync", lambda e: e.nop(), after=(final_out,), name="final")

    b.finalize_and_emit()
    stack.close()
    return nc


_cached = {}


def _get_program():
    if "nc" not in _cached:
        _cached["nc"] = build_program()
    return _cached["nc"]


def kernel(L0_attn_x, L0_attn_Wq, L0_attn_bq, L0_attn_Wk, L0_attn_bk,
           L0_attn_Wv, L0_attn_bv, L0_attn_Wo, L0_attn_bo,
           L0_ffn_gate_w, L0_ffn_gate_b, L0_ffn_w1, L0_ffn_b1,
           L0_ffn_w2, L0_ffn_b2):
    f32 = lambda a: np.ascontiguousarray(np.asarray(a, dtype=np.float32))
    x = f32(L0_attn_x)
    Wq, Wk, Wv, Wo = map(f32, (L0_attn_Wq, L0_attn_Wk, L0_attn_Wv, L0_attn_Wo))
    bq, bk, bv, bo = map(f32, (L0_attn_bq, L0_attn_bk, L0_attn_bv, L0_attn_bo))
    gw, gb = f32(L0_ffn_gate_w), f32(L0_ffn_gate_b)
    w1, b1, w2, b2 = map(f32, (L0_ffn_w1, L0_ffn_b1, L0_ffn_w2, L0_ffn_b2))

    id16, id32, mneg, ktmpl, qtmpl = _make_consts()
    cols = lambda v: np.ascontiguousarray(v.reshape(8, 128).T)
    common = dict(
        wqt=np.ascontiguousarray(Wq.T), wkt=np.ascontiguousarray(Wk.T),
        wvt=np.ascontiguousarray(Wv.T), wot=np.ascontiguousarray(Wo.T),
        bq_cols=cols(bq), bk_cols=cols(bk),
        bv_bc=np.ascontiguousarray(np.broadcast_to(bv, (128, D))),
        bo_bc=np.ascontiguousarray(np.broadcast_to(bo, (128, D))),
        gw_cols=np.ascontiguousarray(gw.reshape(8, 128, 8).transpose(1, 0, 2).reshape(128, 64)),
        gb_bc=np.ascontiguousarray(np.broadcast_to(gb, (128, 8))),
        w1=w1, w2=w2,
        b1_cols=np.ascontiguousarray(b1.reshape(8, 32, 128).transpose(2, 0, 1).reshape(128, 256)),
        b2=b2,
        c_id16=id16, c_id32=id32, c_mneg=mneg, c_ktmpl=ktmpl, c_qtmpl=qtmpl,
    )
    in_maps = []
    for core in range(8):
        m = dict(common)
        m["x"] = np.ascontiguousarray(x[core * SEQS:(core + 1) * SEQS].reshape(T, D))
        in_maps.append(m)

    nc = _get_program()
    res = run_bass_kernel_spmd(nc, in_maps, core_ids=list(range(8)))
    out = np.concatenate([r["out"].reshape(SEQS, S, D) for r in res.results], axis=0)
    return out


# version 05
# speedup vs baseline: 1.6600x; 1.6600x over previous
import math
from contextlib import ExitStack

import numpy as np
import ml_dtypes

import concourse.bass as bass
import concourse.mybir as mybir
from concourse.bass_utils import run_bass_kernel_spmd

F32 = mybir.dt.float32
BF16 = mybir.dt.bfloat16
AF = mybir.ActivationFunctionType
ALU = mybir.AluOpType
AX = mybir.AxisListType
BF = ml_dtypes.bfloat16

D = 1024
H = 16
DH = 64
S = 2048
NEXP = 8
FF = 4096
SEQS = 2
T = SEQS * S
NT = T // 128
NEGBIG = -3.0e38

ENGINES = ("tensor", "vector", "scalar", "gpsimd", "sync")


class Rec:
    __slots__ = ("eng", "idx", "fn", "edges", "signal", "amount", "sig_val", "n")

    def __init__(self, eng, idx, fn, amount, n):
        self.eng = eng
        self.idx = idx
        self.fn = fn
        self.edges = []
        self.signal = False
        self.amount = amount
        self.sig_val = None
        self.n = n


class Builder:
    def __init__(self, nc, stack):
        self.nc = nc
        self.stack = stack
        self.plans = {e: [] for e in ENGINES}
        self.sig = {e: stack.enter_context(nc.semaphore("sig_" + e)) for e in ENGINES}
        self.writer = {}
        self.reader = {}
        self.covered = {e: {f: -1 for f in ENGINES} for e in ENGINES}
        self.last_dma = {}

    def op(self, eng, fn, reads=(), writes=(), after=(), dma=False, name=""):
        plan = self.plans[eng]
        rec = Rec(eng, len(plan), fn, 16 if dma else 1, name)
        if dma:
            rec.signal = True
            prev = self.last_dma.get(eng)
            if prev is not None:
                after = tuple(after) + (prev,)
            self.last_dma[eng] = rec
        raw = []
        other = []
        for t in reads:
            w = self.writer.get(t)
            if w:
                raw.extend(w.values())
        for t in writes:
            w = self.writer.get(t)
            if w:
                other.extend(w.values())
            r = self.reader.get(t)
            if r:
                other.extend(r.values())
        raw.extend(after)
        cov = self.covered[eng]
        for d, is_raw in [(d, True) for d in raw] + [(d, False) for d in other]:
            if d is rec:
                continue
            if d.eng == eng and eng == "tensor" and not dma and d.amount == 1:
                continue
            if d.eng == eng and not is_raw and not dma and d.amount == 1:
                continue
            if d.idx <= cov[d.eng]:
                continue
            d.signal = True
            rec.edges.append(d)
            cov[d.eng] = max(cov[d.eng], d.idx)
        plan.append(rec)
        for t in reads:
            self.reader.setdefault(t, {})[eng] = rec
        for t in writes:
            self.writer.setdefault(t, {})[eng] = rec
            self.reader[t] = {eng: rec}
        return rec

    def barrier(self):
        lasts = [self.plans[e][-1] for e in ENGINES if self.plans[e]]
        for e in ENGINES:
            if self.plans[e]:
                self.op(e, lambda eng: eng.nop(), after=tuple(lasts), name="barrier")

    def finalize_and_emit(self):
        for e in ENGINES:
            tot = 0
            for rec in self.plans[e]:
                if rec.signal:
                    tot += rec.amount
                    rec.sig_val = tot
        nc = self.nc
        sig = self.sig
        with nc.Block() as blk:
            for ename in ENGINES:
                plan = self.plans[ename]
                if not plan:
                    continue

                def make_runner(plan=plan):
                    def run(eng):
                        for rec in plan:
                            if rec.edges:
                                waits = {}
                                for d in rec.edges:
                                    waits[d.eng] = max(waits.get(d.eng, 0), d.sig_val)
                                for e2, v in waits.items():
                                    eng.wait_ge(sig[e2], v)
                            inst = rec.fn(eng)
                            if rec.signal:
                                assert inst is not None, f"signaling rec {rec.n} has no inst"
                                inst.then_inc(sig[rec.eng], rec.amount)
                    return run

                getattr(blk, ename)(make_runner())


def _alibi_slopes(n_heads):
    start = 2 ** (-(2 ** -(math.log2(n_heads) - 3)))
    return [start * start ** i for i in range(n_heads)]


def _make_consts():
    id16 = np.eye(128, dtype=np.float32).astype(BF)
    id32 = np.eye(128, dtype=np.float32)
    j = np.arange(128)
    mneg = np.where(j[None, :] <= j[:, None], 0.0, NEGBIG).astype(np.float32)

    slopes = _alibi_slopes(H)
    ktmpl = np.zeros((128, S), dtype=np.float32)
    qtmpl = np.zeros((128, S), dtype=np.float32)
    khi = (np.arange(S) >> 8).astype(np.float32)
    klo = (np.arange(S) & 255).astype(np.float32)
    qth = (np.arange(S) >> 7).astype(np.float32)
    qtl = (np.arange(S) & 127).astype(np.float32)
    for h in range(H):
        sl = np.float64(slopes[h])
        sa = np.float32(BF(np.float32(sl)))
        sb = np.float32(BF(np.float32(sl) - sa))
        base = 8 * h
        ktmpl[base + 0] = khi
        ktmpl[base + 1] = khi
        ktmpl[base + 2] = klo
        ktmpl[base + 3] = klo
        ktmpl[base + 4] = -sa * 128.0
        ktmpl[base + 5] = -sb * 128.0
        ktmpl[base + 6] = -sa
        ktmpl[base + 7] = -sb
        qtmpl[base + 0] = sa * 256.0
        qtmpl[base + 1] = sb * 256.0
        qtmpl[base + 2] = sa
        qtmpl[base + 3] = sb
        qtmpl[base + 4] = qth
        qtmpl[base + 5] = qth
        qtmpl[base + 6] = qtl
        qtmpl[base + 7] = qtl
    return id16, id32, mneg, ktmpl.astype(BF), qtmpl.astype(BF)


def build_program(debug=False):
    stack = ExitStack()
    nc = bass.Bass("TRN2", target_bir_lowering=False)

    dx = nc.dram_tensor("x", [T, D], F32, kind="ExternalInput")
    dwq = nc.dram_tensor("wqt", [D, D], F32, kind="ExternalInput")
    dwk = nc.dram_tensor("wkt", [D, D], F32, kind="ExternalInput")
    dwv = nc.dram_tensor("wvt", [D, D], F32, kind="ExternalInput")
    dwo = nc.dram_tensor("wot", [D, D], F32, kind="ExternalInput")
    dbq = nc.dram_tensor("bq_cols", [128, 8], F32, kind="ExternalInput")
    dbk = nc.dram_tensor("bk_cols", [128, 8], F32, kind="ExternalInput")
    dbvb = nc.dram_tensor("bv_bc", [128, D], F32, kind="ExternalInput")
    dbob = nc.dram_tensor("bo_bc", [128, D], F32, kind="ExternalInput")
    dgw = nc.dram_tensor("gw_cols", [128, 64], F32, kind="ExternalInput")
    dgbb = nc.dram_tensor("gb_bc", [128, 8], F32, kind="ExternalInput")
    dw1 = nc.dram_tensor("w1", [NEXP, D, FF], F32, kind="ExternalInput")
    db1 = nc.dram_tensor("b1_cols", [128, 256], F32, kind="ExternalInput")
    dw2 = nc.dram_tensor("w2", [NEXP, FF, D], F32, kind="ExternalInput")
    db2 = nc.dram_tensor("b2", [NEXP, D], F32, kind="ExternalInput")
    did16 = nc.dram_tensor("c_id16", [128, 128], BF16, kind="ExternalInput")
    did32 = nc.dram_tensor("c_id32", [128, 128], F32, kind="ExternalInput")
    dmneg = nc.dram_tensor("c_mneg", [128, 128], F32, kind="ExternalInput")
    dktm = nc.dram_tensor("c_ktmpl", [128, S], BF16, kind="ExternalInput")
    dqtm = nc.dram_tensor("c_qtmpl", [128, S], BF16, kind="ExternalInput")

    dout = nc.dram_tensor("out", [T, D], F32, kind="ExternalOutput")
    g_scr = nc.dram_tensor("g_scr", [T, D], F32, kind="ExternalOutput" if debug else "Internal")
    xn_scr = nc.dram_tensor("xn_scr", [8, 128, T], BF16, kind="ExternalOutput" if debug else "Internal")

    if debug:
        dbg_k = nc.dram_tensor("dbg_k", [128, 8 * S], BF16, kind="ExternalOutput")
        dbg_q = nc.dram_tensor("dbg_q", [128, 8 * S], BF16, kind="ExternalOutput")
        dbg_v = nc.dram_tensor("dbg_v", [128, 16 * 1024], BF16, kind="ExternalOutput")
        dbg_y = nc.dram_tensor("dbg_y", [128, 8 * 1024], BF16, kind="ExternalOutput")

    b = Builder(nc, stack)

    id16 = stack.enter_context(nc.sbuf_tensor("id16", [128, 128], BF16))
    id32 = stack.enter_context(nc.sbuf_tensor("id32", [128, 128], F32))
    mneg = stack.enter_context(nc.sbuf_tensor("mneg", [128, 128], F32))
    ktm = stack.enter_context(nc.sbuf_tensor("ktm", [128, S], BF16))
    qtm = stack.enter_context(nc.sbuf_tensor("qtm", [128, S], BF16))
    bq8 = stack.enter_context(nc.sbuf_tensor("bq8", [128, 8], F32))
    bkc = stack.enter_context(nc.sbuf_tensor("bkc", [128, 8], F32))
    comb = stack.enter_context(nc.sbuf_tensor("comb", [128, NT * 8], F32))
    epsv = stack.enter_context(nc.sbuf_tensor("epsv", [128, 1], F32))
    b.op("vector", lambda e: e.memset(epsv[:, :], 1e-5), writes=("epsv",), name="eps")

    b.op("sync", lambda e: e.dma_start(out=id16[:, :], in_=did16[:, :]), writes=("id16",), dma=True, name="l_id16")
    b.op("sync", lambda e: e.dma_start(out=id32[:, :], in_=did32[:, :]), writes=("id32",), dma=True, name="l_id32")
    b.op("sync", lambda e: e.dma_start(out=mneg[:, :], in_=dmneg[:, :]), writes=("mneg",), dma=True, name="l_mneg")
    b.op("sync", lambda e: e.dma_start(out=ktm[:, :], in_=dktm[:, :]), writes=("ktm",), dma=True, name="l_ktm")
    b.op("sync", lambda e: e.dma_start(out=qtm[:, :], in_=dqtm[:, :]), writes=("qtm",), dma=True, name="l_qtm")
    b.op("sync", lambda e: e.dma_start(out=bkc[:, :], in_=dbk[:, :]), writes=("bkc",), dma=True, name="l_bk")
    b.op("sync", lambda e: e.dma_start(out=bq8[:, :], in_=dbq[:, :]), writes=("bq8",), dma=True, name="l_bq")
    b.op("vector", lambda e: e.tensor_scalar(bq8[:, :], bq8[:, :], 0.125, None, op0=ALU.mult),
         reads=("bq8",), writes=("bq8",), name="bq8s")

    with ExitStack() as scA:
        combT = scA.enter_context(nc.sbuf_tensor("combT", [8, NT * 128], F32))
        w1b0_pre = scA.enter_context(nc.sbuf_tensor("w1b0", [128, 4 * 4096], BF16))

        with ExitStack() as sc:
            sb0 = lambda n, s, d: sc.enter_context(nc.sbuf_tensor(n, s, d))
            ps0 = lambda n, s, d: sc.enter_context(nc.psum_tensor(n, s, d))
            xnT = sb0("xnT", [128, 8 * T], BF16)
            gw = sb0("gw", [128, 64], F32)
            gbb = sb0("gbb", [128, 8], F32)
            xb = [sb0(f"xb{i}", [128, D], F32) for i in range(2)]
            xn32 = [sb0(f"xn32_{i}", [128, D], F32) for i in range(2)]
            xn16 = [sb0(f"xn16_{i}", [128, D], BF16) for i in range(2)]
            xT32 = [sb0(f"xT32_{i}", [128, D], F32) for i in range(2)]
            junk = sb0("junk", [128, D], BF16)
            nsum = sb0("nsum", [128, 1], F32)
            negmean = sb0("negmean", [128, 1], F32)
            msq = sb0("msq", [128, 1], F32)
            var = sb0("var", [128, 1], F32)
            lnv = sb0("lnv", [128, 1], F32)
            rstd = sb0("rstd", [128, 1], F32)
            ssq = sb0("ssq", [128, 1], F32)
            logit = sb0("logit", [128, 8], F32)
            m1 = sb0("m1", [128, 1], F32)
            nm1 = sb0("nm1", [128, 1], F32)
            ew = sb0("ew", [128, 8], F32)
            ind1 = sb0("ind1", [128, 8], F32)
            wm = sb0("wm", [128, 8], F32)
            m2 = sb0("m2", [128, 1], F32)
            indt = sb0("indt", [128, 8], F32)
            cu = sb0("cu", [128, 8], F32)
            dsum = sb0("dsum", [128, 1], F32)
            rec_ = sb0("rec_", [128, 1], F32)
            pp16 = [ps0(f"pp16_{i}", [128, 1024], BF16) for i in range(2)]
            pp32 = [ps0(f"pp32_{i}", [128, 1024], F32) for i in range(2)]
            psg = ps0("psg", [128, 8], F32)
            psct = ps0("psct", [8, 512], F32)

            b.op("sync", lambda e: e.dma_start(out=gw[:, :], in_=dgw[:, :]), writes=("gw",), dma=True, name="l_gw")
            b.op("sync", lambda e: e.dma_start(out=gbb[:, :], in_=dgbb[:, :]), writes=("gbb",), dma=True, name="l_gbb")

            for i in range(NT):
                par = i % 2
                t0 = i * 128
                xbp, xn32p, xn16p, xT32p = xb[par], xn32[par], xn16[par], xT32[par]
                nx, n32, n16, nT32 = f"xb{par}", f"xn32_{par}", f"xn16_{par}", f"xT32_{par}"
                pp16p, pp32p = pp16[par], pp32[par]
                n_pp16, n_pp32 = f"pp16_{par}", f"pp32_{par}"

                b.op("sync", lambda e, d=xbp, s=t0: e.dma_start(out=d[:, :], in_=dx[s:s + 128, :]),
                     writes=(nx,), dma=True, name=f"ldx{i}")
                b.op("vector", lambda e, x=xbp: e.reduce_sum(nsum[:, :], x[:, :], axis=AX.X, negate=True),
                     reads=(nx,), writes=("nsum",), name=f"su{i}")
                b.op("scalar", lambda e, x=xbp: e.activation(junk[:, :], x[:, :], AF.Square, accum_out=ssq[:, :]),
                     reads=(nx,), writes=("junk", "ssq"), name=f"sq{i}")
                b.op("vector", lambda e: e.tensor_scalar(negmean[:, :], nsum[:, :], 1.0 / D, None, op0=ALU.mult),
                     reads=("nsum",), writes=("negmean",), name=f"nm{i}")
                b.op("vector", lambda e: e.tensor_mul(msq[:, :], negmean[:, :], negmean[:, :]),
                     reads=("negmean",), writes=("msq",), name=f"ms{i}")
                b.op("vector", lambda e: e.tensor_scalar(var[:, :], ssq[:, :], 1.0 / D, msq[:, :],
                                                         op0=ALU.mult, op1=ALU.subtract),
                     reads=("ssq", "msq"), writes=("var",), name=f"va{i}")
                b.op("scalar", lambda e: e.activation(lnv[:, :], var[:, :], AF.Ln, bias=epsv[:, :]),
                     reads=("var", "epsv"), writes=("lnv",), name=f"ln{i}")
                b.op("scalar", lambda e: e.activation(rstd[:, :], lnv[:, :], AF.Exp, scale=-0.5),
                     reads=("lnv",), writes=("rstd",), name=f"rs{i}")
                b.op("vector", lambda e, x=xbp, o=xn32p: e.tensor_scalar(
                        o[:, :], x[:, :], negmean[:, :], rstd[:, :], op0=ALU.add, op1=ALU.mult),
                     reads=(nx, "negmean", "rstd"), writes=(n32,), name=f"xn{i}")
                b.op("vector", lambda e, s=xn32p, o=xn16p: e.tensor_copy(o[:, :], s[:, :]),
                     reads=(n32,), writes=(n16,), name=f"cst{i}")

                def tp16(e, src=xn16p, dst=pp16p):
                    inst = None
                    for c in range(8):
                        inst = e.matmul(dst[:, c * 128:(c + 1) * 128], src[:, c * 128:(c + 1) * 128],
                                        id16[:, :], start=(c == 0), stop=(c == 7), is_transpose=True)
                    return inst
                b.op("tensor", tp16, reads=(n16, "id16"), writes=(n_pp16,), name=f"t16_{i}")

                def tp32(e, src=xn32p, dst=pp32p):
                    inst = None
                    for c in range(8):
                        inst = e.matmul(dst[:, c * 128:(c + 1) * 128], src[:, c * 128:(c + 1) * 128],
                                        id32[:, :], start=(c in (0, 4)), stop=(c in (3, 7)), is_transpose=True)
                    return inst
                b.op("tensor", tp32, reads=(n32, "id32"), writes=(n_pp32,), name=f"t32_{i}")

                def ev16(e, src=pp16p, t0=t0):
                    dst = bass.AP(xnT, t0, [[8 * T, 128], [T, 8], [1, 128]])
                    return e.tensor_copy(dst, src[:, :])
                b.op("vector", ev16, reads=(n_pp16,), writes=("xnT",), name=f"e16_{i}")
                b.op("vector", lambda e, src=pp32p, dst=xT32p: e.tensor_copy(dst[:, :], src[:, :]),
                     reads=(n_pp32,), writes=(nT32,), name=f"e32_{i}")

                def gmm(e, src=xT32p):
                    inst = None
                    for c in range(8):
                        inst = e.matmul(psg[:, :], src[:, c * 128:(c + 1) * 128],
                                        gw[:, c * 8:(c + 1) * 8], start=(c == 0), stop=(c == 7))
                    return inst
                b.op("tensor", gmm, reads=(nT32, "gw"), writes=("psg",), name=f"gm{i}")
                b.op("vector", lambda e: e.tensor_add(logit[:, :], psg[:, :], gbb[:, :]),
                     reads=("psg", "gbb"), writes=("logit",), name=f"lg{i}")
                b.op("vector", lambda e: e.reduce_max(m1[:, :], logit[:, :], axis=AX.X),
                     reads=("logit",), writes=("m1",), name=f"m1_{i}")
                b.op("vector", lambda e: e.tensor_scalar(nm1[:, :], m1[:, :], -1.0, None, op0=ALU.mult),
                     reads=("m1",), writes=("nm1",), name=f"nm1_{i}")
                b.op("scalar", lambda e: e.activation(ew[:, :], logit[:, :], AF.Exp, bias=nm1[:, :]),
                     reads=("logit", "nm1"), writes=("ew",), name=f"ew{i}")
                b.op("vector", lambda e: e.tensor_scalar(ind1[:, :], logit[:, :], m1[:, :], None, op0=ALU.is_ge),
                     reads=("logit", "m1"), writes=("ind1",), name=f"i1_{i}")
                b.op("vector", lambda e: e.tensor_scalar(ind1[:, :], ind1[:, :], -1.0, 1.0,
                                                         op0=ALU.mult, op1=ALU.add),
                     reads=("ind1",), writes=("ind1",), name=f"ni{i}")
                b.op("vector", lambda e: e.tensor_mul(wm[:, :], ew[:, :], ind1[:, :]),
                     reads=("ew", "ind1"), writes=("wm",), name=f"wm{i}")
                b.op("vector", lambda e: e.reduce_max(m2[:, :], wm[:, :], axis=AX.X),
                     reads=("wm",), writes=("m2",), name=f"m2_{i}")
                b.op("vector", lambda e: e.tensor_scalar(indt[:, :], ew[:, :], m2[:, :], None, op0=ALU.is_ge),
                     reads=("ew", "m2"), writes=("indt",), name=f"it{i}")
                b.op("vector", lambda e: e.tensor_mul(cu[:, :], ew[:, :], indt[:, :]),
                     reads=("ew", "indt"), writes=("cu",), name=f"cu{i}")
                b.op("vector", lambda e: e.reduce_sum(dsum[:, :], cu[:, :], axis=AX.X),
                     reads=("cu",), writes=("dsum",), name=f"ds{i}")
                b.op("vector", lambda e: e.tensor_scalar(dsum[:, :], dsum[:, :], 1e-6, None, op0=ALU.max),
                     reads=("dsum",), writes=("dsum",), name=f"dm{i}")
                b.op("vector", lambda e: e.reciprocal(rec_[:, :], dsum[:, :]),
                     reads=("dsum",), writes=("rec_",), name=f"rc{i}")
                b.op("vector", lambda e, i=i: e.tensor_scalar(comb[:, i * 8:(i + 1) * 8], cu[:, :],
                                                              rec_[:, :], None, op0=ALU.mult),
                     reads=("cu", "rec_"), writes=("comb",), name=f"cb{i}")

                def ctp(e, i=i):
                    return e.matmul(psct[0:8, (i % 4) * 128:(i % 4) * 128 + 128],
                                    comb[:, i * 8:(i + 1) * 8], id32[:, :],
                                    start=(i % 4 == 0), stop=(i % 4 == 3), is_transpose=True)
                b.op("tensor", ctp, reads=("comb", "id32"), writes=("psct",), name=f"ct{i}")
                if i % 4 == 3:
                    b.op("vector", lambda e, i=i: e.tensor_copy(combT[0:8, (i - 3) * 128:(i + 1) * 128],
                                                                psct[0:8, :]),
                         reads=("psct",), writes=("combT",), name=f"ce{i}")

            for c in range(8):
                b.op("sync", lambda e, c=c: e.dma_start(out=xn_scr[c, :, :], in_=xnT[:, c * T:(c + 1) * T]),
                     reads=("xnT",), writes=("xn_scr",), dma=True, name=f"sp{c}")

            for fc in range(4):
                w1src = dw1[0, :, fc * 512:(fc + 1) * 512].rearrange("(c p) f -> p c f", p=128)
                w1dst = bass.AP(w1b0_pre, fc * 4096, [[4 * 4096, 128], [512, 8], [1, 512]])
                b.op("gpsimd", lambda e, d=w1dst, s=w1src: e.dma_start(out=d, in_=s),
                     writes=("w1b0",), dma=True, name=f"w1l0_{fc}")

        b.barrier()

        with ExitStack() as sc:
            sb1 = lambda n, s, d: sc.enter_context(nc.sbuf_tensor(n, s, d))
            ps1 = lambda n, s, d: sc.enter_context(nc.psum_tensor(n, s, d))
            w1b = [w1b0_pre, sb1("w1b1", [128, 4 * 4096], BF16)]
            w2h = sb1("w2h", [128, 16 * 1024], BF16)
            hdnT = [sb1(f"hdnT{i}", [128, 16 * 512], BF16) for i in range(2)]
            xnb1 = [sb1(f"xnb1_{i}", [128, 8 * 512], BF16) for i in range(2)]
            b1c = sb1("b1c", [128, 256], F32)
            b2sb = sb1("b2sb", [8, D], F32)
            gout = [sb1(f"gout{i}", [128, D], F32) for i in range(2)]
            psa = [ps1(f"psa{i}", [128, 512], F32) for i in range(2)]
            psb = [ps1(f"psb{i}", [128, 1024], F32) for i in range(2)]

            b.op("sync", lambda e: e.dma_start(out=b1c[:, :], in_=db1[:, :]), writes=("b1c",), dma=True, name="l_b1")
            b.op("sync", lambda e: e.dma_start(out=b2sb[:, :], in_=db2[:, :]), writes=("b2sb",), dma=True, name="l_b2")

            for gt in range(NT):
                go, ngo = gout[gt % 2], f"gout{gt % 2}"
                pb, npb = psb[gt % 2], f"psb{gt % 2}"

                def b2mm(e, gt=gt, pb=pb):
                    e.matmul(pb[:, 0:512], combT[0:8, gt * 128:(gt + 1) * 128],
                             b2sb[0:8, 0:512], start=True, stop=True)
                    return e.matmul(pb[:, 512:1024], combT[0:8, gt * 128:(gt + 1) * 128],
                                    b2sb[0:8, 512:1024], start=True, stop=True)
                b.op("tensor", b2mm, reads=("combT", "b2sb"), writes=(npb,), name=f"b2m{gt}")
                b.op("vector", lambda e, pb=pb, go=go: e.tensor_copy(go[:, :], pb[:, :]),
                     reads=(npb,), writes=(ngo,), name=f"b2c{gt}")
                b.op("gpsimd", lambda e, go=go, gt=gt: e.dma_start(
                        out=g_scr[gt * 128:(gt + 1) * 128, :], in_=go[:, :]),
                     reads=(ngo,), writes=(f"g{gt}",), dma=True, name=f"b2s{gt}")

            def load_w1(rnd):
                ex, fh = rnd // 2, rnd % 2
                wp, nwp = w1b[rnd % 2], f"w1b{rnd % 2}"
                for fc in range(4):
                    src = dw1[ex, :, fh * 2048 + fc * 512: fh * 2048 + (fc + 1) * 512] \
                        .rearrange("(c p) f -> p c f", p=128)
                    dst = bass.AP(wp, fc * 4096, [[4 * 4096, 128], [512, 8], [1, 512]])
                    b.op("gpsimd", lambda e, d=dst, s=src: e.dma_start(out=d, in_=s),
                         writes=(nwp,), dma=True, name=f"w1l{rnd}_{fc}")

            for rnd in range(16):
                ex, fh = rnd // 2, rnd % 2
                w1p, nw1 = w1b[rnd % 2], f"w1b{rnd % 2}"
                for fc in range(4):
                    src = dw2[ex, fh * 2048 + fc * 512: fh * 2048 + (fc + 1) * 512, :] \
                        .rearrange("(c p) h -> p c h", p=128)
                    dst = bass.AP(w2h, fc * 4 * 1024, [[16 * 1024, 128], [1024, 4], [1, 1024]])
                    b.op("gpsimd", lambda e, d=dst, s=src: e.dma_start(out=d, in_=s),
                         writes=("w2h",), dma=True, name=f"w2l{rnd}_{fc}")
                if rnd + 1 < 16:
                    load_w1(rnd + 1)

                for tg in range(8):
                    tpos = tg * 512
                    xp, nxp = xnb1[tg % 2], f"xnb1_{tg % 2}"
                    src = xn_scr[:, :, tpos:tpos + 512].rearrange("c p t -> p c t")
                    b.op("sync", lambda e, d=xp, s=src: e.dma_start(
                            out=d.ap().rearrange("p (c t) -> p c t", c=8), in_=s),
                         reads=("xn_scr",), writes=(nxp,), dma=True, name=f"xl{rnd}_{tg}")

                    for fb in range(16):
                        fc, fs = fb // 4, fb % 4
                        pa, npa = psa[fb % 2], f"psa{fb % 2}"

                        def amm(e, w1p=w1p, fc=fc, fs=fs, pa=pa, xp=xp):
                            inst = None
                            for dc in range(8):
                                inst = e.matmul(
                                    pa[:, :],
                                    w1p[:, fc * 4096 + dc * 512 + fs * 128: fc * 4096 + dc * 512 + fs * 128 + 128],
                                    xp[:, dc * 512:(dc + 1) * 512],
                                    start=(dc == 0), stop=(dc == 7))
                            return inst
                        b.op("tensor", amm, reads=(nw1, nxp), writes=(npa,), name=f"am{rnd}_{tg}_{fb}")

                        bcol = ex * 32 + fh * 16 + fb
                        hb, nhb = hdnT[tg % 2], f"hdnT{tg % 2}"
                        b.op("scalar", lambda e, pa=pa, fb=fb, bcol=bcol, hb=hb: e.activation(
                                hb[:, fb * 512:(fb + 1) * 512], pa[:, :], AF.Gelu,
                                bias=b1c[:, bcol:bcol + 1]),
                             reads=(npa, "b1c"), writes=(nhb,), name=f"ge{rnd}_{tg}_{fb}")

                    for tl in range(4):
                        gt = tg * 4 + tl
                        pb, npb = psb[tl % 2], f"psb{tl % 2}"
                        go, ngo = gout[tl % 2], f"gout{tl % 2}"

                        hb, nhb = hdnT[tg % 2], f"hdnT{tg % 2}"
                        def bmm(e, tl=tl, pb=pb, hb=hb):
                            inst = None
                            for fb in range(16):
                                lhs = hb[:, fb * 512 + tl * 128: fb * 512 + tl * 128 + 128]
                                inst = e.matmul(pb[:, 0:512], lhs, w2h[:, fb * 1024: fb * 1024 + 512],
                                                start=(fb == 0), stop=(fb == 15))
                                inst = e.matmul(pb[:, 512:1024], lhs, w2h[:, fb * 1024 + 512: fb * 1024 + 1024],
                                                start=(fb == 0), stop=(fb == 15))
                            return inst
                        b.op("tensor", bmm, reads=(nhb, "w2h"), writes=(npb,), name=f"bm{rnd}_{gt}")
                        b.op("vector", lambda e, pb=pb, go=go, gt=gt, ex=ex: e.tensor_scalar(
                                go[:, :], pb[:, :], comb[:, gt * 8 + ex: gt * 8 + ex + 1], None, op0=ALU.mult),
                             reads=(npb, "comb"), writes=(ngo,), name=f"be{rnd}_{gt}")
                        b.op("gpsimd", lambda e, go=go, gt=gt: e.dma_start(
                                out=g_scr[gt * 128:(gt + 1) * 128, :], in_=go[:, :], accum_op=ALU.add),
                             reads=(ngo,), writes=(f"g{gt}",), dma=True, name=f"ga{rnd}_{gt}")

        b.barrier()

    with ExitStack() as sc:
        sb2 = lambda n, s, d: sc.enter_context(nc.sbuf_tensor(n, s, d))
        kT = sb2("kT", [128, 8 * S], BF16)
        qT2 = sb2("qT2", [128, 8 * S], BF16)
        v_tm = sb2("v_tm", [128, 16 * 1024], BF16)
        yT = sb2("yT", [128, 8 * 1024], BF16)
        bvb = sb2("bvb", [128, D], F32)
        bob = sb2("bob", [128, D], F32)
        lcur = sb2("lcur", [128, 1], F32)
        rl2 = sb2("rl2", [128, 1], F32)
        mstat = sb2("mstat", [128, 2], F32)
        lc2 = sb2("lc2", [128, 2], F32)
        negm = sb2("negm", [128, 1], F32)
        stmp = sb2("stmp", [128, D], F32)
        sgb = sb2("sgb", [128, D], F32)

        b.op("sync", lambda e: e.dma_start(out=bvb[:, :], in_=dbvb[:, :]), writes=("bvb",), dma=True, name="l_bvb")
        b.op("sync", lambda e: e.dma_start(out=bob[:, :], in_=dbob[:, :]), writes=("bob",), dma=True, name="l_bob")

        for sq in range(SEQS):
            soff = sq * S
            with ExitStack() as sp:
                sbp = lambda n, s, d: sp.enter_context(nc.sbuf_tensor(n, s, d))
                psp = lambda n, s, d: sp.enter_context(nc.psum_tensor(n, s, d))
                wkb = sbp(f"wkb{sq}", [128, 8 * 1024], BF16)
                wqb = sbp(f"wqb{sq}", [128, 8 * 1024], BF16)
                wvb = sbp(f"wvb{sq}", [128, 8 * 1024], BF16)
                xnb = [sbp(f"xnb2_{sq}_{i}", [128, 8 * 512], BF16) for i in range(2)]
                pko = [psp(f"pko{sq}_{i}", [128, 512], F32) for i in range(2)]
                psv = psp(f"psv{sq}", [128, 1024], F32)

                for wdst, wsrc, wname in ((wkb, dwk, f"wkb{sq}"), (wqb, dwq, f"wqb{sq}"), (wvb, dwv, f"wvb{sq}")):
                    src = wsrc[:, :].rearrange("(c p) i -> p c i", p=128)
                    b.op("gpsimd", lambda e, dd=wdst, s=src: e.dma_start(
                            out=dd.ap().rearrange("p (c i) -> p c i", c=8), in_=s),
                         writes=(wname,), dma=True, name=f"lw_{wname}")

                for tc in range(4):
                    xp, nxp = xnb[tc % 2], f"xnb2_{sq}_{tc % 2}"
                    src = xn_scr[:, :, soff + tc * 512: soff + (tc + 1) * 512].rearrange("c p t -> p c t")
                    b.op("sync", lambda e, d=xp, s=src: e.dma_start(
                            out=d.ap().rearrange("p (c t) -> p c t", c=8), in_=s),
                         reads=("xn_scr",), writes=(nxp,), dma=True, name=f"x2{sq}_{tc}")

                    for proj, wb, nwb, dstT, ndT, bias, scale in (
                            ("k", wkb, f"wkb{sq}", kT, "kT", bkc, 1.0),
                            ("q", wqb, f"wqb{sq}", qT2, "qT2", bq8, 0.125)):
                        for ic in range(8):
                            pk, npk = pko[ic % 2], f"pko{sq}_{ic % 2}"

                            def pmm(e, xp=xp, ic=ic, pk=pk, wb=wb):
                                inst = None
                                for dc in range(8):
                                    inst = e.matmul(pk[:, :],
                                                    wb[:, dc * 1024 + ic * 128: dc * 1024 + ic * 128 + 128],
                                                    xp[:, dc * 512:(dc + 1) * 512],
                                                    start=(dc == 0), stop=(dc == 7))
                                return inst
                            b.op("tensor", pmm, reads=(nwb, nxp), writes=(npk,), name=f"{proj}m{sq}_{tc}_{ic}")
                            b.op("scalar", lambda e, pk=pk, ic=ic, tc=tc, dstT=dstT, bias=bias, scale=scale:
                                    e.activation(dstT[:, ic * S + tc * 512: ic * S + (tc + 1) * 512], pk[:, :],
                                                 AF.Identity, bias=bias[:, ic:ic + 1], scale=scale),
                                 reads=(npk, "bkc", "bq8"), writes=(ndT,), name=f"{proj}e{sq}_{tc}_{ic}")

                    for tl in range(4):
                        tj = tc * 4 + tl

                        def vmm(e, xp=xp, tl=tl, wb=wvb):
                            inst = None
                            for dc in range(8):
                                lhs = xp[:, dc * 512 + tl * 128: dc * 512 + tl * 128 + 128]
                                inst = e.matmul(psv[:, 0:512], lhs, wb[:, dc * 1024: dc * 1024 + 512],
                                                start=(dc == 0), stop=(dc == 7))
                                inst = e.matmul(psv[:, 512:1024], lhs, wb[:, dc * 1024 + 512: dc * 1024 + 1024],
                                                start=(dc == 0), stop=(dc == 7))
                            return inst
                        b.op("tensor", vmm, reads=(f"wvb{sq}", nxp), writes=(f"psv{sq}",), name=f"vm{sq}_{tc}_{tl}")
                        b.op("vector", lambda e, tj=tj: e.scalar_tensor_tensor(
                                v_tm[:, tj * 1024:(tj + 1) * 1024], psv[:, :], 1.0, bvb[:, :],
                                op0=ALU.mult, op1=ALU.add),
                             reads=(f"psv{sq}", "bvb"), writes=("v_tm",), name=f"ve{sq}_{tc}_{tl}")

            if debug and sq == 0:
                b.op("sync", lambda e: e.dma_start(out=dbg_k[:, :], in_=kT[:, :]),
                     reads=("kT",), dma=True, name="dbgk")
                b.op("sync", lambda e: e.dma_start(out=dbg_q[:, :], in_=qT2[:, :]),
                     reads=("qT2",), dma=True, name="dbgq")
                b.op("sync", lambda e: e.dma_start(out=dbg_v[:, :], in_=v_tm[:, :]),
                     reads=("v_tm",), dma=True, name="dbgv")
            b.barrier()

            with ExitStack() as sp:
                sbp = lambda n, s, d: sp.enter_context(nc.sbuf_tensor(n, s, d))
                psp = lambda n, s, d: sp.enter_context(nc.psum_tensor(n, s, d))
                wob = sbp(f"wob{sq}", [128, 8 * 1024], BF16)
                kqa = [sbp(f"kqa{sq}_{i}", [72, S], BF16) for i in range(2)]
                qaug = [sbp(f"qaug{sq}_{i}", [72, 1024], BF16) for i in range(2)]
                Ebuf = [sbp(f"Eb{sq}_{i}", [128, S], BF16) for i in range(2)]
                ETb = [sbp(f"ET{sq}_{i}", [128, 16 * 128], BF16) for i in range(2)]
                gti = [sbp(f"gti{sq}_{i}", [128, D], F32) for i in range(2)]
                xti = [sbp(f"xti{sq}_{i}", [128, D], F32) for i in range(2)]
                outb = [sbp(f"outb{sq}_{i}", [128, D], F32) for i in range(2)]
                pssp = [psp(f"pss{sq}_{i}", [128, 1024], F32) for i in range(2)]
                pset = psp(f"pset{sq}", [128, 1024], BF16)
                psy = psp(f"psy{sq}", [64, 128], F32)
                pso = psp(f"pso{sq}", [128, 1024], F32)

                src = dwo[:, :].rearrange("(c p) i -> p c i", p=128)
                b.op("gpsimd", lambda e, s=src: e.dma_start(
                        out=wob.ap().rearrange("p (c i) -> p c i", c=8), in_=s),
                     writes=(f"wob{sq}",), dma=True, name=f"lwo{sq}")

                for half in range(2):
                    pending = None
                    for h in range(H):
                        hp = h % 2
                        kq, nkq = kqa[hp], f"kqa{sq}_{hp}"
                        qa, nqa = qaug[hp], f"qaug{sq}_{hp}"
                        b.op("sync", lambda e, kq=kq, h=h: e.dma_start(
                                out=kq[0:64, :],
                                in_=kT[(h % 2) * 64:(h % 2) * 64 + 64, (h // 2) * S:(h // 2 + 1) * S]),
                             reads=("kT",), writes=(nkq,), dma=True, name=f"bk{sq}_{half}_{h}")
                        b.op("sync", lambda e, kq=kq, h=h: e.dma_start(
                                out=kq[64:72, :], in_=ktm[8 * h:8 * h + 8, :]),
                             reads=("ktm",), writes=(nkq,), dma=True, name=f"bkt{sq}_{half}_{h}")
                        b.op("sync", lambda e, qa=qa, h=h, half=half: e.dma_start(
                                out=qa[0:64, :],
                                in_=qT2[(h % 2) * 64:(h % 2) * 64 + 64,
                                        (h // 2) * S + half * 1024:(h // 2) * S + half * 1024 + 1024]),
                             reads=("qT2",), writes=(nqa,), dma=True, name=f"bq{sq}_{half}_{h}")
                        b.op("sync", lambda e, qa=qa, h=h, half=half: e.dma_start(
                                out=qa[64:72, :], in_=qtm[8 * h:8 * h + 8, half * 1024: half * 1024 + 1024]),
                             reads=("qtm",), writes=(nqa,), dma=True, name=f"bqt{sq}_{half}_{h}")

                        for A in range(8):
                            Ag = half * 8 + A
                            W = (Ag + 1) * 128
                            Ep, nEp = Ebuf[A % 2], f"Eb{sq}_{A % 2}"
                            ETp, nETp = ETb[A % 2], f"ET{sq}_{A % 2}"
                            if Ag <= 7:
                                pieces = [(A % 2, 0, W)]
                            else:
                                pieces = [(0, 0, 1024), (1, 1024, W)]
                            tag = f"{sq}_{half}_{h}_{A}"

                            n512 = Ag // 4
                            rem = Ag % 4
                            chunks = [(c * 512, (c + 1) * 512, True, True) for c in range(n512)]
                            if rem:
                                chunks.append((n512 * 512, n512 * 512 + rem * 128, True, False))
                                chunks.append((W - 128, W, False, True))
                            else:
                                chunks.append((W - 128, W, True, True))

                            for pi, (pb, lo, hi) in enumerate(pieces):
                                ps_ = pssp[pb]
                                nps = f"pss{sq}_{pb}"

                                def smm(e, qa=qa, kq=kq, A=A, ps_=ps_, lo=lo, hi=hi, chunks=tuple(chunks)):
                                    inst = None
                                    lhs = qa[0:72, A * 128:(A + 1) * 128]
                                    for gl, gh, st, sp in chunks:
                                        if gl < lo or gl >= hi:
                                            continue
                                        inst = e.matmul(ps_[:, gl - lo:gh - lo], lhs,
                                                        kq[0:72, gl:gh], start=st, stop=sp)
                                    return inst
                                b.op("tensor", smm, reads=(nqa, nkq), writes=(nps,), name=f"s{tag}_{pi}")

                            if pending is not None:
                                pending()
                                pending = None

                            lastpb, lastlo, lasthi = pieces[-1]
                            nlast = f"pss{sq}_{lastpb}"
                            if len(pieces) == 1:
                                ps_, lo = pssp[pieces[0][0]], pieces[0][1]
                                b.op("vector", lambda e, ps_=ps_, W=W: e.tensor_add(
                                        ps_[:, W - 128:W], ps_[:, W - 128:W], mneg[:, :]),
                                     reads=(nlast, "mneg"), writes=(nlast,), name=f"k{tag}")
                                b.op("vector", lambda e, ps_=ps_, W=W: e.reduce_max(
                                        negm[:, :], ps_[:, 0:W], axis=AX.X, negate=True),
                                     reads=(nlast,), writes=("negm",), name=f"r{tag}")
                                b.op("scalar", lambda e, Ep=Ep, ps_=ps_, W=W: e.activation(
                                        Ep[:, 0:W], ps_[:, 0:W], AF.Exp, bias=negm[:, :],
                                        accum_out=lcur[:, :]),
                                     reads=(nlast, "negm"), writes=(nEp, "lcur"), name=f"x{tag}")
                            else:
                                p0n, p1n = f"pss{sq}_0", f"pss{sq}_1"
                                b.op("vector", lambda e: e.reduce_max(
                                        mstat[:, 0:1], pssp[0][:, :], axis=AX.X),
                                     reads=(p0n,), writes=("mstat",), name=f"r0{tag}")
                                b.op("vector", lambda e, W=W: e.tensor_add(
                                        pssp[1][:, W - 1152:W - 1024], pssp[1][:, W - 1152:W - 1024], mneg[:, :]),
                                     reads=(p1n, "mneg"), writes=(p1n,), name=f"k{tag}")
                                b.op("vector", lambda e, W=W: e.reduce_max(
                                        mstat[:, 1:2], pssp[1][:, 0:W - 1024], axis=AX.X),
                                     reads=(p1n,), writes=("mstat",), name=f"r1{tag}")
                                b.op("vector", lambda e: e.reduce_max(
                                        negm[:, :], mstat[:, 0:2], axis=AX.X, negate=True),
                                     reads=("mstat",), writes=("negm",), name=f"rm{tag}")
                                b.op("scalar", lambda e, Ep=Ep: e.activation(
                                        Ep[:, 0:1024], pssp[0][:, :], AF.Exp, bias=negm[:, :],
                                        accum_out=lc2[:, 0:1]),
                                     reads=(p0n, "negm"), writes=(nEp, "lc2"), name=f"x0{tag}")
                                b.op("scalar", lambda e, Ep=Ep, W=W: e.activation(
                                        Ep[:, 1024:W], pssp[1][:, 0:W - 1024], AF.Exp, bias=negm[:, :],
                                        accum_out=lc2[:, 1:2]),
                                     reads=(p1n, "negm"), writes=(nEp, "lc2"), name=f"x1{tag}")
                                b.op("vector", lambda e: e.tensor_add(lcur[:, :], lc2[:, 0:1], lc2[:, 1:2]),
                                     reads=("lc2",), writes=("lcur",), name=f"la{tag}")
                            b.op("vector", lambda e: e.reciprocal(rl2[:, :], lcur[:, :]),
                                 reads=("lcur",), writes=("rl2",), name=f"rl{tag}")
                            b.op("vector", lambda e, Ep=Ep, W=W: e.tensor_scalar(
                                    Ep[:, 0:W], Ep[:, 0:W], rl2[:, :], None, op0=ALU.mult),
                                 reads=(nEp, "rl2"), writes=(nEp,), name=f"en{tag}")

                            def make_tail(Ep=Ep, nEp=nEp, ETp=ETp, nETp=nETp, h=h, A=A, Ag=Ag, tag=tag):
                                def tail():
                                    ngroups = (Ag + 8) // 8
                                    for g in range(ngroups):
                                        j0, j1 = g * 8, min(g * 8 + 8, Ag + 1)

                                        def tpe(e, Ep=Ep, j0=j0, j1=j1):
                                            inst = None
                                            for j in range(j0, j1):
                                                inst = e.matmul(pset[:, (j - j0) * 128:(j - j0 + 1) * 128],
                                                                Ep[:, j * 128:(j + 1) * 128], id16[:, :],
                                                                start=(j == j0), stop=(j == j1 - 1),
                                                                is_transpose=True)
                                            return inst
                                        b.op("tensor", tpe, reads=(nEp, "id16"), writes=("pset",),
                                             name=f"t{tag}_{g}")
                                        b.op("scalar", lambda e, ETp=ETp, j0=j0, j1=j1: e.copy(
                                                ETp[:, j0 * 128:j1 * 128], pset[:, 0:(j1 - j0) * 128]),
                                             reads=("pset",), writes=(nETp,), name=f"u{tag}_{g}")

                                    def avm(e, ETp=ETp, h=h, Ag=Ag):
                                        inst = None
                                        for jj in range(Ag + 1):
                                            inst = e.matmul(psy[0:64, :],
                                                            v_tm[:, jj * 1024 + h * 64: jj * 1024 + h * 64 + 64],
                                                            ETp[:, jj * 128:(jj + 1) * 128],
                                                            start=(jj == 0), stop=(jj == Ag))
                                        return inst
                                    b.op("tensor", avm, reads=(nETp, "v_tm"), writes=("psy",), name=f"a{tag}")
                                    b.op("vector", lambda e, h=h, A=A: e.tensor_copy(
                                            yT[(h % 2) * 64:(h % 2) * 64 + 64,
                                               (h // 2) * 1024 + A * 128:(h // 2) * 1024 + A * 128 + 128],
                                            psy[0:64, :]),
                                         reads=("psy",), writes=("yT",), name=f"y{tag}")
                                return tail
                            pending = make_tail()
                    if pending is not None:
                        pending()
                        pending = None

                    if debug and sq == 0 and half == 0:
                        b.op("sync", lambda e: e.dma_start(out=dbg_y[:, :], in_=yT[:, :]),
                             reads=("yT",), dma=True, name="dbgy")
                    for tl in range(8):
                        t0 = soff + half * 1024 + tl * 128
                        gp, ngp = gti[tl % 2], f"gti{sq}_{tl % 2}"
                        xp2, nxp2 = xti[tl % 2], f"xti{sq}_{tl % 2}"
                        ob, nob = outb[tl % 2], f"outb{sq}_{tl % 2}"
                        gt_glob = t0 // 128
                        b.op("sync", lambda e, gp=gp, t0=t0: e.dma_start(out=gp[:, :], in_=g_scr[t0:t0 + 128, :]),
                             reads=(f"g{gt_glob}",), writes=(ngp,), dma=True, name=f"lg{sq}_{half}_{tl}")
                        b.op("sync", lambda e, xp2=xp2, t0=t0: e.dma_start(out=xp2[:, :], in_=dx[t0:t0 + 128, :]),
                             writes=(nxp2,), dma=True, name=f"lx{sq}_{half}_{tl}")

                        def omm(e, tl=tl):
                            inst = None
                            for ic in range(8):
                                lhs = yT[:, ic * 1024 + tl * 128: ic * 1024 + tl * 128 + 128]
                                inst = e.matmul(pso[:, 0:512], lhs, wob[:, ic * 1024: ic * 1024 + 512],
                                                start=(ic == 0), stop=(ic == 7))
                                inst = e.matmul(pso[:, 512:1024], lhs, wob[:, ic * 1024 + 512: ic * 1024 + 1024],
                                                start=(ic == 0), stop=(ic == 7))
                            return inst
                        b.op("tensor", omm, reads=("yT", f"wob{sq}"), writes=(f"pso{sq}",), name=f"o{sq}_{half}_{tl}")
                        b.op("vector", lambda e: e.scalar_tensor_tensor(
                                stmp[:, :], pso[:, :], 1.0, bob[:, :], op0=ALU.mult, op1=ALU.add),
                             reads=(f"pso{sq}", "bob"), writes=("stmp",), name=f"se{sq}_{half}_{tl}")
                        b.op("scalar", lambda e, gp=gp: e.activation(gp[:, :], gp[:, :], AF.Silu),
                             reads=(ngp,), writes=(ngp,), name=f"si{sq}_{half}_{tl}")
                        b.op("vector", lambda e, gp=gp: e.tensor_mul(sgb[:, :], stmp[:, :], gp[:, :]),
                             reads=("stmp", ngp), writes=("sgb",), name=f"sg{sq}_{half}_{tl}")
                        b.op("vector", lambda e, xp2=xp2, ob=ob: e.tensor_add(ob[:, :], sgb[:, :], xp2[:, :]),
                             reads=("sgb", nxp2), writes=(nob,), name=f"oa{sq}_{half}_{tl}")
                        b.op("sync", lambda e, ob=ob, t0=t0: e.dma_start(out=dout[t0:t0 + 128, :], in_=ob[:, :]),
                             reads=(nob,), writes=("out",), dma=True, name=f"so{sq}_{half}_{tl}")

            if sq + 1 < SEQS:
                b.barrier()

        final_out = b.writer["out"]["sync"]
        b.op("sync", lambda e: e.nop(), after=(final_out,), name="final")

    b.finalize_and_emit()
    stack.close()
    return nc


_cached = {}


def _get_program():
    if "nc" not in _cached:
        _cached["nc"] = build_program()
    return _cached["nc"]


def kernel(L0_attn_x, L0_attn_Wq, L0_attn_bq, L0_attn_Wk, L0_attn_bk,
           L0_attn_Wv, L0_attn_bv, L0_attn_Wo, L0_attn_bo,
           L0_ffn_gate_w, L0_ffn_gate_b, L0_ffn_w1, L0_ffn_b1,
           L0_ffn_w2, L0_ffn_b2):
    f32 = lambda a: np.ascontiguousarray(np.asarray(a, dtype=np.float32))
    x = f32(L0_attn_x)
    Wq, Wk, Wv, Wo = map(f32, (L0_attn_Wq, L0_attn_Wk, L0_attn_Wv, L0_attn_Wo))
    bq, bk, bv, bo = map(f32, (L0_attn_bq, L0_attn_bk, L0_attn_bv, L0_attn_bo))
    gw, gb = f32(L0_ffn_gate_w), f32(L0_ffn_gate_b)
    w1, b1, w2, b2 = map(f32, (L0_ffn_w1, L0_ffn_b1, L0_ffn_w2, L0_ffn_b2))

    id16, id32, mneg, ktmpl, qtmpl = _make_consts()
    cols = lambda v: np.ascontiguousarray(v.reshape(8, 128).T)
    common = dict(
        wqt=np.ascontiguousarray(Wq.T), wkt=np.ascontiguousarray(Wk.T),
        wvt=np.ascontiguousarray(Wv.T), wot=np.ascontiguousarray(Wo.T),
        bq_cols=cols(bq), bk_cols=cols(bk),
        bv_bc=np.ascontiguousarray(np.broadcast_to(bv, (128, D))),
        bo_bc=np.ascontiguousarray(np.broadcast_to(bo, (128, D))),
        gw_cols=np.ascontiguousarray(gw.reshape(8, 128, 8).transpose(1, 0, 2).reshape(128, 64)),
        gb_bc=np.ascontiguousarray(np.broadcast_to(gb, (128, 8))),
        w1=w1, w2=w2,
        b1_cols=np.ascontiguousarray(b1.reshape(8, 32, 128).transpose(2, 0, 1).reshape(128, 256)),
        b2=b2,
        c_id16=id16, c_id32=id32, c_mneg=mneg, c_ktmpl=ktmpl, c_qtmpl=qtmpl,
    )
    in_maps = []
    for core in range(8):
        m = dict(common)
        m["x"] = np.ascontiguousarray(x[core * SEQS:(core + 1) * SEQS].reshape(T, D))
        in_maps.append(m)

    nc = _get_program()
    res = run_bass_kernel_spmd(nc, in_maps, core_ids=list(range(8)))
    out = np.concatenate([r["out"].reshape(SEQS, S, D) for r in res.results], axis=0)
    return out


# version 07
# speedup vs baseline: 1.7287x; 1.0414x over previous
import math
from contextlib import ExitStack

import numpy as np
import ml_dtypes

import concourse.bass as bass
import concourse.mybir as mybir
from concourse.bass_utils import run_bass_kernel_spmd

F32 = mybir.dt.float32
BF16 = mybir.dt.bfloat16
AF = mybir.ActivationFunctionType
ALU = mybir.AluOpType
AX = mybir.AxisListType
BF = ml_dtypes.bfloat16

D = 1024
H = 16
DH = 64
S = 2048
NEXP = 8
FF = 4096
SEQS = 2
T = SEQS * S
NT = T // 128
NEGBIG = -3.0e38

ENGINES = ("tensor", "vector", "scalar", "gpsimd", "sync")


class Rec:
    __slots__ = ("eng", "idx", "fn", "edges", "signal", "amount", "sig_val", "n")

    def __init__(self, eng, idx, fn, amount, n):
        self.eng = eng
        self.idx = idx
        self.fn = fn
        self.edges = []
        self.signal = False
        self.amount = amount
        self.sig_val = None
        self.n = n


class Builder:
    def __init__(self, nc, stack):
        self.nc = nc
        self.stack = stack
        self.plans = {e: [] for e in ENGINES}
        self.sig = {e: stack.enter_context(nc.semaphore("sig_" + e)) for e in ENGINES}
        self.writer = {}
        self.reader = {}
        self.covered = {e: {f: -1 for f in ENGINES} for e in ENGINES}
        self.last_dma = {}

    def op(self, eng, fn, reads=(), writes=(), after=(), dma=False, name=""):
        plan = self.plans[eng]
        rec = Rec(eng, len(plan), fn, 16 if dma else 1, name)
        if dma:
            rec.signal = True
            prev = self.last_dma.get(eng)
            if prev is not None:
                after = tuple(after) + (prev,)
            self.last_dma[eng] = rec
        raw = []
        other = []
        for t in reads:
            w = self.writer.get(t)
            if w:
                raw.extend(w.values())
        for t in writes:
            w = self.writer.get(t)
            if w:
                other.extend(w.values())
            r = self.reader.get(t)
            if r:
                other.extend(r.values())
        raw.extend(after)
        cov = self.covered[eng]
        for d, is_raw in [(d, True) for d in raw] + [(d, False) for d in other]:
            if d is rec:
                continue
            if d.eng == eng and eng == "tensor" and not dma and d.amount == 1:
                continue
            if d.eng == eng and not is_raw and not dma and d.amount == 1:
                continue
            if d.idx <= cov[d.eng]:
                continue
            d.signal = True
            rec.edges.append(d)
            cov[d.eng] = max(cov[d.eng], d.idx)
        plan.append(rec)
        for t in reads:
            self.reader.setdefault(t, {})[eng] = rec
        for t in writes:
            self.writer.setdefault(t, {})[eng] = rec
            self.reader[t] = {eng: rec}
        return rec

    def barrier(self):
        lasts = [self.plans[e][-1] for e in ENGINES if self.plans[e]]
        for e in ENGINES:
            if self.plans[e]:
                self.op(e, lambda eng: eng.nop(), after=tuple(lasts), name="barrier")

    def finalize_and_emit(self):
        for e in ENGINES:
            tot = 0
            for rec in self.plans[e]:
                if rec.signal:
                    tot += rec.amount
                    rec.sig_val = tot
        nc = self.nc
        sig = self.sig
        with nc.Block() as blk:
            for ename in ENGINES:
                plan = self.plans[ename]
                if not plan:
                    continue

                def make_runner(plan=plan):
                    def run(eng):
                        for rec in plan:
                            if rec.edges:
                                waits = {}
                                for d in rec.edges:
                                    waits[d.eng] = max(waits.get(d.eng, 0), d.sig_val)
                                for e2, v in waits.items():
                                    eng.wait_ge(sig[e2], v)
                            inst = rec.fn(eng)
                            if rec.signal:
                                assert inst is not None, f"signaling rec {rec.n} has no inst"
                                inst.then_inc(sig[rec.eng], rec.amount)
                    return run

                getattr(blk, ename)(make_runner())


def _alibi_slopes(n_heads):
    start = 2 ** (-(2 ** -(math.log2(n_heads) - 3)))
    return [start * start ** i for i in range(n_heads)]


def _make_consts():
    id16 = np.eye(128, dtype=np.float32).astype(BF)
    id32 = np.eye(128, dtype=np.float32)
    j = np.arange(128)
    mneg = np.where(j[None, :] <= j[:, None], 0.0, NEGBIG).astype(np.float32)

    slopes = _alibi_slopes(H)
    ktmpl = np.zeros((128, S), dtype=np.float32)
    qtmpl = np.zeros((128, S), dtype=np.float32)
    khi = (np.arange(S) >> 8).astype(np.float32)
    klo = (np.arange(S) & 255).astype(np.float32)
    qth = (np.arange(S) >> 7).astype(np.float32)
    qtl = (np.arange(S) & 127).astype(np.float32)
    for h in range(H):
        sl = np.float64(slopes[h])
        sa = np.float32(BF(np.float32(sl)))
        sb = np.float32(BF(np.float32(sl) - sa))
        base = 8 * h
        ktmpl[base + 0] = khi
        ktmpl[base + 1] = khi
        ktmpl[base + 2] = klo
        ktmpl[base + 3] = klo
        ktmpl[base + 4] = -sa * 128.0
        ktmpl[base + 5] = -sb * 128.0
        ktmpl[base + 6] = -sa
        ktmpl[base + 7] = -sb
        qtmpl[base + 0] = sa * 256.0
        qtmpl[base + 1] = sb * 256.0
        qtmpl[base + 2] = sa
        qtmpl[base + 3] = sb
        qtmpl[base + 4] = qth
        qtmpl[base + 5] = qth
        qtmpl[base + 6] = qtl
        qtmpl[base + 7] = qtl
    maskx = np.zeros((128, 1152), dtype=np.float32)
    maskx[:, 1024:] = mneg
    return id16, id32, mneg, maskx, ktmpl.astype(BF), qtmpl.astype(BF)


def build_program(debug=False):
    stack = ExitStack()
    nc = bass.Bass("TRN2", target_bir_lowering=False)

    dx = nc.dram_tensor("x", [T, D], F32, kind="ExternalInput")
    dwq = nc.dram_tensor("wqt", [D, D], F32, kind="ExternalInput")
    dwk = nc.dram_tensor("wkt", [D, D], F32, kind="ExternalInput")
    dwv = nc.dram_tensor("wvt", [D, D], F32, kind="ExternalInput")
    dwo = nc.dram_tensor("wot", [D, D], F32, kind="ExternalInput")
    dbq = nc.dram_tensor("bq_cols", [128, 8], F32, kind="ExternalInput")
    dbk = nc.dram_tensor("bk_cols", [128, 8], F32, kind="ExternalInput")
    dbvb = nc.dram_tensor("bv_bc", [128, D], F32, kind="ExternalInput")
    dbob = nc.dram_tensor("bo_bc", [128, D], F32, kind="ExternalInput")
    dgw = nc.dram_tensor("gw_cols", [128, 64], F32, kind="ExternalInput")
    dgbb = nc.dram_tensor("gb_bc", [128, 8], F32, kind="ExternalInput")
    dw1 = nc.dram_tensor("w1", [NEXP, D, FF], F32, kind="ExternalInput")
    db1 = nc.dram_tensor("b1_cols", [128, 256], F32, kind="ExternalInput")
    dw2 = nc.dram_tensor("w2", [NEXP, FF, D], F32, kind="ExternalInput")
    db2 = nc.dram_tensor("b2", [NEXP, D], F32, kind="ExternalInput")
    did16 = nc.dram_tensor("c_id16", [128, 128], BF16, kind="ExternalInput")
    did32 = nc.dram_tensor("c_id32", [128, 128], F32, kind="ExternalInput")
    dmneg = nc.dram_tensor("c_mneg", [128, 128], F32, kind="ExternalInput")
    dmaskx = nc.dram_tensor("c_maskx", [128, 1152], F32, kind="ExternalInput")
    dktm = nc.dram_tensor("c_ktmpl", [128, S], BF16, kind="ExternalInput")
    dqtm = nc.dram_tensor("c_qtmpl", [128, S], BF16, kind="ExternalInput")

    dout = nc.dram_tensor("out", [T, D], F32, kind="ExternalOutput")
    g_scr = nc.dram_tensor("g_scr", [T, D], F32, kind="ExternalOutput" if debug else "Internal")
    xn_scr = nc.dram_tensor("xn_scr", [8, 128, T], BF16, kind="ExternalOutput" if debug else "Internal")

    if debug:
        dbg_k = nc.dram_tensor("dbg_k", [128, 8 * S], BF16, kind="ExternalOutput")
        dbg_q = nc.dram_tensor("dbg_q", [128, 8 * S], BF16, kind="ExternalOutput")
        dbg_v = nc.dram_tensor("dbg_v", [128, 16 * 1024], BF16, kind="ExternalOutput")
        dbg_y = nc.dram_tensor("dbg_y", [128, 8 * 1024], BF16, kind="ExternalOutput")

    b = Builder(nc, stack)

    id16 = stack.enter_context(nc.sbuf_tensor("id16", [128, 128], BF16))
    id32 = stack.enter_context(nc.sbuf_tensor("id32", [128, 128], F32))
    mneg = stack.enter_context(nc.sbuf_tensor("mneg", [128, 128], F32))
    ktm = stack.enter_context(nc.sbuf_tensor("ktm", [128, S], BF16))
    qtm = stack.enter_context(nc.sbuf_tensor("qtm", [128, S], BF16))
    bq8 = stack.enter_context(nc.sbuf_tensor("bq8", [128, 8], F32))
    bkc = stack.enter_context(nc.sbuf_tensor("bkc", [128, 8], F32))
    comb = stack.enter_context(nc.sbuf_tensor("comb", [128, NT * 8], F32))
    epsv = stack.enter_context(nc.sbuf_tensor("epsv", [128, 1], F32))
    b.op("vector", lambda e: e.memset(epsv[:, :], 1e-5), writes=("epsv",), name="eps")

    b.op("sync", lambda e: e.dma_start(out=id16[:, :], in_=did16[:, :]), writes=("id16",), dma=True, name="l_id16")
    b.op("sync", lambda e: e.dma_start(out=id32[:, :], in_=did32[:, :]), writes=("id32",), dma=True, name="l_id32")
    b.op("sync", lambda e: e.dma_start(out=mneg[:, :], in_=dmneg[:, :]), writes=("mneg",), dma=True, name="l_mneg")
    b.op("sync", lambda e: e.dma_start(out=ktm[:, :], in_=dktm[:, :]), writes=("ktm",), dma=True, name="l_ktm")
    b.op("sync", lambda e: e.dma_start(out=qtm[:, :], in_=dqtm[:, :]), writes=("qtm",), dma=True, name="l_qtm")
    b.op("sync", lambda e: e.dma_start(out=bkc[:, :], in_=dbk[:, :]), writes=("bkc",), dma=True, name="l_bk")
    b.op("sync", lambda e: e.dma_start(out=bq8[:, :], in_=dbq[:, :]), writes=("bq8",), dma=True, name="l_bq")
    b.op("vector", lambda e: e.tensor_scalar(bq8[:, :], bq8[:, :], 0.125, None, op0=ALU.mult),
         reads=("bq8",), writes=("bq8",), name="bq8s")

    with ExitStack() as scA:
        combT = scA.enter_context(nc.sbuf_tensor("combT", [8, NT * 128], F32))
        w1b0_pre = scA.enter_context(nc.sbuf_tensor("w1b0", [128, 4 * 4096], BF16))

        with ExitStack() as sc:
            sb0 = lambda n, s, d: sc.enter_context(nc.sbuf_tensor(n, s, d))
            ps0 = lambda n, s, d: sc.enter_context(nc.psum_tensor(n, s, d))
            xnT = sb0("xnT", [128, 8 * T], BF16)
            gw = sb0("gw", [128, 64], F32)
            gbb = sb0("gbb", [128, 8], F32)
            xb = [sb0(f"xb{i}", [128, D], F32) for i in range(2)]
            xn32 = [sb0(f"xn32_{i}", [128, D], F32) for i in range(2)]
            xn16 = [sb0(f"xn16_{i}", [128, D], BF16) for i in range(2)]
            xT32 = [sb0(f"xT32_{i}", [128, D], F32) for i in range(2)]
            junk = sb0("junk", [128, D], BF16)
            nsum = sb0("nsum", [128, 1], F32)
            negmean = sb0("negmean", [128, 1], F32)
            msq = sb0("msq", [128, 1], F32)
            var = sb0("var", [128, 1], F32)
            lnv = sb0("lnv", [128, 1], F32)
            rstd = sb0("rstd", [128, 1], F32)
            ssq = sb0("ssq", [128, 1], F32)
            logit = sb0("logit", [128, 8], F32)
            m1 = sb0("m1", [128, 1], F32)
            nm1 = sb0("nm1", [128, 1], F32)
            ew = sb0("ew", [128, 8], F32)
            ind1 = sb0("ind1", [128, 8], F32)
            wm = sb0("wm", [128, 8], F32)
            m2 = sb0("m2", [128, 1], F32)
            indt = sb0("indt", [128, 8], F32)
            cu = sb0("cu", [128, 8], F32)
            dsum = sb0("dsum", [128, 1], F32)
            rec_ = sb0("rec_", [128, 1], F32)
            pp16 = [ps0(f"pp16_{i}", [128, 1024], BF16) for i in range(2)]
            pp32 = [ps0(f"pp32_{i}", [128, 1024], F32) for i in range(2)]
            psg = ps0("psg", [128, 8], F32)
            psct = ps0("psct", [8, 512], F32)

            b.op("sync", lambda e: e.dma_start(out=gw[:, :], in_=dgw[:, :]), writes=("gw",), dma=True, name="l_gw")
            b.op("sync", lambda e: e.dma_start(out=gbb[:, :], in_=dgbb[:, :]), writes=("gbb",), dma=True, name="l_gbb")

            for i in range(NT):
                par = i % 2
                t0 = i * 128
                xbp, xn32p, xn16p, xT32p = xb[par], xn32[par], xn16[par], xT32[par]
                nx, n32, n16, nT32 = f"xb{par}", f"xn32_{par}", f"xn16_{par}", f"xT32_{par}"
                pp16p, pp32p = pp16[par], pp32[par]
                n_pp16, n_pp32 = f"pp16_{par}", f"pp32_{par}"

                b.op("sync", lambda e, d=xbp, s=t0: e.dma_start(out=d[:, :], in_=dx[s:s + 128, :]),
                     writes=(nx,), dma=True, name=f"ldx{i}")
                b.op("vector", lambda e, x=xbp: e.reduce_sum(nsum[:, :], x[:, :], axis=AX.X, negate=True),
                     reads=(nx,), writes=("nsum",), name=f"su{i}")
                b.op("scalar", lambda e, x=xbp: e.activation(junk[:, :], x[:, :], AF.Square, accum_out=ssq[:, :]),
                     reads=(nx,), writes=("junk", "ssq"), name=f"sq{i}")
                b.op("vector", lambda e: e.tensor_scalar(negmean[:, :], nsum[:, :], 1.0 / D, None, op0=ALU.mult),
                     reads=("nsum",), writes=("negmean",), name=f"nm{i}")
                b.op("vector", lambda e: e.tensor_mul(msq[:, :], negmean[:, :], negmean[:, :]),
                     reads=("negmean",), writes=("msq",), name=f"ms{i}")
                b.op("vector", lambda e: e.tensor_scalar(var[:, :], ssq[:, :], 1.0 / D, msq[:, :],
                                                         op0=ALU.mult, op1=ALU.subtract),
                     reads=("ssq", "msq"), writes=("var",), name=f"va{i}")
                b.op("scalar", lambda e: e.activation(lnv[:, :], var[:, :], AF.Ln, bias=epsv[:, :]),
                     reads=("var", "epsv"), writes=("lnv",), name=f"ln{i}")
                b.op("scalar", lambda e: e.activation(rstd[:, :], lnv[:, :], AF.Exp, scale=-0.5),
                     reads=("lnv",), writes=("rstd",), name=f"rs{i}")
                b.op("vector", lambda e, x=xbp, o=xn32p: e.tensor_scalar(
                        o[:, :], x[:, :], negmean[:, :], rstd[:, :], op0=ALU.add, op1=ALU.mult),
                     reads=(nx, "negmean", "rstd"), writes=(n32,), name=f"xn{i}")
                b.op("vector", lambda e, s=xn32p, o=xn16p: e.tensor_copy(o[:, :], s[:, :]),
                     reads=(n32,), writes=(n16,), name=f"cst{i}")

                def tp16(e, src=xn16p, dst=pp16p):
                    inst = None
                    for c in range(8):
                        inst = e.matmul(dst[:, c * 128:(c + 1) * 128], src[:, c * 128:(c + 1) * 128],
                                        id16[:, :], start=(c == 0), stop=(c == 7), is_transpose=True)
                    return inst
                b.op("tensor", tp16, reads=(n16, "id16"), writes=(n_pp16,), name=f"t16_{i}")

                def tp32(e, src=xn32p, dst=pp32p):
                    inst = None
                    for c in range(8):
                        inst = e.matmul(dst[:, c * 128:(c + 1) * 128], src[:, c * 128:(c + 1) * 128],
                                        id32[:, :], start=(c in (0, 4)), stop=(c in (3, 7)), is_transpose=True)
                    return inst
                b.op("tensor", tp32, reads=(n32, "id32"), writes=(n_pp32,), name=f"t32_{i}")

                def ev16(e, src=pp16p, t0=t0):
                    dst = bass.AP(xnT, t0, [[8 * T, 128], [T, 8], [1, 128]])
                    return e.tensor_copy(dst, src[:, :])
                b.op("vector", ev16, reads=(n_pp16,), writes=("xnT",), name=f"e16_{i}")
                b.op("vector", lambda e, src=pp32p, dst=xT32p: e.tensor_copy(dst[:, :], src[:, :]),
                     reads=(n_pp32,), writes=(nT32,), name=f"e32_{i}")

                def gmm(e, src=xT32p):
                    inst = None
                    for c in range(8):
                        inst = e.matmul(psg[:, :], src[:, c * 128:(c + 1) * 128],
                                        gw[:, c * 8:(c + 1) * 8], start=(c == 0), stop=(c == 7))
                    return inst
                b.op("tensor", gmm, reads=(nT32, "gw"), writes=("psg",), name=f"gm{i}")
                b.op("vector", lambda e: e.tensor_add(logit[:, :], psg[:, :], gbb[:, :]),
                     reads=("psg", "gbb"), writes=("logit",), name=f"lg{i}")
                b.op("vector", lambda e: e.reduce_max(m1[:, :], logit[:, :], axis=AX.X),
                     reads=("logit",), writes=("m1",), name=f"m1_{i}")
                b.op("vector", lambda e: e.tensor_scalar(nm1[:, :], m1[:, :], -1.0, None, op0=ALU.mult),
                     reads=("m1",), writes=("nm1",), name=f"nm1_{i}")
                b.op("scalar", lambda e: e.activation(ew[:, :], logit[:, :], AF.Exp, bias=nm1[:, :]),
                     reads=("logit", "nm1"), writes=("ew",), name=f"ew{i}")
                b.op("vector", lambda e: e.tensor_scalar(ind1[:, :], logit[:, :], m1[:, :], None, op0=ALU.is_ge),
                     reads=("logit", "m1"), writes=("ind1",), name=f"i1_{i}")
                b.op("vector", lambda e: e.tensor_scalar(ind1[:, :], ind1[:, :], -1.0, 1.0,
                                                         op0=ALU.mult, op1=ALU.add),
                     reads=("ind1",), writes=("ind1",), name=f"ni{i}")
                b.op("vector", lambda e: e.tensor_mul(wm[:, :], ew[:, :], ind1[:, :]),
                     reads=("ew", "ind1"), writes=("wm",), name=f"wm{i}")
                b.op("vector", lambda e: e.reduce_max(m2[:, :], wm[:, :], axis=AX.X),
                     reads=("wm",), writes=("m2",), name=f"m2_{i}")
                b.op("vector", lambda e: e.tensor_scalar(indt[:, :], ew[:, :], m2[:, :], None, op0=ALU.is_ge),
                     reads=("ew", "m2"), writes=("indt",), name=f"it{i}")
                b.op("vector", lambda e: e.tensor_mul(cu[:, :], ew[:, :], indt[:, :]),
                     reads=("ew", "indt"), writes=("cu",), name=f"cu{i}")
                b.op("vector", lambda e: e.reduce_sum(dsum[:, :], cu[:, :], axis=AX.X),
                     reads=("cu",), writes=("dsum",), name=f"ds{i}")
                b.op("vector", lambda e: e.tensor_scalar(dsum[:, :], dsum[:, :], 1e-6, None, op0=ALU.max),
                     reads=("dsum",), writes=("dsum",), name=f"dm{i}")
                b.op("vector", lambda e: e.reciprocal(rec_[:, :], dsum[:, :]),
                     reads=("dsum",), writes=("rec_",), name=f"rc{i}")
                b.op("vector", lambda e, i=i: e.tensor_scalar(comb[:, i * 8:(i + 1) * 8], cu[:, :],
                                                              rec_[:, :], None, op0=ALU.mult),
                     reads=("cu", "rec_"), writes=("comb",), name=f"cb{i}")

                def ctp(e, i=i):
                    return e.matmul(psct[0:8, (i % 4) * 128:(i % 4) * 128 + 128],
                                    comb[:, i * 8:(i + 1) * 8], id32[:, :],
                                    start=(i % 4 == 0), stop=(i % 4 == 3), is_transpose=True)
                b.op("tensor", ctp, reads=("comb", "id32"), writes=("psct",), name=f"ct{i}")
                if i % 4 == 3:
                    b.op("vector", lambda e, i=i: e.tensor_copy(combT[0:8, (i - 3) * 128:(i + 1) * 128],
                                                                psct[0:8, :]),
                         reads=("psct",), writes=("combT",), name=f"ce{i}")

            for c in range(8):
                b.op("sync", lambda e, c=c: e.dma_start(out=xn_scr[c, :, :], in_=xnT[:, c * T:(c + 1) * T]),
                     reads=("xnT",), writes=("xn_scr",), dma=True, name=f"sp{c}")

            for fc in range(4):
                w1src = dw1[0, :, fc * 512:(fc + 1) * 512].rearrange("(c p) f -> p c f", p=128)
                w1dst = bass.AP(w1b0_pre, fc * 4096, [[4 * 4096, 128], [512, 8], [1, 512]])
                b.op("gpsimd", lambda e, d=w1dst, s=w1src: e.dma_start(out=d, in_=s),
                     writes=("w1b0",), dma=True, name=f"w1l0_{fc}")

        b.barrier()

        with ExitStack() as sc:
            sb1 = lambda n, s, d: sc.enter_context(nc.sbuf_tensor(n, s, d))
            ps1 = lambda n, s, d: sc.enter_context(nc.psum_tensor(n, s, d))
            w1b = [w1b0_pre, sb1("w1b1", [128, 4 * 4096], BF16)]
            w2h = sb1("w2h", [128, 16 * 1024], BF16)
            hdnT = [sb1(f"hdnT{i}", [128, 16 * 512], BF16) for i in range(2)]
            xnb1 = [sb1(f"xnb1_{i}", [128, 8 * 512], BF16) for i in range(2)]
            b1c = sb1("b1c", [128, 256], F32)
            b2sb = sb1("b2sb", [8, D], F32)
            gout = [sb1(f"gout{i}", [128, D], F32) for i in range(2)]
            psa = [ps1(f"psa{i}", [128, 512], F32) for i in range(2)]
            psb = [ps1(f"psb{i}", [128, 1024], F32) for i in range(2)]

            b.op("sync", lambda e: e.dma_start(out=b1c[:, :], in_=db1[:, :]), writes=("b1c",), dma=True, name="l_b1")
            b.op("sync", lambda e: e.dma_start(out=b2sb[:, :], in_=db2[:, :]), writes=("b2sb",), dma=True, name="l_b2")

            for gt in range(NT):
                go, ngo = gout[gt % 2], f"gout{gt % 2}"
                pb, npb = psb[gt % 2], f"psb{gt % 2}"

                def b2mm(e, gt=gt, pb=pb):
                    e.matmul(pb[:, 0:512], combT[0:8, gt * 128:(gt + 1) * 128],
                             b2sb[0:8, 0:512], start=True, stop=True)
                    return e.matmul(pb[:, 512:1024], combT[0:8, gt * 128:(gt + 1) * 128],
                                    b2sb[0:8, 512:1024], start=True, stop=True)
                b.op("tensor", b2mm, reads=("combT", "b2sb"), writes=(npb,), name=f"b2m{gt}")
                b.op("vector", lambda e, pb=pb, go=go: e.tensor_copy(go[:, :], pb[:, :]),
                     reads=(npb,), writes=(ngo,), name=f"b2c{gt}")
                b.op("gpsimd", lambda e, go=go, gt=gt: e.dma_start(
                        out=g_scr[gt * 128:(gt + 1) * 128, :], in_=go[:, :]),
                     reads=(ngo,), writes=(f"g{gt}",), dma=True, name=f"b2s{gt}")

            def load_w1(rnd):
                ex, fh = rnd // 2, rnd % 2
                wp, nwp = w1b[rnd % 2], f"w1b{rnd % 2}"
                for fc in range(4):
                    src = dw1[ex, :, fh * 2048 + fc * 512: fh * 2048 + (fc + 1) * 512] \
                        .rearrange("(c p) f -> p c f", p=128)
                    dst = bass.AP(wp, fc * 4096, [[4 * 4096, 128], [512, 8], [1, 512]])
                    b.op("gpsimd", lambda e, d=dst, s=src: e.dma_start(out=d, in_=s),
                         writes=(nwp,), dma=True, name=f"w1l{rnd}_{fc}")

            for rnd in range(16):
                ex, fh = rnd // 2, rnd % 2
                w1p, nw1 = w1b[rnd % 2], f"w1b{rnd % 2}"
                for fc in range(4):
                    src = dw2[ex, fh * 2048 + fc * 512: fh * 2048 + (fc + 1) * 512, :] \
                        .rearrange("(c p) h -> p c h", p=128)
                    dst = bass.AP(w2h, fc * 4 * 1024, [[16 * 1024, 128], [1024, 4], [1, 1024]])
                    b.op("gpsimd", lambda e, d=dst, s=src: e.dma_start(out=d, in_=s),
                         writes=("w2h",), dma=True, name=f"w2l{rnd}_{fc}")
                if rnd + 1 < 16:
                    load_w1(rnd + 1)

                for tg in range(8):
                    tpos = tg * 512
                    xp, nxp = xnb1[tg % 2], f"xnb1_{tg % 2}"
                    src = xn_scr[:, :, tpos:tpos + 512].rearrange("c p t -> p c t")
                    b.op("sync", lambda e, d=xp, s=src: e.dma_start(
                            out=d.ap().rearrange("p (c t) -> p c t", c=8), in_=s),
                         reads=("xn_scr",), writes=(nxp,), dma=True, name=f"xl{rnd}_{tg}")

                    for fb in range(16):
                        fc, fs = fb // 4, fb % 4
                        pa, npa = psa[fb % 2], f"psa{fb % 2}"

                        def amm(e, w1p=w1p, fc=fc, fs=fs, pa=pa, xp=xp):
                            inst = None
                            for dc in range(8):
                                inst = e.matmul(
                                    pa[:, :],
                                    w1p[:, fc * 4096 + dc * 512 + fs * 128: fc * 4096 + dc * 512 + fs * 128 + 128],
                                    xp[:, dc * 512:(dc + 1) * 512],
                                    start=(dc == 0), stop=(dc == 7))
                            return inst
                        b.op("tensor", amm, reads=(nw1, nxp), writes=(npa,), name=f"am{rnd}_{tg}_{fb}")

                        bcol = ex * 32 + fh * 16 + fb
                        hb, nhb = hdnT[tg % 2], f"hdnT{tg % 2}"
                        b.op("scalar", lambda e, pa=pa, fb=fb, bcol=bcol, hb=hb: e.activation(
                                hb[:, fb * 512:(fb + 1) * 512], pa[:, :], AF.Gelu,
                                bias=b1c[:, bcol:bcol + 1]),
                             reads=(npa, "b1c"), writes=(nhb,), name=f"ge{rnd}_{tg}_{fb}")

                    for tl in range(4):
                        gt = tg * 4 + tl
                        pb, npb = psb[tl % 2], f"psb{tl % 2}"
                        go, ngo = gout[tl % 2], f"gout{tl % 2}"

                        hb, nhb = hdnT[tg % 2], f"hdnT{tg % 2}"
                        def bmm(e, tl=tl, pb=pb, hb=hb):
                            inst = None
                            for fb in range(16):
                                lhs = hb[:, fb * 512 + tl * 128: fb * 512 + tl * 128 + 128]
                                inst = e.matmul(pb[:, 0:512], lhs, w2h[:, fb * 1024: fb * 1024 + 512],
                                                start=(fb == 0), stop=(fb == 15))
                                inst = e.matmul(pb[:, 512:1024], lhs, w2h[:, fb * 1024 + 512: fb * 1024 + 1024],
                                                start=(fb == 0), stop=(fb == 15))
                            return inst
                        b.op("tensor", bmm, reads=(nhb, "w2h"), writes=(npb,), name=f"bm{rnd}_{gt}")
                        b.op("vector", lambda e, pb=pb, go=go, gt=gt, ex=ex: e.tensor_scalar(
                                go[:, :], pb[:, :], comb[:, gt * 8 + ex: gt * 8 + ex + 1], None, op0=ALU.mult),
                             reads=(npb, "comb"), writes=(ngo,), name=f"be{rnd}_{gt}")
                        b.op("gpsimd", lambda e, go=go, gt=gt: e.dma_start(
                                out=g_scr[gt * 128:(gt + 1) * 128, :], in_=go[:, :], accum_op=ALU.add),
                             reads=(ngo,), writes=(f"g{gt}",), dma=True, name=f"ga{rnd}_{gt}")

        b.barrier()

    with ExitStack() as sc:
        sb2 = lambda n, s, d: sc.enter_context(nc.sbuf_tensor(n, s, d))
        kT = sb2("kT", [128, 8 * S], BF16)
        qT2 = sb2("qT2", [128, 8 * S], BF16)
        v_tm = sb2("v_tm", [128, 16 * 1024], BF16)
        yT = sb2("yT", [128, 8 * 1024], BF16)
        bob = sb2("bob", [128, D], F32)
        lcur = [sb2(f"lcur{i}", [128, 1], F32) for i in range(2)]
        rl2 = [sb2(f"rl2_{i}", [128, 1], F32) for i in range(2)]
        mstat = [sb2(f"mstat{i}", [128, 3], F32) for i in range(2)]
        lc2 = [sb2(f"lc2_{i}", [128, 2], F32) for i in range(2)]
        negm = [sb2(f"negm{i}", [128, 1], F32) for i in range(2)]
        stmp = sb2("stmp", [128, D], F32)
        sgb = sb2("sgb", [128, D], F32)

        b.op("sync", lambda e: e.dma_start(out=bob[:, :], in_=dbob[:, :]), writes=("bob",), dma=True, name="l_bob")

        for sq in range(SEQS):
            soff = sq * S
            with ExitStack() as sp:
                sbp = lambda n, s, d: sp.enter_context(nc.sbuf_tensor(n, s, d))
                psp = lambda n, s, d: sp.enter_context(nc.psum_tensor(n, s, d))
                wkb = sbp(f"wkb{sq}", [128, 8 * 1024], BF16)
                wqb = sbp(f"wqb{sq}", [128, 8 * 1024], BF16)
                wvb = sbp(f"wvb{sq}", [128, 8 * 1024], BF16)
                xnb = [sbp(f"xnb2_{sq}_{i}", [128, 8 * 512], BF16) for i in range(2)]
                bvb = sbp(f"bvb{sq}", [128, D], F32)
                pko = [psp(f"pko{sq}_{i}", [128, 512], F32) for i in range(2)]
                b.op("sync", lambda e, bvb=bvb: e.dma_start(out=bvb[:, :], in_=dbvb[:, :]),
                     writes=(f"bvb{sq}",), dma=True, name=f"l_bvb{sq}")
                psv = psp(f"psv{sq}", [128, 1024], F32)

                for wdst, wsrc, wname in ((wkb, dwk, f"wkb{sq}"), (wqb, dwq, f"wqb{sq}"), (wvb, dwv, f"wvb{sq}")):
                    src = wsrc[:, :].rearrange("(c p) i -> p c i", p=128)
                    b.op("gpsimd", lambda e, dd=wdst, s=src: e.dma_start(
                            out=dd.ap().rearrange("p (c i) -> p c i", c=8), in_=s),
                         writes=(wname,), dma=True, name=f"lw_{wname}")

                for tc in range(4):
                    xp, nxp = xnb[tc % 2], f"xnb2_{sq}_{tc % 2}"
                    src = xn_scr[:, :, soff + tc * 512: soff + (tc + 1) * 512].rearrange("c p t -> p c t")
                    b.op("sync", lambda e, d=xp, s=src: e.dma_start(
                            out=d.ap().rearrange("p (c t) -> p c t", c=8), in_=s),
                         reads=("xn_scr",), writes=(nxp,), dma=True, name=f"x2{sq}_{tc}")

                    for proj, wb, nwb, dstT, ndT, bias, scale in (
                            ("k", wkb, f"wkb{sq}", kT, "kT", bkc, 1.0),
                            ("q", wqb, f"wqb{sq}", qT2, "qT2", bq8, 0.125)):
                        for ic in range(8):
                            pk, npk = pko[ic % 2], f"pko{sq}_{ic % 2}"

                            def pmm(e, xp=xp, ic=ic, pk=pk, wb=wb):
                                inst = None
                                for dc in range(8):
                                    inst = e.matmul(pk[:, :],
                                                    wb[:, dc * 1024 + ic * 128: dc * 1024 + ic * 128 + 128],
                                                    xp[:, dc * 512:(dc + 1) * 512],
                                                    start=(dc == 0), stop=(dc == 7))
                                return inst
                            b.op("tensor", pmm, reads=(nwb, nxp), writes=(npk,), name=f"{proj}m{sq}_{tc}_{ic}")
                            b.op("scalar", lambda e, pk=pk, ic=ic, tc=tc, dstT=dstT, bias=bias, scale=scale:
                                    e.activation(dstT[:, ic * S + tc * 512: ic * S + (tc + 1) * 512], pk[:, :],
                                                 AF.Identity, bias=bias[:, ic:ic + 1], scale=scale),
                                 reads=(npk, "bkc", "bq8"), writes=(ndT,), name=f"{proj}e{sq}_{tc}_{ic}")

                    for tl in range(4):
                        tj = tc * 4 + tl

                        def vmm(e, xp=xp, tl=tl, wb=wvb):
                            inst = None
                            for dc in range(8):
                                lhs = xp[:, dc * 512 + tl * 128: dc * 512 + tl * 128 + 128]
                                inst = e.matmul(psv[:, 0:512], lhs, wb[:, dc * 1024: dc * 1024 + 512],
                                                start=(dc == 0), stop=(dc == 7))
                                inst = e.matmul(psv[:, 512:1024], lhs, wb[:, dc * 1024 + 512: dc * 1024 + 1024],
                                                start=(dc == 0), stop=(dc == 7))
                            return inst
                        b.op("tensor", vmm, reads=(f"wvb{sq}", nxp), writes=(f"psv{sq}",), name=f"vm{sq}_{tc}_{tl}")
                        b.op("vector", lambda e, tj=tj: e.scalar_tensor_tensor(
                                v_tm[:, tj * 1024:(tj + 1) * 1024], psv[:, :], 1.0, bvb[:, :],
                                op0=ALU.mult, op1=ALU.add),
                             reads=(f"psv{sq}", f"bvb{sq}"), writes=("v_tm",), name=f"ve{sq}_{tc}_{tl}")

            if debug and sq == 0:
                b.op("sync", lambda e: e.dma_start(out=dbg_k[:, :], in_=kT[:, :]),
                     reads=("kT",), dma=True, name="dbgk")
                b.op("sync", lambda e: e.dma_start(out=dbg_q[:, :], in_=qT2[:, :]),
                     reads=("qT2",), dma=True, name="dbgq")
                b.op("sync", lambda e: e.dma_start(out=dbg_v[:, :], in_=v_tm[:, :]),
                     reads=("v_tm",), dma=True, name="dbgv")
            b.barrier()

            with ExitStack() as sp:
                sbp = lambda n, s, d: sp.enter_context(nc.sbuf_tensor(n, s, d))
                psp = lambda n, s, d: sp.enter_context(nc.psum_tensor(n, s, d))
                wob = sbp(f"wob{sq}", [128, 8 * 1024], BF16)
                kqa = [sbp(f"kqa{sq}_{i}", [72, S], BF16) for i in range(2)]
                qaug = [sbp(f"qaug{sq}_{i}", [72, 1024], BF16) for i in range(2)]
                Ebuf = [sbp(f"Eb{sq}_{i}", [128, S], BF16) for i in range(3)]
                ETb = [sbp(f"ET{sq}_{i}", [128, 16 * 128], BF16) for i in range(2)]
                sbs = [sbp(f"sbs{sq}_{i}", [128, S], F32) for i in range(2)]
                gti = [sbp(f"gti{sq}_0", [128, D], F32)] * 2
                xti = [sbp(f"xti{sq}_0", [128, D], F32)] * 2
                pssp = [psp(f"pss{sq}_{i}", [128, 1024], F32) for i in range(2)]
                pset = psp(f"pset{sq}", [128, 1024], BF16)
                psy = psp(f"psy{sq}", [64, 128], F32)
                pso = psp(f"pso{sq}", [128, 1024], F32)

                src = dwo[:, :].rearrange("(c p) i -> p c i", p=128)
                b.op("gpsimd", lambda e, s=src: e.dma_start(
                        out=wob.ap().rearrange("p (c i) -> p c i", c=8), in_=s),
                     writes=(f"wob{sq}",), dma=True, name=f"lwo{sq}")

                gc = [0]
                for half in range(2):
                    stq = []
                    def flush_one():
                        d2, t1, t2 = stq.pop(0)
                        d2(); t1()
                        stq.append((None, None, t2))
                    for h in range(H):
                        hp = h % 2
                        kq, nkq = kqa[hp], f"kqa{sq}_{hp}"
                        qa, nqa = qaug[hp], f"qaug{sq}_{hp}"
                        b.op("sync", lambda e, kq=kq, h=h: e.dma_start(
                                out=kq[0:64, :],
                                in_=kT[(h % 2) * 64:(h % 2) * 64 + 64, (h // 2) * S:(h // 2 + 1) * S]),
                             reads=("kT",), writes=(nkq,), dma=True, name=f"bk{sq}_{half}_{h}")
                        b.op("sync", lambda e, kq=kq, h=h: e.dma_start(
                                out=kq[64:72, :], in_=ktm[8 * h:8 * h + 8, :]),
                             reads=("ktm",), writes=(nkq,), dma=True, name=f"bkt{sq}_{half}_{h}")
                        b.op("sync", lambda e, qa=qa, h=h, half=half: e.dma_start(
                                out=qa[0:64, :],
                                in_=qT2[(h % 2) * 64:(h % 2) * 64 + 64,
                                        (h // 2) * S + half * 1024:(h // 2) * S + half * 1024 + 1024]),
                             reads=("qT2",), writes=(nqa,), dma=True, name=f"bq{sq}_{half}_{h}")
                        b.op("sync", lambda e, qa=qa, h=h, half=half: e.dma_start(
                                out=qa[64:72, :], in_=qtm[8 * h:8 * h + 8, half * 1024: half * 1024 + 1024]),
                             reads=("qtm",), writes=(nqa,), dma=True, name=f"bqt{sq}_{half}_{h}")

                        for A in range(8):
                            Ag = half * 8 + A
                            W = (Ag + 1) * 128
                            ep_i = gc[0] % 3
                            et_i = gc[0] % 2
                            spar = A % 2
                            gc[0] += 1
                            Ep, nEp = Ebuf[ep_i], f"Eb{sq}_{ep_i}"
                            ETp, nETp = ETb[et_i], f"ET{sq}_{et_i}"
                            sbp_ = sbs[spar]
                            nsbs = f"sbs{sq}_{spar}"
                            nneg = f"negm{sq}_{spar}"
                            nms = f"mstat{sq}_{spar}"
                            tag = f"{sq}_{half}_{h}_{A}"
                            if Ag <= 7:
                                pieces = [(A % 2, 0, W)]
                            else:
                                pieces = [(A % 2, 0, 1024), ((A + 1) % 2, 1024, W)]
                            n512 = Ag // 4
                            rem = Ag % 4
                            chunks = [(c * 512, (c + 1) * 512, True, True) for c in range(n512)]
                            if rem:
                                chunks.append((n512 * 512, n512 * 512 + rem * 128, True, False))
                                chunks.append((W - 128, W, False, True))
                            else:
                                chunks.append((W - 128, W, True, True))

                            mains = 0
                            for pi, (pb, lo, hi) in enumerate(pieces):
                                ps_ = pssp[pb]
                                nps = f"pss{sq}_{pb}"

                                def smm(e, qa=qa, kq=kq, A=A, ps_=ps_, lo=lo, hi=hi, chunks=tuple(chunks)):
                                    inst = None
                                    lhs = qa[0:72, A * 128:(A + 1) * 128]
                                    for gl, gh, st, sp in chunks:
                                        if gl < lo or gl >= hi:
                                            continue
                                        inst = e.matmul(ps_[:, gl - lo:gh - lo], lhs,
                                                        kq[0:72, gl:gh], start=st, stop=sp)
                                    return inst
                                b.op("tensor", smm, reads=(nqa, nkq), writes=(nps,), name=f"s{tag}_{pi}")
                                hid = min(hi, W - 128)
                                if hid > lo:
                                    mc = mains
                                    mains += 1
                                    b.op("vector", lambda e, ps_=ps_, lo=lo, hid=hid, mc=mc, sbp_=sbp_,
                                                spar=spar: e.tensor_scalar(
                                            sbp_[:, lo:hid], ps_[:, 0:hid - lo], 1.0, None,
                                            op0=ALU.mult, op1=ALU.max,
                                            accum_out=mstat[spar][:, mc:mc + 1]),
                                         reads=(nps,), writes=(nsbs, nms), name=f"e{tag}_{pi}")
                                if hi == W:
                                    dc = mains
                                    b.op("vector", lambda e, ps_=ps_, lo=lo, W=W, sbp_=sbp_: e.scalar_tensor_tensor(
                                            sbp_[:, W - 128:W], ps_[:, W - 128 - lo:W - lo], 1.0, mneg[:, :],
                                            op0=ALU.mult, op1=ALU.add),
                                         reads=(nps, "mneg"), writes=(nsbs,), name=f"d{tag}")
                                    b.op("vector", lambda e, W=W, sbp_=sbp_, spar=spar, dc=dc: e.tensor_scalar(
                                            sbp_[:, W - 128:W], sbp_[:, W - 128:W], 1.0, None,
                                            op0=ALU.mult, op1=ALU.max,
                                            accum_out=mstat[spar][:, dc:dc + 1]),
                                         reads=(nsbs,), writes=(nsbs, nms), name=f"dm{tag}")
                            ncols = mains + 1
                            b.op("vector", lambda e, spar=spar, ncols=ncols: e.reduce_max(
                                    negm[spar][:, :], mstat[spar][:, 0:ncols], axis=AX.X, negate=True),
                                 reads=(nms,), writes=(nneg,), name=f"rm{tag}")
                            b.op("scalar", lambda e, Ep=Ep, W=W, sbp_=sbp_, spar=spar: e.activation(
                                    Ep[:, 0:W], sbp_[:, 0:W], AF.Exp, bias=negm[spar][:, :],
                                    accum_out=lcur[spar][:, :]),
                                 reads=(nsbs, nneg), writes=(nEp, f"lcur{sq}_{spar}"), name=f"x{tag}")

                            def mk_d2(Ep=Ep, nEp=nEp, W=W, spar=spar, tag=tag):
                                def d2():
                                    b.op("vector", lambda e: e.reciprocal(rl2[spar][:, :], lcur[spar][:, :]),
                                         reads=(f"lcur{sq}_{spar}",), writes=(f"rl2_{sq}_{spar}",),
                                         name=f"rl{tag}")
                                    b.op("vector", lambda e: e.tensor_scalar(
                                            Ep[:, 0:W], Ep[:, 0:W], rl2[spar][:, :], None, op0=ALU.mult),
                                         reads=(nEp, f"rl2_{sq}_{spar}"), writes=(nEp,), name=f"en{tag}")
                                return d2

                            def mk_t1(Ep=Ep, nEp=nEp, ETp=ETp, nETp=nETp, Ag=Ag, tag=tag):
                                def t1():
                                    ngroups = (Ag + 8) // 8
                                    for g in range(ngroups):
                                        j0, j1 = g * 8, min(g * 8 + 8, Ag + 1)

                                        def tpe(e, Ep=Ep, j0=j0, j1=j1):
                                            inst = None
                                            for j in range(j0, j1):
                                                inst = e.matmul(pset[:, (j - j0) * 128:(j - j0 + 1) * 128],
                                                                Ep[:, j * 128:(j + 1) * 128], id16[:, :],
                                                                start=(j == j0), stop=(j == j1 - 1),
                                                                is_transpose=True)
                                            return inst
                                        b.op("tensor", tpe, reads=(nEp, "id16"), writes=("pset",),
                                             name=f"t{tag}_{g}")
                                        b.op("scalar", lambda e, ETp=ETp, j0=j0, j1=j1: e.copy(
                                                ETp[:, j0 * 128:j1 * 128], pset[:, 0:(j1 - j0) * 128]),
                                             reads=("pset",), writes=(nETp,), name=f"u{tag}_{g}")
                                return t1

                            def mk_t2(ETp=ETp, nETp=nETp, h=h, A=A, Ag=Ag, tag=tag):
                                def t2():
                                    def avm(e, ETp=ETp, h=h, Ag=Ag):
                                        inst = None
                                        for jj in range(Ag + 1):
                                            inst = e.matmul(psy[0:64, :],
                                                            v_tm[:, jj * 1024 + h * 64: jj * 1024 + h * 64 + 64],
                                                            ETp[:, jj * 128:(jj + 1) * 128],
                                                            start=(jj == 0), stop=(jj == Ag))
                                        return inst
                                    b.op("tensor", avm, reads=(nETp, "v_tm"), writes=("psy",), name=f"a{tag}")
                                    b.op("vector", lambda e, h=h, A=A: e.tensor_copy(
                                            yT[(h % 2) * 64:(h % 2) * 64 + 64,
                                               (h // 2) * 1024 + A * 128:(h // 2) * 1024 + A * 128 + 128],
                                            psy[0:64, :]),
                                         reads=("psy",), writes=("yT",), name=f"y{tag}")
                                return t2

                            stq.append((mk_d2(), mk_t1(), mk_t2()))
                            if len(stq) > 1:
                                if stq[0][0] is None:
                                    _, _, t2 = stq.pop(0)
                                    t2()
                                if len(stq) > 1:
                                    d2, t1, t2 = stq.pop(0)
                                    d2(); t1()
                                    stq.insert(0, (None, None, t2))
                    while stq:
                        d2, t1, t2 = stq.pop(0)
                        if d2 is not None:
                            d2(); t1()
                        t2()

                    if debug and sq == 0 and half == 0:
                        b.op("sync", lambda e: e.dma_start(out=dbg_y[:, :], in_=yT[:, :]),
                             reads=("yT",), dma=True, name="dbgy")
                    for tl in range(8):
                        t0 = soff + half * 1024 + tl * 128
                        gp, ngp = gti[0], f"gti{sq}_0"
                        xp2, nxp2 = xti[0], f"xti{sq}_0"
                        ob, nob = xp2, nxp2
                        gt_glob = t0 // 128
                        b.op("sync", lambda e, gp=gp, t0=t0: e.dma_start(out=gp[:, :], in_=g_scr[t0:t0 + 128, :]),
                             reads=(f"g{gt_glob}",), writes=(ngp,), dma=True, name=f"lg{sq}_{half}_{tl}")
                        b.op("sync", lambda e, xp2=xp2, t0=t0: e.dma_start(out=xp2[:, :], in_=dx[t0:t0 + 128, :]),
                             writes=(nxp2,), dma=True, name=f"lx{sq}_{half}_{tl}")

                        def omm(e, tl=tl):
                            inst = None
                            for ic in range(8):
                                lhs = yT[:, ic * 1024 + tl * 128: ic * 1024 + tl * 128 + 128]
                                inst = e.matmul(pso[:, 0:512], lhs, wob[:, ic * 1024: ic * 1024 + 512],
                                                start=(ic == 0), stop=(ic == 7))
                                inst = e.matmul(pso[:, 512:1024], lhs, wob[:, ic * 1024 + 512: ic * 1024 + 1024],
                                                start=(ic == 0), stop=(ic == 7))
                            return inst
                        b.op("tensor", omm, reads=("yT", f"wob{sq}"), writes=(f"pso{sq}",), name=f"o{sq}_{half}_{tl}")
                        b.op("vector", lambda e: e.scalar_tensor_tensor(
                                stmp[:, :], pso[:, :], 1.0, bob[:, :], op0=ALU.mult, op1=ALU.add),
                             reads=(f"pso{sq}", "bob"), writes=("stmp",), name=f"se{sq}_{half}_{tl}")
                        b.op("scalar", lambda e, gp=gp: e.activation(gp[:, :], gp[:, :], AF.Silu),
                             reads=(ngp,), writes=(ngp,), name=f"si{sq}_{half}_{tl}")
                        b.op("vector", lambda e, gp=gp: e.tensor_mul(sgb[:, :], stmp[:, :], gp[:, :]),
                             reads=("stmp", ngp), writes=("sgb",), name=f"sg{sq}_{half}_{tl}")
                        b.op("vector", lambda e, xp2=xp2: e.tensor_add(xp2[:, :], sgb[:, :], xp2[:, :]),
                             reads=("sgb", nxp2), writes=(nxp2,), name=f"oa{sq}_{half}_{tl}")
                        b.op("sync", lambda e, ob=ob, t0=t0: e.dma_start(out=dout[t0:t0 + 128, :], in_=ob[:, :]),
                             reads=(nob,), writes=("out",), dma=True, name=f"so{sq}_{half}_{tl}")

            if sq + 1 < SEQS:
                b.barrier()

        final_out = b.writer["out"]["sync"]
        b.op("sync", lambda e: e.nop(), after=(final_out,), name="final")

    b.finalize_and_emit()
    stack.close()
    return nc


_cached = {}


def _get_program():
    if "nc" not in _cached:
        _cached["nc"] = build_program()
    return _cached["nc"]


def kernel(L0_attn_x, L0_attn_Wq, L0_attn_bq, L0_attn_Wk, L0_attn_bk,
           L0_attn_Wv, L0_attn_bv, L0_attn_Wo, L0_attn_bo,
           L0_ffn_gate_w, L0_ffn_gate_b, L0_ffn_w1, L0_ffn_b1,
           L0_ffn_w2, L0_ffn_b2):
    f32 = lambda a: np.ascontiguousarray(np.asarray(a, dtype=np.float32))
    x = f32(L0_attn_x)
    Wq, Wk, Wv, Wo = map(f32, (L0_attn_Wq, L0_attn_Wk, L0_attn_Wv, L0_attn_Wo))
    bq, bk, bv, bo = map(f32, (L0_attn_bq, L0_attn_bk, L0_attn_bv, L0_attn_bo))
    gw, gb = f32(L0_ffn_gate_w), f32(L0_ffn_gate_b)
    w1, b1, w2, b2 = map(f32, (L0_ffn_w1, L0_ffn_b1, L0_ffn_w2, L0_ffn_b2))

    id16, id32, mneg, maskx, ktmpl, qtmpl = _make_consts()
    cols = lambda v: np.ascontiguousarray(v.reshape(8, 128).T)
    common = dict(
        c_maskx=maskx,
        wqt=np.ascontiguousarray(Wq.T), wkt=np.ascontiguousarray(Wk.T),
        wvt=np.ascontiguousarray(Wv.T), wot=np.ascontiguousarray(Wo.T),
        bq_cols=cols(bq), bk_cols=cols(bk),
        bv_bc=np.ascontiguousarray(np.broadcast_to(bv, (128, D))),
        bo_bc=np.ascontiguousarray(np.broadcast_to(bo, (128, D))),
        gw_cols=np.ascontiguousarray(gw.reshape(8, 128, 8).transpose(1, 0, 2).reshape(128, 64)),
        gb_bc=np.ascontiguousarray(np.broadcast_to(gb, (128, 8))),
        w1=w1, w2=w2,
        b1_cols=np.ascontiguousarray(b1.reshape(8, 32, 128).transpose(2, 0, 1).reshape(128, 256)),
        b2=b2,
        c_id16=id16, c_id32=id32, c_mneg=mneg, c_ktmpl=ktmpl, c_qtmpl=qtmpl,
    )
    in_maps = []
    for core in range(8):
        m = dict(common)
        m["x"] = np.ascontiguousarray(x[core * SEQS:(core + 1) * SEQS].reshape(T, D))
        in_maps.append(m)

    nc = _get_program()
    res = run_bass_kernel_spmd(nc, in_maps, core_ids=list(range(8)))
    out = np.concatenate([r["out"].reshape(SEQS, S, D) for r in res.results], axis=0)
    return out


# version 08
# speedup vs baseline: 1.7972x; 1.0396x over previous
import math
from contextlib import ExitStack

import numpy as np
import ml_dtypes

import concourse.bass as bass
import concourse.mybir as mybir
from concourse.bass_utils import run_bass_kernel_spmd

F32 = mybir.dt.float32
BF16 = mybir.dt.bfloat16
AF = mybir.ActivationFunctionType
ALU = mybir.AluOpType
AX = mybir.AxisListType
BF = ml_dtypes.bfloat16

D = 1024
H = 16
DH = 64
S = 2048
NEXP = 8
FF = 4096
SEQS = 2
T = SEQS * S
NT = T // 128
NEGBIG = -3.0e38

ENGINES = ("tensor", "vector", "scalar", "gpsimd", "sync")


class Rec:
    __slots__ = ("eng", "idx", "fn", "edges", "signal", "amount", "sig_val", "n")

    def __init__(self, eng, idx, fn, amount, n):
        self.eng = eng
        self.idx = idx
        self.fn = fn
        self.edges = []
        self.signal = False
        self.amount = amount
        self.sig_val = None
        self.n = n


class Builder:
    def __init__(self, nc, stack):
        self.nc = nc
        self.stack = stack
        self.plans = {e: [] for e in ENGINES}
        self.sig = {e: stack.enter_context(nc.semaphore("sig_" + e)) for e in ENGINES}
        self.writer = {}
        self.reader = {}
        self.covered = {e: {f: -1 for f in ENGINES} for e in ENGINES}
        self.last_dma = {}

    def op(self, eng, fn, reads=(), writes=(), after=(), dma=False, name=""):
        plan = self.plans[eng]
        rec = Rec(eng, len(plan), fn, 16 if dma else 1, name)
        if dma:
            rec.signal = True
            prev = self.last_dma.get(eng)
            if prev is not None:
                after = tuple(after) + (prev,)
            self.last_dma[eng] = rec
        raw = []
        other = []
        for t in reads:
            w = self.writer.get(t)
            if w:
                raw.extend(w.values())
        for t in writes:
            w = self.writer.get(t)
            if w:
                other.extend(w.values())
            r = self.reader.get(t)
            if r:
                other.extend(r.values())
        raw.extend(after)
        cov = self.covered[eng]
        for d, is_raw in [(d, True) for d in raw] + [(d, False) for d in other]:
            if d is rec:
                continue
            if d.eng == eng and eng == "tensor" and not dma and d.amount == 1:
                continue
            if d.eng == eng and not is_raw and not dma and d.amount == 1:
                continue
            if d.idx <= cov[d.eng]:
                continue
            d.signal = True
            rec.edges.append(d)
            cov[d.eng] = max(cov[d.eng], d.idx)
        plan.append(rec)
        for t in reads:
            self.reader.setdefault(t, {})[eng] = rec
        for t in writes:
            self.writer.setdefault(t, {})[eng] = rec
            self.reader[t] = {eng: rec}
        return rec

    def barrier(self):
        lasts = [self.plans[e][-1] for e in ENGINES if self.plans[e]]
        for e in ENGINES:
            if self.plans[e]:
                self.op(e, lambda eng: eng.nop(), after=tuple(lasts), name="barrier")

    def finalize_and_emit(self):
        for e in ENGINES:
            tot = 0
            for rec in self.plans[e]:
                if rec.signal:
                    tot += rec.amount
                    rec.sig_val = tot
        nc = self.nc
        sig = self.sig
        with nc.Block() as blk:
            for ename in ENGINES:
                plan = self.plans[ename]
                if not plan:
                    continue

                def make_runner(plan=plan):
                    def run(eng):
                        for rec in plan:
                            if rec.edges:
                                waits = {}
                                for d in rec.edges:
                                    waits[d.eng] = max(waits.get(d.eng, 0), d.sig_val)
                                for e2, v in waits.items():
                                    eng.wait_ge(sig[e2], v)
                            inst = rec.fn(eng)
                            if rec.signal:
                                assert inst is not None, f"signaling rec {rec.n} has no inst"
                                inst.then_inc(sig[rec.eng], rec.amount)
                    return run

                getattr(blk, ename)(make_runner())


def _alibi_slopes(n_heads):
    start = 2 ** (-(2 ** -(math.log2(n_heads) - 3)))
    return [start * start ** i for i in range(n_heads)]


def _make_consts():
    id16 = np.eye(128, dtype=np.float32).astype(BF)
    id32 = np.eye(128, dtype=np.float32)
    j = np.arange(128)
    mneg = np.where(j[None, :] <= j[:, None], 0.0, NEGBIG).astype(np.float32)

    slopes = _alibi_slopes(H)
    ktmpl = np.zeros((128, S), dtype=np.float32)
    qtmpl = np.zeros((128, S), dtype=np.float32)
    khi = (np.arange(S) >> 8).astype(np.float32)
    klo = (np.arange(S) & 255).astype(np.float32)
    qth = (np.arange(S) >> 7).astype(np.float32)
    qtl = (np.arange(S) & 127).astype(np.float32)
    for h in range(H):
        sl = np.float64(slopes[h])
        sa = np.float32(BF(np.float32(sl)))
        sb = np.float32(BF(np.float32(sl) - sa))
        base = 8 * h
        ktmpl[base + 0] = khi
        ktmpl[base + 1] = khi
        ktmpl[base + 2] = klo
        ktmpl[base + 3] = klo
        ktmpl[base + 4] = -sa * 128.0
        ktmpl[base + 5] = -sb * 128.0
        ktmpl[base + 6] = -sa
        ktmpl[base + 7] = -sb
        qtmpl[base + 0] = sa * 256.0
        qtmpl[base + 1] = sb * 256.0
        qtmpl[base + 2] = sa
        qtmpl[base + 3] = sb
        qtmpl[base + 4] = qth
        qtmpl[base + 5] = qth
        qtmpl[base + 6] = qtl
        qtmpl[base + 7] = qtl
    maskx = np.zeros((128, 1152), dtype=np.float32)
    maskx[:, 1024:] = mneg
    return id16, id32, mneg, maskx, ktmpl.astype(BF), qtmpl.astype(BF)


def build_program(debug=False):
    stack = ExitStack()
    nc = bass.Bass("TRN2", target_bir_lowering=False)

    dx = nc.dram_tensor("x", [T, D], F32, kind="ExternalInput")
    dwq = nc.dram_tensor("wqt", [D, D], F32, kind="ExternalInput")
    dwk = nc.dram_tensor("wkt", [D, D], F32, kind="ExternalInput")
    dwv = nc.dram_tensor("wvt", [D, D], F32, kind="ExternalInput")
    dwo = nc.dram_tensor("wot", [D, D], F32, kind="ExternalInput")
    dbq = nc.dram_tensor("bq_cols", [128, 8], F32, kind="ExternalInput")
    dbk = nc.dram_tensor("bk_cols", [128, 8], F32, kind="ExternalInput")
    dbvb = nc.dram_tensor("bv_bc", [128, D], F32, kind="ExternalInput")
    dbob = nc.dram_tensor("bo_bc", [128, D], F32, kind="ExternalInput")
    dgw = nc.dram_tensor("gw_cols", [128, 64], F32, kind="ExternalInput")
    dgbb = nc.dram_tensor("gb_bc", [128, 8], F32, kind="ExternalInput")
    dw1 = nc.dram_tensor("w1", [NEXP, D, FF], F32, kind="ExternalInput")
    db1 = nc.dram_tensor("b1_cols", [128, 256], F32, kind="ExternalInput")
    dw2 = nc.dram_tensor("w2", [NEXP, FF, D], F32, kind="ExternalInput")
    db2 = nc.dram_tensor("b2", [NEXP, D], F32, kind="ExternalInput")
    did16 = nc.dram_tensor("c_id16", [128, 128], BF16, kind="ExternalInput")
    did32 = nc.dram_tensor("c_id32", [128, 128], F32, kind="ExternalInput")
    dmneg = nc.dram_tensor("c_mneg", [128, 128], F32, kind="ExternalInput")
    dmaskx = nc.dram_tensor("c_maskx", [128, 1152], F32, kind="ExternalInput")
    dktm = nc.dram_tensor("c_ktmpl", [128, S], BF16, kind="ExternalInput")
    dqtm = nc.dram_tensor("c_qtmpl", [128, S], BF16, kind="ExternalInput")

    dout = nc.dram_tensor("out", [T, D], F32, kind="ExternalOutput")
    g_scr = nc.dram_tensor("g_scr", [T, D], F32, kind="ExternalOutput" if debug else "Internal")
    xn_scr = nc.dram_tensor("xn_scr", [8, 128, T], BF16, kind="ExternalOutput" if debug else "Internal")

    if debug:
        dbg_k = nc.dram_tensor("dbg_k", [128, 8 * S], BF16, kind="ExternalOutput")
        dbg_q = nc.dram_tensor("dbg_q", [128, 8 * S], BF16, kind="ExternalOutput")
        dbg_v = nc.dram_tensor("dbg_v", [128, 16 * 1024], BF16, kind="ExternalOutput")
        dbg_y = nc.dram_tensor("dbg_y", [128, 8 * 1024], BF16, kind="ExternalOutput")

    b = Builder(nc, stack)

    id16 = stack.enter_context(nc.sbuf_tensor("id16", [128, 128], BF16))
    id32 = stack.enter_context(nc.sbuf_tensor("id32", [128, 128], F32))
    mneg = stack.enter_context(nc.sbuf_tensor("mneg", [128, 128], F32))
    ktm = stack.enter_context(nc.sbuf_tensor("ktm", [128, S], BF16))
    qtm = stack.enter_context(nc.sbuf_tensor("qtm", [128, S], BF16))
    bq8 = stack.enter_context(nc.sbuf_tensor("bq8", [128, 8], F32))
    bkc = stack.enter_context(nc.sbuf_tensor("bkc", [128, 8], F32))
    comb = stack.enter_context(nc.sbuf_tensor("comb", [128, NT * 8], F32))
    epsv = stack.enter_context(nc.sbuf_tensor("epsv", [128, 1], F32))
    b.op("vector", lambda e: e.memset(epsv[:, :], 1e-5), writes=("epsv",), name="eps")

    b.op("sync", lambda e: e.dma_start(out=id16[:, :], in_=did16[:, :]), writes=("id16",), dma=True, name="l_id16")
    b.op("sync", lambda e: e.dma_start(out=id32[:, :], in_=did32[:, :]), writes=("id32",), dma=True, name="l_id32")
    b.op("sync", lambda e: e.dma_start(out=mneg[:, :], in_=dmneg[:, :]), writes=("mneg",), dma=True, name="l_mneg")
    b.op("sync", lambda e: e.dma_start(out=ktm[:, :], in_=dktm[:, :]), writes=("ktm",), dma=True, name="l_ktm")
    b.op("sync", lambda e: e.dma_start(out=qtm[:, :], in_=dqtm[:, :]), writes=("qtm",), dma=True, name="l_qtm")
    b.op("sync", lambda e: e.dma_start(out=bkc[:, :], in_=dbk[:, :]), writes=("bkc",), dma=True, name="l_bk")
    b.op("sync", lambda e: e.dma_start(out=bq8[:, :], in_=dbq[:, :]), writes=("bq8",), dma=True, name="l_bq")
    b.op("vector", lambda e: e.tensor_scalar(bq8[:, :], bq8[:, :], 0.125, None, op0=ALU.mult),
         reads=("bq8",), writes=("bq8",), name="bq8s")

    with ExitStack() as scA:
        combT = scA.enter_context(nc.sbuf_tensor("combT", [8, NT * 128], F32))
        w1b0_pre = scA.enter_context(nc.sbuf_tensor("w1b0", [128, 4 * 4096], BF16))

        with ExitStack() as sc:
            sb0 = lambda n, s, d: sc.enter_context(nc.sbuf_tensor(n, s, d))
            ps0 = lambda n, s, d: sc.enter_context(nc.psum_tensor(n, s, d))
            xnT = sb0("xnT", [128, 8 * T], BF16)
            gw = sb0("gw", [128, 64], F32)
            gbb = sb0("gbb", [128, 8], F32)
            xb = [sb0(f"xb{i}", [128, D], F32) for i in range(2)]
            xn32 = [sb0(f"xn32_{i}", [128, D], F32) for i in range(2)]
            xn16 = [sb0(f"xn16_{i}", [128, D], BF16) for i in range(2)]
            xT32 = [sb0(f"xT32_{i}", [128, D], F32) for i in range(2)]
            junk = sb0("junk", [128, D], BF16)
            nsum = sb0("nsum", [128, 1], F32)
            negmean = sb0("negmean", [128, 1], F32)
            msq = sb0("msq", [128, 1], F32)
            var = sb0("var", [128, 1], F32)
            lnv = sb0("lnv", [128, 1], F32)
            rstd = sb0("rstd", [128, 1], F32)
            ssq = sb0("ssq", [128, 1], F32)
            logit = sb0("logit", [128, 8], F32)
            m1 = sb0("m1", [128, 1], F32)
            nm1 = sb0("nm1", [128, 1], F32)
            ew = sb0("ew", [128, 8], F32)
            ind1 = sb0("ind1", [128, 8], F32)
            wm = sb0("wm", [128, 8], F32)
            m2 = sb0("m2", [128, 1], F32)
            indt = sb0("indt", [128, 8], F32)
            cu = sb0("cu", [128, 8], F32)
            dsum = sb0("dsum", [128, 1], F32)
            rec_ = sb0("rec_", [128, 1], F32)
            pp16 = [ps0(f"pp16_{i}", [128, 1024], BF16) for i in range(2)]
            pp32 = [ps0(f"pp32_{i}", [128, 1024], F32) for i in range(2)]
            psg = ps0("psg", [128, 8], F32)
            psct = ps0("psct", [8, 512], F32)

            b.op("sync", lambda e: e.dma_start(out=gw[:, :], in_=dgw[:, :]), writes=("gw",), dma=True, name="l_gw")
            b.op("sync", lambda e: e.dma_start(out=gbb[:, :], in_=dgbb[:, :]), writes=("gbb",), dma=True, name="l_gbb")

            for i in range(NT):
                par = i % 2
                t0 = i * 128
                xbp, xn32p, xn16p, xT32p = xb[par], xn32[par], xn16[par], xT32[par]
                nx, n32, n16, nT32 = f"xb{par}", f"xn32_{par}", f"xn16_{par}", f"xT32_{par}"
                pp16p, pp32p = pp16[par], pp32[par]
                n_pp16, n_pp32 = f"pp16_{par}", f"pp32_{par}"

                b.op("sync", lambda e, d=xbp, s=t0: e.dma_start(out=d[:, :], in_=dx[s:s + 128, :]),
                     writes=(nx,), dma=True, name=f"ldx{i}")
                b.op("vector", lambda e, x=xbp: e.reduce_sum(nsum[:, :], x[:, :], axis=AX.X, negate=True),
                     reads=(nx,), writes=("nsum",), name=f"su{i}")
                b.op("scalar", lambda e, x=xbp: e.activation(junk[:, :], x[:, :], AF.Square, accum_out=ssq[:, :]),
                     reads=(nx,), writes=("junk", "ssq"), name=f"sq{i}")
                b.op("vector", lambda e: e.tensor_scalar(negmean[:, :], nsum[:, :], 1.0 / D, None, op0=ALU.mult),
                     reads=("nsum",), writes=("negmean",), name=f"nm{i}")
                b.op("vector", lambda e: e.tensor_mul(msq[:, :], negmean[:, :], negmean[:, :]),
                     reads=("negmean",), writes=("msq",), name=f"ms{i}")
                b.op("vector", lambda e: e.tensor_scalar(var[:, :], ssq[:, :], 1.0 / D, msq[:, :],
                                                         op0=ALU.mult, op1=ALU.subtract),
                     reads=("ssq", "msq"), writes=("var",), name=f"va{i}")
                b.op("scalar", lambda e: e.activation(lnv[:, :], var[:, :], AF.Ln, bias=epsv[:, :]),
                     reads=("var", "epsv"), writes=("lnv",), name=f"ln{i}")
                b.op("scalar", lambda e: e.activation(rstd[:, :], lnv[:, :], AF.Exp, scale=-0.5),
                     reads=("lnv",), writes=("rstd",), name=f"rs{i}")
                b.op("vector", lambda e, x=xbp, o=xn32p: e.tensor_scalar(
                        o[:, :], x[:, :], negmean[:, :], rstd[:, :], op0=ALU.add, op1=ALU.mult),
                     reads=(nx, "negmean", "rstd"), writes=(n32,), name=f"xn{i}")
                b.op("vector", lambda e, s=xn32p, o=xn16p: e.tensor_copy(o[:, :], s[:, :]),
                     reads=(n32,), writes=(n16,), name=f"cst{i}")

                def tp16(e, src=xn16p, dst=pp16p):
                    inst = None
                    for c in range(8):
                        inst = e.matmul(dst[:, c * 128:(c + 1) * 128], src[:, c * 128:(c + 1) * 128],
                                        id16[:, :], start=(c == 0), stop=(c == 7), is_transpose=True)
                    return inst
                b.op("tensor", tp16, reads=(n16, "id16"), writes=(n_pp16,), name=f"t16_{i}")

                def tp32(e, src=xn32p, dst=pp32p):
                    inst = None
                    for c in range(8):
                        inst = e.matmul(dst[:, c * 128:(c + 1) * 128], src[:, c * 128:(c + 1) * 128],
                                        id32[:, :], start=(c in (0, 4)), stop=(c in (3, 7)), is_transpose=True)
                    return inst
                b.op("tensor", tp32, reads=(n32, "id32"), writes=(n_pp32,), name=f"t32_{i}")

                def ev16(e, src=pp16p, t0=t0):
                    dst = bass.AP(xnT, t0, [[8 * T, 128], [T, 8], [1, 128]])
                    return e.tensor_copy(dst, src[:, :])
                b.op("vector", ev16, reads=(n_pp16,), writes=("xnT",), name=f"e16_{i}")
                b.op("vector", lambda e, src=pp32p, dst=xT32p: e.tensor_copy(dst[:, :], src[:, :]),
                     reads=(n_pp32,), writes=(nT32,), name=f"e32_{i}")

                def gmm(e, src=xT32p):
                    inst = None
                    for c in range(8):
                        inst = e.matmul(psg[:, :], src[:, c * 128:(c + 1) * 128],
                                        gw[:, c * 8:(c + 1) * 8], start=(c == 0), stop=(c == 7))
                    return inst
                b.op("tensor", gmm, reads=(nT32, "gw"), writes=("psg",), name=f"gm{i}")
                b.op("vector", lambda e: e.tensor_add(logit[:, :], psg[:, :], gbb[:, :]),
                     reads=("psg", "gbb"), writes=("logit",), name=f"lg{i}")
                b.op("vector", lambda e: e.reduce_max(m1[:, :], logit[:, :], axis=AX.X),
                     reads=("logit",), writes=("m1",), name=f"m1_{i}")
                b.op("vector", lambda e: e.tensor_scalar(nm1[:, :], m1[:, :], -1.0, None, op0=ALU.mult),
                     reads=("m1",), writes=("nm1",), name=f"nm1_{i}")
                b.op("scalar", lambda e: e.activation(ew[:, :], logit[:, :], AF.Exp, bias=nm1[:, :]),
                     reads=("logit", "nm1"), writes=("ew",), name=f"ew{i}")
                b.op("vector", lambda e: e.tensor_scalar(ind1[:, :], logit[:, :], m1[:, :], None, op0=ALU.is_ge),
                     reads=("logit", "m1"), writes=("ind1",), name=f"i1_{i}")
                b.op("vector", lambda e: e.tensor_scalar(ind1[:, :], ind1[:, :], -1.0, 1.0,
                                                         op0=ALU.mult, op1=ALU.add),
                     reads=("ind1",), writes=("ind1",), name=f"ni{i}")
                b.op("vector", lambda e: e.tensor_mul(wm[:, :], ew[:, :], ind1[:, :]),
                     reads=("ew", "ind1"), writes=("wm",), name=f"wm{i}")
                b.op("vector", lambda e: e.reduce_max(m2[:, :], wm[:, :], axis=AX.X),
                     reads=("wm",), writes=("m2",), name=f"m2_{i}")
                b.op("vector", lambda e: e.tensor_scalar(indt[:, :], ew[:, :], m2[:, :], None, op0=ALU.is_ge),
                     reads=("ew", "m2"), writes=("indt",), name=f"it{i}")
                b.op("vector", lambda e: e.tensor_mul(cu[:, :], ew[:, :], indt[:, :]),
                     reads=("ew", "indt"), writes=("cu",), name=f"cu{i}")
                b.op("vector", lambda e: e.reduce_sum(dsum[:, :], cu[:, :], axis=AX.X),
                     reads=("cu",), writes=("dsum",), name=f"ds{i}")
                b.op("vector", lambda e: e.tensor_scalar(dsum[:, :], dsum[:, :], 1e-6, None, op0=ALU.max),
                     reads=("dsum",), writes=("dsum",), name=f"dm{i}")
                b.op("vector", lambda e: e.reciprocal(rec_[:, :], dsum[:, :]),
                     reads=("dsum",), writes=("rec_",), name=f"rc{i}")
                b.op("vector", lambda e, i=i: e.tensor_scalar(comb[:, i * 8:(i + 1) * 8], cu[:, :],
                                                              rec_[:, :], None, op0=ALU.mult),
                     reads=("cu", "rec_"), writes=("comb",), name=f"cb{i}")

                def ctp(e, i=i):
                    return e.matmul(psct[0:8, (i % 4) * 128:(i % 4) * 128 + 128],
                                    comb[:, i * 8:(i + 1) * 8], id32[:, :],
                                    start=(i % 4 == 0), stop=(i % 4 == 3), is_transpose=True)
                b.op("tensor", ctp, reads=("comb", "id32"), writes=("psct",), name=f"ct{i}")
                if i % 4 == 3:
                    b.op("vector", lambda e, i=i: e.tensor_copy(combT[0:8, (i - 3) * 128:(i + 1) * 128],
                                                                psct[0:8, :]),
                         reads=("psct",), writes=("combT",), name=f"ce{i}")

            for c in range(8):
                b.op("sync", lambda e, c=c: e.dma_start(out=xn_scr[c, :, :], in_=xnT[:, c * T:(c + 1) * T]),
                     reads=("xnT",), writes=("xn_scr",), dma=True, name=f"sp{c}")

            for fc in range(4):
                w1src = dw1[0, :, fc * 512:(fc + 1) * 512].rearrange("(c p) f -> p c f", p=128)
                w1dst = bass.AP(w1b0_pre, fc * 4096, [[4 * 4096, 128], [512, 8], [1, 512]])
                b.op("gpsimd", lambda e, d=w1dst, s=w1src: e.dma_start(out=d, in_=s),
                     writes=("w1b0",), dma=True, name=f"w1l0_{fc}")

        b.barrier()

        with ExitStack() as sc:
            sb1 = lambda n, s, d: sc.enter_context(nc.sbuf_tensor(n, s, d))
            ps1 = lambda n, s, d: sc.enter_context(nc.psum_tensor(n, s, d))
            w1b = [w1b0_pre, sb1("w1b1", [128, 4 * 4096], BF16)]
            w2h = sb1("w2h", [128, 16 * 1024], BF16)
            hdnT = [sb1(f"hdnT{i}", [128, 16 * 512], BF16) for i in range(2)]
            xnb1 = [sb1(f"xnb1_{i}", [128, 8 * 512], BF16) for i in range(2)]
            b1c = sb1("b1c", [128, 256], F32)
            b2sb = sb1("b2sb", [8, D], F32)
            gout = [sb1(f"gout{i}", [128, D], F32) for i in range(2)]
            psa = [ps1(f"psa{i}", [128, 512], F32) for i in range(2)]
            psb = [ps1(f"psb{i}", [128, 1024], F32) for i in range(2)]

            b.op("sync", lambda e: e.dma_start(out=b1c[:, :], in_=db1[:, :]), writes=("b1c",), dma=True, name="l_b1")
            b.op("sync", lambda e: e.dma_start(out=b2sb[:, :], in_=db2[:, :]), writes=("b2sb",), dma=True, name="l_b2")

            for gt in range(NT):
                go, ngo = gout[gt % 2], f"gout{gt % 2}"
                pb, npb = psb[gt % 2], f"psb{gt % 2}"

                def b2mm(e, gt=gt, pb=pb):
                    e.matmul(pb[:, 0:512], combT[0:8, gt * 128:(gt + 1) * 128],
                             b2sb[0:8, 0:512], start=True, stop=True)
                    return e.matmul(pb[:, 512:1024], combT[0:8, gt * 128:(gt + 1) * 128],
                                    b2sb[0:8, 512:1024], start=True, stop=True)
                b.op("tensor", b2mm, reads=("combT", "b2sb"), writes=(npb,), name=f"b2m{gt}")
                b.op("vector", lambda e, pb=pb, go=go: e.tensor_copy(go[:, :], pb[:, :]),
                     reads=(npb,), writes=(ngo,), name=f"b2c{gt}")
                b.op("gpsimd", lambda e, go=go, gt=gt: e.dma_start(
                        out=g_scr[gt * 128:(gt + 1) * 128, :], in_=go[:, :]),
                     reads=(ngo,), writes=(f"g{gt}",), dma=True, name=f"b2s{gt}")

            def load_w1(rnd):
                ex, fh = rnd // 2, rnd % 2
                wp, nwp = w1b[rnd % 2], f"w1b{rnd % 2}"
                for fc in range(4):
                    src = dw1[ex, :, fh * 2048 + fc * 512: fh * 2048 + (fc + 1) * 512] \
                        .rearrange("(c p) f -> p c f", p=128)
                    dst = bass.AP(wp, fc * 4096, [[4 * 4096, 128], [512, 8], [1, 512]])
                    b.op("gpsimd", lambda e, d=dst, s=src: e.dma_start(out=d, in_=s),
                         writes=(nwp,), dma=True, name=f"w1l{rnd}_{fc}")

            for rnd in range(16):
                ex, fh = rnd // 2, rnd % 2
                w1p, nw1 = w1b[rnd % 2], f"w1b{rnd % 2}"
                for fc in range(4):
                    src = dw2[ex, fh * 2048 + fc * 512: fh * 2048 + (fc + 1) * 512, :] \
                        .rearrange("(c p) h -> p c h", p=128)
                    dst = bass.AP(w2h, fc * 4 * 1024, [[16 * 1024, 128], [1024, 4], [1, 1024]])
                    b.op("gpsimd", lambda e, d=dst, s=src: e.dma_start(out=d, in_=s),
                         writes=("w2h",), dma=True, name=f"w2l{rnd}_{fc}")
                if rnd + 1 < 16:
                    load_w1(rnd + 1)

                for tg in range(8):
                    tpos = tg * 512
                    xp, nxp = xnb1[tg % 2], f"xnb1_{tg % 2}"
                    src = xn_scr[:, :, tpos:tpos + 512].rearrange("c p t -> p c t")
                    b.op("sync", lambda e, d=xp, s=src: e.dma_start(
                            out=d.ap().rearrange("p (c t) -> p c t", c=8), in_=s),
                         reads=("xn_scr",), writes=(nxp,), dma=True, name=f"xl{rnd}_{tg}")

                    for fb in range(16):
                        fc, fs = fb // 4, fb % 4
                        pa, npa = psa[fb % 2], f"psa{fb % 2}"

                        def amm(e, w1p=w1p, fc=fc, fs=fs, pa=pa, xp=xp):
                            inst = None
                            for dc in range(8):
                                inst = e.matmul(
                                    pa[:, :],
                                    w1p[:, fc * 4096 + dc * 512 + fs * 128: fc * 4096 + dc * 512 + fs * 128 + 128],
                                    xp[:, dc * 512:(dc + 1) * 512],
                                    start=(dc == 0), stop=(dc == 7))
                            return inst
                        b.op("tensor", amm, reads=(nw1, nxp), writes=(npa,), name=f"am{rnd}_{tg}_{fb}")

                        bcol = ex * 32 + fh * 16 + fb
                        hb, nhb = hdnT[tg % 2], f"hdnT{tg % 2}"
                        b.op("scalar", lambda e, pa=pa, fb=fb, bcol=bcol, hb=hb: e.activation(
                                hb[:, fb * 512:(fb + 1) * 512], pa[:, :], AF.Gelu,
                                bias=b1c[:, bcol:bcol + 1]),
                             reads=(npa, "b1c"), writes=(nhb,), name=f"ge{rnd}_{tg}_{fb}")

                    for tl in range(4):
                        gt = tg * 4 + tl
                        pb, npb = psb[tl % 2], f"psb{tl % 2}"
                        go, ngo = gout[tl % 2], f"gout{tl % 2}"

                        hb, nhb = hdnT[tg % 2], f"hdnT{tg % 2}"
                        def bmm(e, tl=tl, pb=pb, hb=hb):
                            inst = None
                            for fb in range(16):
                                lhs = hb[:, fb * 512 + tl * 128: fb * 512 + tl * 128 + 128]
                                inst = e.matmul(pb[:, 0:512], lhs, w2h[:, fb * 1024: fb * 1024 + 512],
                                                start=(fb == 0), stop=(fb == 15))
                                inst = e.matmul(pb[:, 512:1024], lhs, w2h[:, fb * 1024 + 512: fb * 1024 + 1024],
                                                start=(fb == 0), stop=(fb == 15))
                            return inst
                        b.op("tensor", bmm, reads=(nhb, "w2h"), writes=(npb,), name=f"bm{rnd}_{gt}")
                        b.op("vector", lambda e, pb=pb, go=go, gt=gt, ex=ex: e.tensor_scalar(
                                go[:, :], pb[:, :], comb[:, gt * 8 + ex: gt * 8 + ex + 1], None, op0=ALU.mult),
                             reads=(npb, "comb"), writes=(ngo,), name=f"be{rnd}_{gt}")
                        b.op("gpsimd", lambda e, go=go, gt=gt: e.dma_start(
                                out=g_scr[gt * 128:(gt + 1) * 128, :], in_=go[:, :], accum_op=ALU.add),
                             reads=(ngo,), writes=(f"g{gt}",), dma=True, name=f"ga{rnd}_{gt}")

        b.barrier()

    with ExitStack() as sc:
        sb2 = lambda n, s, d: sc.enter_context(nc.sbuf_tensor(n, s, d))
        kT = sb2("kT", [128, 8 * S], BF16)
        qT2 = sb2("qT2", [128, 8 * S], BF16)
        v_tm = sb2("v_tm", [128, 16 * 1024], BF16)
        yT = sb2("yT", [128, 8 * 1024], BF16)
        bob = sb2("bob", [128, D], F32)
        lcur = [sb2(f"lcur{i}", [128, 1], F32) for i in range(2)]
        rl2 = [sb2(f"rl2_{i}", [128, 1], F32) for i in range(2)]
        mstat = [sb2(f"mstat{i}", [128, 3], F32) for i in range(2)]
        lc2 = [sb2(f"lc2_{i}", [128, 2], F32) for i in range(2)]
        negm = [sb2(f"negm{i}", [128, 1], F32) for i in range(2)]
        stmp = sb2("stmp", [128, D], F32)
        sgb = sb2("sgb", [128, D], F32)

        b.op("sync", lambda e: e.dma_start(out=bob[:, :], in_=dbob[:, :]), writes=("bob",), dma=True, name="l_bob")

        for sq in range(SEQS):
            soff = sq * S
            with ExitStack() as sp:
                sbp = lambda n, s, d: sp.enter_context(nc.sbuf_tensor(n, s, d))
                psp = lambda n, s, d: sp.enter_context(nc.psum_tensor(n, s, d))
                wkb = sbp(f"wkb{sq}", [128, 8 * 1024], BF16)
                wqb = sbp(f"wqb{sq}", [128, 8 * 1024], BF16)
                wvb = sbp(f"wvb{sq}", [128, 8 * 1024], BF16)
                xnb = [sbp(f"xnb2_{sq}_{i}", [128, 8 * 512], BF16) for i in range(2)]
                bvb = sbp(f"bvb{sq}", [128, D], F32)
                pko = [psp(f"pko{sq}_{i}", [128, 512], F32) for i in range(2)]
                b.op("sync", lambda e, bvb=bvb: e.dma_start(out=bvb[:, :], in_=dbvb[:, :]),
                     writes=(f"bvb{sq}",), dma=True, name=f"l_bvb{sq}")
                psv = psp(f"psv{sq}", [128, 1024], F32)

                for wdst, wsrc, wname in ((wkb, dwk, f"wkb{sq}"), (wqb, dwq, f"wqb{sq}"), (wvb, dwv, f"wvb{sq}")):
                    src = wsrc[:, :].rearrange("(c p) i -> p c i", p=128)
                    b.op("gpsimd", lambda e, dd=wdst, s=src: e.dma_start(
                            out=dd.ap().rearrange("p (c i) -> p c i", c=8), in_=s),
                         writes=(wname,), dma=True, name=f"lw_{wname}")

                for tc in range(4):
                    xp, nxp = xnb[tc % 2], f"xnb2_{sq}_{tc % 2}"
                    src = xn_scr[:, :, soff + tc * 512: soff + (tc + 1) * 512].rearrange("c p t -> p c t")
                    b.op("sync", lambda e, d=xp, s=src: e.dma_start(
                            out=d.ap().rearrange("p (c t) -> p c t", c=8), in_=s),
                         reads=("xn_scr",), writes=(nxp,), dma=True, name=f"x2{sq}_{tc}")

                    for proj, wb, nwb, dstT, ndT, bias, scale in (
                            ("k", wkb, f"wkb{sq}", kT, "kT", bkc, 1.0),
                            ("q", wqb, f"wqb{sq}", qT2, "qT2", bq8, 0.125)):
                        for ic in range(8):
                            pk, npk = pko[ic % 2], f"pko{sq}_{ic % 2}"

                            def pmm(e, xp=xp, ic=ic, pk=pk, wb=wb):
                                inst = None
                                for dc in range(8):
                                    inst = e.matmul(pk[:, :],
                                                    wb[:, dc * 1024 + ic * 128: dc * 1024 + ic * 128 + 128],
                                                    xp[:, dc * 512:(dc + 1) * 512],
                                                    start=(dc == 0), stop=(dc == 7))
                                return inst
                            b.op("tensor", pmm, reads=(nwb, nxp), writes=(npk,), name=f"{proj}m{sq}_{tc}_{ic}")
                            b.op("scalar", lambda e, pk=pk, ic=ic, tc=tc, dstT=dstT, bias=bias, scale=scale:
                                    e.activation(dstT[:, ic * S + tc * 512: ic * S + (tc + 1) * 512], pk[:, :],
                                                 AF.Identity, bias=bias[:, ic:ic + 1], scale=scale),
                                 reads=(npk, "bkc", "bq8"), writes=(ndT,), name=f"{proj}e{sq}_{tc}_{ic}")

                    for tl in range(4):
                        tj = tc * 4 + tl

                        def vmm(e, xp=xp, tl=tl, wb=wvb):
                            inst = None
                            for dc in range(8):
                                lhs = xp[:, dc * 512 + tl * 128: dc * 512 + tl * 128 + 128]
                                inst = e.matmul(psv[:, 0:512], lhs, wb[:, dc * 1024: dc * 1024 + 512],
                                                start=(dc == 0), stop=(dc == 7))
                                inst = e.matmul(psv[:, 512:1024], lhs, wb[:, dc * 1024 + 512: dc * 1024 + 1024],
                                                start=(dc == 0), stop=(dc == 7))
                            return inst
                        b.op("tensor", vmm, reads=(f"wvb{sq}", nxp), writes=(f"psv{sq}",), name=f"vm{sq}_{tc}_{tl}")
                        b.op("vector", lambda e, tj=tj: e.scalar_tensor_tensor(
                                v_tm[:, tj * 1024:(tj + 1) * 1024], psv[:, :], 1.0, bvb[:, :],
                                op0=ALU.mult, op1=ALU.add),
                             reads=(f"psv{sq}", f"bvb{sq}"), writes=("v_tm",), name=f"ve{sq}_{tc}_{tl}")

            if debug and sq == 0:
                b.op("sync", lambda e: e.dma_start(out=dbg_k[:, :], in_=kT[:, :]),
                     reads=("kT",), dma=True, name="dbgk")
                b.op("sync", lambda e: e.dma_start(out=dbg_q[:, :], in_=qT2[:, :]),
                     reads=("qT2",), dma=True, name="dbgq")
                b.op("sync", lambda e: e.dma_start(out=dbg_v[:, :], in_=v_tm[:, :]),
                     reads=("v_tm",), dma=True, name="dbgv")
            b.barrier()

            with ExitStack() as sp:
                sbp = lambda n, s, d: sp.enter_context(nc.sbuf_tensor(n, s, d))
                psp = lambda n, s, d: sp.enter_context(nc.psum_tensor(n, s, d))
                wob = sbp(f"wob{sq}", [128, 8 * 1024], BF16)
                kqa = [sbp(f"kqa{sq}_{i}", [72, S], BF16) for i in range(2)]
                qaug = [sbp(f"qaug{sq}_{i}", [72, 1024], BF16) for i in range(2)]
                Ebuf = [sbp(f"Eb{sq}_{i}", [128, S], BF16) for i in range(3)]
                ETb = [sbp(f"ET{sq}_{i}", [128, 16 * 128], BF16) for i in range(2)]
                sbs = [sbp(f"sbs{sq}_{i}", [128, S], F32) for i in range(2)]
                gti = [sbp(f"gti{sq}_0", [128, D], F32)] * 2
                xti = [sbp(f"xti{sq}_0", [128, D], F32)] * 2
                pssp = [psp(f"pss{sq}_{i}", [128, 1024], F32) for i in range(2)]
                pset = psp(f"pset{sq}", [128, 1024], BF16)
                psy = psp(f"psy{sq}", [64, 128], F32)
                pso = psp(f"pso{sq}", [128, 1024], F32)

                src = dwo[:, :].rearrange("(c p) i -> p c i", p=128)
                b.op("gpsimd", lambda e, s=src: e.dma_start(
                        out=wob.ap().rearrange("p (c i) -> p c i", c=8), in_=s),
                     writes=(f"wob{sq}",), dma=True, name=f"lwo{sq}")

                gc = [0]
                for half in range(2):
                    stq = []
                    def flush_one():
                        d2, t1, t2 = stq.pop(0)
                        d2(); t1()
                        stq.append((None, None, t2))
                    for h in range(H):
                        hp = h % 2
                        kq, nkq = kqa[hp], f"kqa{sq}_{hp}"
                        qa, nqa = qaug[hp], f"qaug{sq}_{hp}"
                        b.op("sync", lambda e, kq=kq, h=h: e.dma_start(
                                out=kq[0:64, :],
                                in_=kT[(h % 2) * 64:(h % 2) * 64 + 64, (h // 2) * S:(h // 2 + 1) * S]),
                             reads=("kT",), writes=(nkq,), dma=True, name=f"bk{sq}_{half}_{h}")
                        b.op("sync", lambda e, kq=kq, h=h: e.dma_start(
                                out=kq[64:72, :], in_=ktm[8 * h:8 * h + 8, :]),
                             reads=("ktm",), writes=(nkq,), dma=True, name=f"bkt{sq}_{half}_{h}")
                        b.op("sync", lambda e, qa=qa, h=h, half=half: e.dma_start(
                                out=qa[0:64, :],
                                in_=qT2[(h % 2) * 64:(h % 2) * 64 + 64,
                                        (h // 2) * S + half * 1024:(h // 2) * S + half * 1024 + 1024]),
                             reads=("qT2",), writes=(nqa,), dma=True, name=f"bq{sq}_{half}_{h}")
                        b.op("sync", lambda e, qa=qa, h=h, half=half: e.dma_start(
                                out=qa[64:72, :], in_=qtm[8 * h:8 * h + 8, half * 1024: half * 1024 + 1024]),
                             reads=("qtm",), writes=(nqa,), dma=True, name=f"bqt{sq}_{half}_{h}")

                        for A in range(8):
                            Ag = half * 8 + A
                            W = (Ag + 1) * 128
                            ep_i = gc[0] % 3
                            et_i = gc[0] % 2
                            spar = A % 2
                            gc[0] += 1
                            Ep, nEp = Ebuf[ep_i], f"Eb{sq}_{ep_i}"
                            ETp, nETp = ETb[et_i], f"ET{sq}_{et_i}"
                            sbp_ = sbs[spar]
                            nsbs = f"sbs{sq}_{spar}"
                            nneg = f"negm{sq}_{spar}"
                            nms = f"mstat{sq}_{spar}"
                            tag = f"{sq}_{half}_{h}_{A}"
                            if Ag <= 7:
                                pieces = [(A % 2, 0, W)]
                            else:
                                pieces = [(A % 2, 0, 1024), ((A + 1) % 2, 1024, W)]
                            n512 = Ag // 4
                            rem = Ag % 4
                            chunks = [(c * 512, (c + 1) * 512, True, True) for c in range(n512)]
                            if rem:
                                chunks.append((n512 * 512, n512 * 512 + rem * 128, True, False))
                                chunks.append((W - 128, W, False, True))
                            else:
                                chunks.append((W - 128, W, True, True))

                            piece_info = []
                            for pi, (pb, lo, hi) in enumerate(pieces):
                                ps_ = pssp[pb]
                                nps = f"pss{sq}_{pb}"

                                def smm(e, qa=qa, kq=kq, A=A, ps_=ps_, lo=lo, hi=hi, chunks=tuple(chunks)):
                                    inst = None
                                    lhs = qa[0:72, A * 128:(A + 1) * 128]
                                    for gl, gh, st, sp in chunks:
                                        if gl < lo or gl >= hi:
                                            continue
                                        inst = e.matmul(ps_[:, gl - lo:gh - lo], lhs,
                                                        kq[0:72, gl:gh], start=st, stop=sp)
                                    return inst
                                b.op("tensor", smm, reads=(nqa, nkq), writes=(nps,), name=f"s{tag}_{pi}")
                                piece_info.append((pi, pb, lo, hi, ps_, nps))

                            if len(stq) > 0:
                                mid = None
                                if stq and stq[0][0] is None:
                                    _, _, mid = stq.pop(0)
                                if stq:
                                    d2p, t1p, t2p = stq.pop(0)
                                    d2p()
                                    t1p(mid)
                                    stq.insert(0, (None, None, t2p))
                                elif mid is not None:
                                    mid()

                            mains = 0
                            for pi, pb, lo, hi, ps_, nps in piece_info:
                                hid = min(hi, W - 128)
                                if hid > lo:
                                    mc = mains
                                    mains += 1
                                    b.op("vector", lambda e, ps_=ps_, lo=lo, hid=hid, mc=mc, sbp_=sbp_,
                                                spar=spar: e.tensor_scalar(
                                            sbp_[:, lo:hid], ps_[:, 0:hid - lo], 1.0, None,
                                            op0=ALU.mult, op1=ALU.max,
                                            accum_out=mstat[spar][:, mc:mc + 1]),
                                         reads=(nps,), writes=(nsbs, nms), name=f"e{tag}_{pi}")
                                if hi == W:
                                    dc = mains
                                    b.op("vector", lambda e, ps_=ps_, lo=lo, W=W, sbp_=sbp_: e.scalar_tensor_tensor(
                                            sbp_[:, W - 128:W], ps_[:, W - 128 - lo:W - lo], 1.0, mneg[:, :],
                                            op0=ALU.mult, op1=ALU.add),
                                         reads=(nps, "mneg"), writes=(nsbs,), name=f"d{tag}")
                                    b.op("vector", lambda e, W=W, sbp_=sbp_, spar=spar, dc=dc: e.tensor_scalar(
                                            sbp_[:, W - 128:W], sbp_[:, W - 128:W], 1.0, None,
                                            op0=ALU.mult, op1=ALU.max,
                                            accum_out=mstat[spar][:, dc:dc + 1]),
                                         reads=(nsbs,), writes=(nsbs, nms), name=f"dm{tag}")
                            ncols = mains + 1
                            b.op("vector", lambda e, spar=spar, ncols=ncols: e.reduce_max(
                                    negm[spar][:, :], mstat[spar][:, 0:ncols], axis=AX.X, negate=True),
                                 reads=(nms,), writes=(nneg,), name=f"rm{tag}")
                            b.op("scalar", lambda e, Ep=Ep, W=W, sbp_=sbp_, spar=spar: e.activation(
                                    Ep[:, 0:W], sbp_[:, 0:W], AF.Exp, bias=negm[spar][:, :],
                                    accum_out=lcur[spar][:, :]),
                                 reads=(nsbs, nneg), writes=(nEp, f"lcur{sq}_{spar}"), name=f"x{tag}")

                            def mk_d2(Ep=Ep, nEp=nEp, W=W, spar=spar, tag=tag):
                                def d2():
                                    b.op("vector", lambda e: e.reciprocal(rl2[spar][:, :], lcur[spar][:, :]),
                                         reads=(f"lcur{sq}_{spar}",), writes=(f"rl2_{sq}_{spar}",),
                                         name=f"rl{tag}")
                                    b.op("vector", lambda e: e.tensor_scalar(
                                            Ep[:, 0:W], Ep[:, 0:W], rl2[spar][:, :], None, op0=ALU.mult),
                                         reads=(nEp, f"rl2_{sq}_{spar}"), writes=(nEp,), name=f"en{tag}")
                                return d2

                            def mk_t1(Ep=Ep, nEp=nEp, ETp=ETp, nETp=nETp, Ag=Ag, tag=tag):
                                def t1(mid=None):
                                    ngroups = (Ag + 8) // 8
                                    for g in range(ngroups):
                                        if g == 1 and mid is not None:
                                            mid()
                                            mid = None
                                        j0, j1 = g * 8, min(g * 8 + 8, Ag + 1)

                                        def tpe(e, Ep=Ep, j0=j0, j1=j1):
                                            inst = None
                                            for j in range(j0, j1):
                                                inst = e.matmul(pset[:, (j - j0) * 128:(j - j0 + 1) * 128],
                                                                Ep[:, j * 128:(j + 1) * 128], id16[:, :],
                                                                start=(j == j0), stop=(j == j1 - 1),
                                                                is_transpose=True)
                                            return inst
                                        b.op("tensor", tpe, reads=(nEp, "id16"), writes=("pset",),
                                             name=f"t{tag}_{g}")
                                        b.op("scalar", lambda e, ETp=ETp, j0=j0, j1=j1: e.copy(
                                                ETp[:, j0 * 128:j1 * 128], pset[:, 0:(j1 - j0) * 128]),
                                             reads=("pset",), writes=(nETp,), name=f"u{tag}_{g}")
                                    if mid is not None:
                                        mid()
                                return t1

                            def mk_t2(ETp=ETp, nETp=nETp, h=h, A=A, Ag=Ag, tag=tag):
                                def t2():
                                    def avm(e, ETp=ETp, h=h, Ag=Ag):
                                        inst = None
                                        for jj in range(Ag + 1):
                                            inst = e.matmul(psy[0:64, :],
                                                            v_tm[:, jj * 1024 + h * 64: jj * 1024 + h * 64 + 64],
                                                            ETp[:, jj * 128:(jj + 1) * 128],
                                                            start=(jj == 0), stop=(jj == Ag))
                                        return inst
                                    b.op("tensor", avm, reads=(nETp, "v_tm"), writes=("psy",), name=f"a{tag}")
                                    b.op("vector", lambda e, h=h, A=A: e.tensor_copy(
                                            yT[(h % 2) * 64:(h % 2) * 64 + 64,
                                               (h // 2) * 1024 + A * 128:(h // 2) * 1024 + A * 128 + 128],
                                            psy[0:64, :]),
                                         reads=("psy",), writes=("yT",), name=f"y{tag}")
                                return t2

                            stq.append((mk_d2(), mk_t1(), mk_t2()))
                    while stq:
                        d2, t1, t2 = stq.pop(0)
                        if d2 is not None:
                            d2(); t1()
                        t2()

                    if debug and sq == 0 and half == 0:
                        b.op("sync", lambda e: e.dma_start(out=dbg_y[:, :], in_=yT[:, :]),
                             reads=("yT",), dma=True, name="dbgy")
                    for tl in range(8):
                        t0 = soff + half * 1024 + tl * 128
                        gp, ngp = gti[0], f"gti{sq}_0"
                        xp2, nxp2 = xti[0], f"xti{sq}_0"
                        ob, nob = xp2, nxp2
                        gt_glob = t0 // 128
                        b.op("sync", lambda e, gp=gp, t0=t0: e.dma_start(out=gp[:, :], in_=g_scr[t0:t0 + 128, :]),
                             reads=(f"g{gt_glob}",), writes=(ngp,), dma=True, name=f"lg{sq}_{half}_{tl}")
                        b.op("sync", lambda e, xp2=xp2, t0=t0: e.dma_start(out=xp2[:, :], in_=dx[t0:t0 + 128, :]),
                             writes=(nxp2,), dma=True, name=f"lx{sq}_{half}_{tl}")

                        def omm(e, tl=tl):
                            inst = None
                            for ic in range(8):
                                lhs = yT[:, ic * 1024 + tl * 128: ic * 1024 + tl * 128 + 128]
                                inst = e.matmul(pso[:, 0:512], lhs, wob[:, ic * 1024: ic * 1024 + 512],
                                                start=(ic == 0), stop=(ic == 7))
                                inst = e.matmul(pso[:, 512:1024], lhs, wob[:, ic * 1024 + 512: ic * 1024 + 1024],
                                                start=(ic == 0), stop=(ic == 7))
                            return inst
                        b.op("tensor", omm, reads=("yT", f"wob{sq}"), writes=(f"pso{sq}",), name=f"o{sq}_{half}_{tl}")
                        b.op("vector", lambda e: e.scalar_tensor_tensor(
                                stmp[:, :], pso[:, :], 1.0, bob[:, :], op0=ALU.mult, op1=ALU.add),
                             reads=(f"pso{sq}", "bob"), writes=("stmp",), name=f"se{sq}_{half}_{tl}")
                        b.op("scalar", lambda e, gp=gp: e.activation(gp[:, :], gp[:, :], AF.Silu),
                             reads=(ngp,), writes=(ngp,), name=f"si{sq}_{half}_{tl}")
                        b.op("vector", lambda e, gp=gp: e.tensor_mul(sgb[:, :], stmp[:, :], gp[:, :]),
                             reads=("stmp", ngp), writes=("sgb",), name=f"sg{sq}_{half}_{tl}")
                        b.op("vector", lambda e, xp2=xp2: e.tensor_add(xp2[:, :], sgb[:, :], xp2[:, :]),
                             reads=("sgb", nxp2), writes=(nxp2,), name=f"oa{sq}_{half}_{tl}")
                        b.op("sync", lambda e, ob=ob, t0=t0: e.dma_start(out=dout[t0:t0 + 128, :], in_=ob[:, :]),
                             reads=(nob,), writes=("out",), dma=True, name=f"so{sq}_{half}_{tl}")

            if sq + 1 < SEQS:
                b.barrier()

        final_out = b.writer["out"]["sync"]
        b.op("sync", lambda e: e.nop(), after=(final_out,), name="final")

    b.finalize_and_emit()
    stack.close()
    return nc


_cached = {}


def _get_program():
    if "nc" not in _cached:
        _cached["nc"] = build_program()
    return _cached["nc"]


def kernel(L0_attn_x, L0_attn_Wq, L0_attn_bq, L0_attn_Wk, L0_attn_bk,
           L0_attn_Wv, L0_attn_bv, L0_attn_Wo, L0_attn_bo,
           L0_ffn_gate_w, L0_ffn_gate_b, L0_ffn_w1, L0_ffn_b1,
           L0_ffn_w2, L0_ffn_b2):
    f32 = lambda a: np.ascontiguousarray(np.asarray(a, dtype=np.float32))
    x = f32(L0_attn_x)
    Wq, Wk, Wv, Wo = map(f32, (L0_attn_Wq, L0_attn_Wk, L0_attn_Wv, L0_attn_Wo))
    bq, bk, bv, bo = map(f32, (L0_attn_bq, L0_attn_bk, L0_attn_bv, L0_attn_bo))
    gw, gb = f32(L0_ffn_gate_w), f32(L0_ffn_gate_b)
    w1, b1, w2, b2 = map(f32, (L0_ffn_w1, L0_ffn_b1, L0_ffn_w2, L0_ffn_b2))

    id16, id32, mneg, maskx, ktmpl, qtmpl = _make_consts()
    cols = lambda v: np.ascontiguousarray(v.reshape(8, 128).T)
    common = dict(
        c_maskx=maskx,
        wqt=np.ascontiguousarray(Wq.T), wkt=np.ascontiguousarray(Wk.T),
        wvt=np.ascontiguousarray(Wv.T), wot=np.ascontiguousarray(Wo.T),
        bq_cols=cols(bq), bk_cols=cols(bk),
        bv_bc=np.ascontiguousarray(np.broadcast_to(bv, (128, D))),
        bo_bc=np.ascontiguousarray(np.broadcast_to(bo, (128, D))),
        gw_cols=np.ascontiguousarray(gw.reshape(8, 128, 8).transpose(1, 0, 2).reshape(128, 64)),
        gb_bc=np.ascontiguousarray(np.broadcast_to(gb, (128, 8))),
        w1=w1, w2=w2,
        b1_cols=np.ascontiguousarray(b1.reshape(8, 32, 128).transpose(2, 0, 1).reshape(128, 256)),
        b2=b2,
        c_id16=id16, c_id32=id32, c_mneg=mneg, c_ktmpl=ktmpl, c_qtmpl=qtmpl,
    )
    in_maps = []
    for core in range(8):
        m = dict(common)
        m["x"] = np.ascontiguousarray(x[core * SEQS:(core + 1) * SEQS].reshape(T, D))
        in_maps.append(m)

    nc = _get_program()
    res = run_bass_kernel_spmd(nc, in_maps, core_ids=list(range(8)))
    out = np.concatenate([r["out"].reshape(SEQS, S, D) for r in res.results], axis=0)
    return out


# version 09
# speedup vs baseline: 1.8373x; 1.0223x over previous
import math
from contextlib import ExitStack

import numpy as np
import ml_dtypes

import concourse.bass as bass
import concourse.mybir as mybir
from concourse.bass_utils import run_bass_kernel_spmd

F32 = mybir.dt.float32
BF16 = mybir.dt.bfloat16
AF = mybir.ActivationFunctionType
ALU = mybir.AluOpType
AX = mybir.AxisListType
BF = ml_dtypes.bfloat16

D = 1024
H = 16
DH = 64
S = 2048
NEXP = 8
FF = 4096
SEQS = 2
T = SEQS * S
NT = T // 128
NEGBIG = -3.0e38

ENGINES = ("tensor", "vector", "scalar", "gpsimd", "sync")


class Rec:
    __slots__ = ("eng", "idx", "fn", "edges", "signal", "amount", "sig_val", "n")

    def __init__(self, eng, idx, fn, amount, n):
        self.eng = eng
        self.idx = idx
        self.fn = fn
        self.edges = []
        self.signal = False
        self.amount = amount
        self.sig_val = None
        self.n = n


class Builder:
    def __init__(self, nc, stack):
        self.nc = nc
        self.stack = stack
        self.plans = {e: [] for e in ENGINES}
        self.sig = {e: stack.enter_context(nc.semaphore("sig_" + e)) for e in ENGINES}
        self.writer = {}
        self.reader = {}
        self.covered = {e: {f: -1 for f in ENGINES} for e in ENGINES}
        self.last_dma = {}

    def op(self, eng, fn, reads=(), writes=(), after=(), dma=False, name=""):
        plan = self.plans[eng]
        rec = Rec(eng, len(plan), fn, 16 if dma else 1, name)
        if dma:
            rec.signal = True
            prev = self.last_dma.get(eng)
            if prev is not None:
                after = tuple(after) + (prev,)
            self.last_dma[eng] = rec
        raw = []
        other = []
        for t in reads:
            w = self.writer.get(t)
            if w:
                raw.extend(w.values())
        for t in writes:
            w = self.writer.get(t)
            if w:
                other.extend(w.values())
            r = self.reader.get(t)
            if r:
                other.extend(r.values())
        raw.extend(after)
        cov = self.covered[eng]
        for d, is_raw in [(d, True) for d in raw] + [(d, False) for d in other]:
            if d is rec:
                continue
            if d.eng == eng and eng == "tensor" and not dma and d.amount == 1:
                continue
            if d.eng == eng and not is_raw and not dma and d.amount == 1:
                continue
            if d.idx <= cov[d.eng]:
                continue
            d.signal = True
            rec.edges.append(d)
            cov[d.eng] = max(cov[d.eng], d.idx)
        plan.append(rec)
        for t in reads:
            self.reader.setdefault(t, {})[eng] = rec
        for t in writes:
            self.writer.setdefault(t, {})[eng] = rec
            self.reader[t] = {eng: rec}
        return rec

    def barrier(self):
        lasts = [self.plans[e][-1] for e in ENGINES if self.plans[e]]
        for e in ENGINES:
            if self.plans[e]:
                self.op(e, lambda eng: eng.nop(), after=tuple(lasts), name="barrier")

    def finalize_and_emit(self):
        for e in ENGINES:
            tot = 0
            for rec in self.plans[e]:
                if rec.signal:
                    tot += rec.amount
                    rec.sig_val = tot
        nc = self.nc
        sig = self.sig
        with nc.Block() as blk:
            for ename in ENGINES:
                plan = self.plans[ename]
                if not plan:
                    continue

                def make_runner(plan=plan):
                    def run(eng):
                        for rec in plan:
                            if rec.edges:
                                waits = {}
                                for d in rec.edges:
                                    waits[d.eng] = max(waits.get(d.eng, 0), d.sig_val)
                                for e2, v in waits.items():
                                    eng.wait_ge(sig[e2], v)
                            inst = rec.fn(eng)
                            if rec.signal:
                                assert inst is not None, f"signaling rec {rec.n} has no inst"
                                inst.then_inc(sig[rec.eng], rec.amount)
                    return run

                getattr(blk, ename)(make_runner())


def _alibi_slopes(n_heads):
    start = 2 ** (-(2 ** -(math.log2(n_heads) - 3)))
    return [start * start ** i for i in range(n_heads)]


def _make_consts():
    id16 = np.eye(128, dtype=np.float32).astype(BF)
    id32 = np.eye(128, dtype=np.float32)
    j = np.arange(128)
    mneg = np.where(j[None, :] <= j[:, None], 0.0, NEGBIG).astype(np.float32)

    slopes = _alibi_slopes(H)
    ktmpl = np.zeros((128, S), dtype=np.float32)
    qtmpl = np.zeros((128, S), dtype=np.float32)
    khi = (np.arange(S) >> 8).astype(np.float32)
    klo = (np.arange(S) & 255).astype(np.float32)
    qth = (np.arange(S) >> 7).astype(np.float32)
    qtl = (np.arange(S) & 127).astype(np.float32)
    for h in range(H):
        sl = np.float64(slopes[h])
        sa = np.float32(BF(np.float32(sl)))
        sb = np.float32(BF(np.float32(sl) - sa))
        base = 8 * h
        ktmpl[base + 0] = khi
        ktmpl[base + 1] = khi
        ktmpl[base + 2] = klo
        ktmpl[base + 3] = klo
        ktmpl[base + 4] = -sa * 128.0
        ktmpl[base + 5] = -sb * 128.0
        ktmpl[base + 6] = -sa
        ktmpl[base + 7] = -sb
        qtmpl[base + 0] = sa * 256.0
        qtmpl[base + 1] = sb * 256.0
        qtmpl[base + 2] = sa
        qtmpl[base + 3] = sb
        qtmpl[base + 4] = qth
        qtmpl[base + 5] = qth
        qtmpl[base + 6] = qtl
        qtmpl[base + 7] = qtl
    maskx = np.zeros((128, 1152), dtype=np.float32)
    maskx[:, 1024:] = mneg
    return id16, id32, mneg, maskx, ktmpl.astype(BF), qtmpl.astype(BF)


def build_program(debug=False):
    stack = ExitStack()
    nc = bass.Bass("TRN2", target_bir_lowering=False)

    dx = nc.dram_tensor("x", [T, D], F32, kind="ExternalInput")
    dwq = nc.dram_tensor("wqt", [D, D], F32, kind="ExternalInput")
    dwk = nc.dram_tensor("wkt", [D, D], F32, kind="ExternalInput")
    dwv = nc.dram_tensor("wvt", [D, D], F32, kind="ExternalInput")
    dwo = nc.dram_tensor("wot", [D, D], F32, kind="ExternalInput")
    dbq = nc.dram_tensor("bq_cols", [128, 8], F32, kind="ExternalInput")
    dbk = nc.dram_tensor("bk_cols", [128, 8], F32, kind="ExternalInput")
    dbvb = nc.dram_tensor("bv_bc", [128, D], F32, kind="ExternalInput")
    dbob = nc.dram_tensor("bo_bc", [128, D], F32, kind="ExternalInput")
    dgw = nc.dram_tensor("gw_cols", [128, 64], F32, kind="ExternalInput")
    dgbb = nc.dram_tensor("gb_bc", [128, 8], F32, kind="ExternalInput")
    dw1 = nc.dram_tensor("w1", [NEXP, D, FF], F32, kind="ExternalInput")
    db1 = nc.dram_tensor("b1_cols", [128, 256], F32, kind="ExternalInput")
    dw2 = nc.dram_tensor("w2", [NEXP, FF, D], F32, kind="ExternalInput")
    db2 = nc.dram_tensor("b2", [NEXP, D], F32, kind="ExternalInput")
    did16 = nc.dram_tensor("c_id16", [128, 128], BF16, kind="ExternalInput")
    did32 = nc.dram_tensor("c_id32", [128, 128], F32, kind="ExternalInput")
    dmneg = nc.dram_tensor("c_mneg", [128, 128], F32, kind="ExternalInput")
    dmaskx = nc.dram_tensor("c_maskx", [128, 1152], F32, kind="ExternalInput")
    dktm = nc.dram_tensor("c_ktmpl", [128, S], BF16, kind="ExternalInput")
    dqtm = nc.dram_tensor("c_qtmpl", [128, S], BF16, kind="ExternalInput")

    dout = nc.dram_tensor("out", [T, D], F32, kind="ExternalOutput")
    g_scr = nc.dram_tensor("g_scr", [T, D], F32, kind="ExternalOutput" if debug else "Internal")
    xn_scr = nc.dram_tensor("xn_scr", [8, 128, T], BF16, kind="ExternalOutput" if debug else "Internal")

    if debug:
        dbg_k = nc.dram_tensor("dbg_k", [128, 8 * S], BF16, kind="ExternalOutput")
        dbg_q = nc.dram_tensor("dbg_q", [128, 8 * S], BF16, kind="ExternalOutput")
        dbg_v = nc.dram_tensor("dbg_v", [128, 16 * 1024], BF16, kind="ExternalOutput")
        dbg_y = nc.dram_tensor("dbg_y", [128, 8 * 1024], BF16, kind="ExternalOutput")

    b = Builder(nc, stack)

    id16 = stack.enter_context(nc.sbuf_tensor("id16", [128, 128], BF16))
    id32 = stack.enter_context(nc.sbuf_tensor("id32", [128, 128], F32))
    mneg = stack.enter_context(nc.sbuf_tensor("mneg", [128, 128], F32))
    ktm = stack.enter_context(nc.sbuf_tensor("ktm", [128, S], BF16))
    qtm = stack.enter_context(nc.sbuf_tensor("qtm", [128, S], BF16))
    bq8 = stack.enter_context(nc.sbuf_tensor("bq8", [128, 8], F32))
    bkc = stack.enter_context(nc.sbuf_tensor("bkc", [128, 8], F32))
    comb = stack.enter_context(nc.sbuf_tensor("comb", [128, NT * 8], F32))
    epsv = stack.enter_context(nc.sbuf_tensor("epsv", [128, 1], F32))
    b.op("vector", lambda e: e.memset(epsv[:, :], 1e-5), writes=("epsv",), name="eps")

    b.op("sync", lambda e: e.dma_start(out=id16[:, :], in_=did16[:, :]), writes=("id16",), dma=True, name="l_id16")
    b.op("sync", lambda e: e.dma_start(out=id32[:, :], in_=did32[:, :]), writes=("id32",), dma=True, name="l_id32")
    b.op("sync", lambda e: e.dma_start(out=mneg[:, :], in_=dmneg[:, :]), writes=("mneg",), dma=True, name="l_mneg")
    b.op("sync", lambda e: e.dma_start(out=ktm[:, :], in_=dktm[:, :]), writes=("ktm",), dma=True, name="l_ktm")
    b.op("sync", lambda e: e.dma_start(out=qtm[:, :], in_=dqtm[:, :]), writes=("qtm",), dma=True, name="l_qtm")
    b.op("sync", lambda e: e.dma_start(out=bkc[:, :], in_=dbk[:, :]), writes=("bkc",), dma=True, name="l_bk")
    b.op("sync", lambda e: e.dma_start(out=bq8[:, :], in_=dbq[:, :]), writes=("bq8",), dma=True, name="l_bq")
    b.op("vector", lambda e: e.tensor_scalar(bq8[:, :], bq8[:, :], 0.125, None, op0=ALU.mult),
         reads=("bq8",), writes=("bq8",), name="bq8s")

    with ExitStack() as scA:
        combT = scA.enter_context(nc.sbuf_tensor("combT", [8, NT * 128], F32))
        w1b0_pre = scA.enter_context(nc.sbuf_tensor("w1b0", [128, 4 * 4096], BF16))

        with ExitStack() as sc:
            sb0 = lambda n, s, d: sc.enter_context(nc.sbuf_tensor(n, s, d))
            ps0 = lambda n, s, d: sc.enter_context(nc.psum_tensor(n, s, d))
            xnT = sb0("xnT", [128, 8 * T], BF16)
            gw = sb0("gw", [128, 64], F32)
            gbb = sb0("gbb", [128, 8], F32)
            xb = [sb0(f"xb{i}", [128, D], F32) for i in range(2)]
            xn32 = [sb0(f"xn32_{i}", [128, D], F32) for i in range(2)]
            xn16 = [sb0(f"xn16_{i}", [128, D], BF16) for i in range(2)]
            xT32 = [sb0(f"xT32_{i}", [128, D], F32) for i in range(2)]
            junk = sb0("junk", [128, D], BF16)
            nsum = sb0("nsum", [128, 1], F32)
            negmean = sb0("negmean", [128, 1], F32)
            msq = sb0("msq", [128, 1], F32)
            var = sb0("var", [128, 1], F32)
            lnv = sb0("lnv", [128, 1], F32)
            rstd = sb0("rstd", [128, 1], F32)
            ssq = sb0("ssq", [128, 1], F32)
            logit = sb0("logit", [128, 8], F32)
            m1 = sb0("m1", [128, 1], F32)
            nm1 = sb0("nm1", [128, 1], F32)
            ew = sb0("ew", [128, 8], F32)
            ind1 = sb0("ind1", [128, 8], F32)
            wm = sb0("wm", [128, 8], F32)
            m2 = sb0("m2", [128, 1], F32)
            indt = sb0("indt", [128, 8], F32)
            cu = sb0("cu", [128, 8], F32)
            dsum = sb0("dsum", [128, 1], F32)
            rec_ = sb0("rec_", [128, 1], F32)
            pp16 = [ps0(f"pp16_{i}", [128, 1024], BF16) for i in range(2)]
            pp32 = [ps0(f"pp32_{i}", [128, 1024], F32) for i in range(2)]
            psg = ps0("psg", [128, 8], F32)
            psct = ps0("psct", [8, 512], F32)

            b.op("sync", lambda e: e.dma_start(out=gw[:, :], in_=dgw[:, :]), writes=("gw",), dma=True, name="l_gw")
            b.op("sync", lambda e: e.dma_start(out=gbb[:, :], in_=dgbb[:, :]), writes=("gbb",), dma=True, name="l_gbb")

            for i in range(NT):
                par = i % 2
                t0 = i * 128
                xbp, xn32p, xn16p, xT32p = xb[par], xn32[par], xn16[par], xT32[par]
                nx, n32, n16, nT32 = f"xb{par}", f"xn32_{par}", f"xn16_{par}", f"xT32_{par}"
                pp16p, pp32p = pp16[par], pp32[par]
                n_pp16, n_pp32 = f"pp16_{par}", f"pp32_{par}"

                b.op("sync", lambda e, d=xbp, s=t0: e.dma_start(out=d[:, :], in_=dx[s:s + 128, :]),
                     writes=(nx,), dma=True, name=f"ldx{i}")
                b.op("vector", lambda e, x=xbp: e.reduce_sum(nsum[:, :], x[:, :], axis=AX.X, negate=True),
                     reads=(nx,), writes=("nsum",), name=f"su{i}")
                b.op("scalar", lambda e, x=xbp: e.activation(junk[:, :], x[:, :], AF.Square, accum_out=ssq[:, :]),
                     reads=(nx,), writes=("junk", "ssq"), name=f"sq{i}")
                b.op("vector", lambda e: e.tensor_scalar(negmean[:, :], nsum[:, :], 1.0 / D, None, op0=ALU.mult),
                     reads=("nsum",), writes=("negmean",), name=f"nm{i}")
                b.op("vector", lambda e: e.tensor_mul(msq[:, :], negmean[:, :], negmean[:, :]),
                     reads=("negmean",), writes=("msq",), name=f"ms{i}")
                b.op("vector", lambda e: e.tensor_scalar(var[:, :], ssq[:, :], 1.0 / D, msq[:, :],
                                                         op0=ALU.mult, op1=ALU.subtract),
                     reads=("ssq", "msq"), writes=("var",), name=f"va{i}")
                b.op("scalar", lambda e: e.activation(lnv[:, :], var[:, :], AF.Ln, bias=epsv[:, :]),
                     reads=("var", "epsv"), writes=("lnv",), name=f"ln{i}")
                b.op("scalar", lambda e: e.activation(rstd[:, :], lnv[:, :], AF.Exp, scale=-0.5),
                     reads=("lnv",), writes=("rstd",), name=f"rs{i}")
                b.op("vector", lambda e, x=xbp, o=xn32p: e.tensor_scalar(
                        o[:, :], x[:, :], negmean[:, :], rstd[:, :], op0=ALU.add, op1=ALU.mult),
                     reads=(nx, "negmean", "rstd"), writes=(n32,), name=f"xn{i}")
                b.op("vector", lambda e, s=xn32p, o=xn16p: e.tensor_copy(o[:, :], s[:, :]),
                     reads=(n32,), writes=(n16,), name=f"cst{i}")

                def tp16(e, src=xn16p, dst=pp16p):
                    inst = None
                    for c in range(8):
                        inst = e.matmul(dst[:, c * 128:(c + 1) * 128], src[:, c * 128:(c + 1) * 128],
                                        id16[:, :], start=(c == 0), stop=(c == 7), is_transpose=True)
                    return inst
                b.op("tensor", tp16, reads=(n16, "id16"), writes=(n_pp16,), name=f"t16_{i}")

                def tp32(e, src=xn32p, dst=pp32p):
                    inst = None
                    for c in range(8):
                        inst = e.matmul(dst[:, c * 128:(c + 1) * 128], src[:, c * 128:(c + 1) * 128],
                                        id32[:, :], start=(c in (0, 4)), stop=(c in (3, 7)), is_transpose=True)
                    return inst
                b.op("tensor", tp32, reads=(n32, "id32"), writes=(n_pp32,), name=f"t32_{i}")

                def ev16(e, src=pp16p, t0=t0):
                    dst = bass.AP(xnT, t0, [[8 * T, 128], [T, 8], [1, 128]])
                    return e.tensor_copy(dst, src[:, :])
                b.op("vector", ev16, reads=(n_pp16,), writes=("xnT",), name=f"e16_{i}")
                b.op("vector", lambda e, src=pp32p, dst=xT32p: e.tensor_copy(dst[:, :], src[:, :]),
                     reads=(n_pp32,), writes=(nT32,), name=f"e32_{i}")

                def gmm(e, src=xT32p):
                    inst = None
                    for c in range(8):
                        inst = e.matmul(psg[:, :], src[:, c * 128:(c + 1) * 128],
                                        gw[:, c * 8:(c + 1) * 8], start=(c == 0), stop=(c == 7))
                    return inst
                b.op("tensor", gmm, reads=(nT32, "gw"), writes=("psg",), name=f"gm{i}")
                b.op("vector", lambda e: e.tensor_add(logit[:, :], psg[:, :], gbb[:, :]),
                     reads=("psg", "gbb"), writes=("logit",), name=f"lg{i}")
                b.op("vector", lambda e: e.reduce_max(m1[:, :], logit[:, :], axis=AX.X),
                     reads=("logit",), writes=("m1",), name=f"m1_{i}")
                b.op("vector", lambda e: e.tensor_scalar(nm1[:, :], m1[:, :], -1.0, None, op0=ALU.mult),
                     reads=("m1",), writes=("nm1",), name=f"nm1_{i}")
                b.op("scalar", lambda e: e.activation(ew[:, :], logit[:, :], AF.Exp, bias=nm1[:, :]),
                     reads=("logit", "nm1"), writes=("ew",), name=f"ew{i}")
                b.op("vector", lambda e: e.tensor_scalar(ind1[:, :], logit[:, :], m1[:, :], None, op0=ALU.is_ge),
                     reads=("logit", "m1"), writes=("ind1",), name=f"i1_{i}")
                b.op("vector", lambda e: e.tensor_scalar(ind1[:, :], ind1[:, :], -1.0, 1.0,
                                                         op0=ALU.mult, op1=ALU.add),
                     reads=("ind1",), writes=("ind1",), name=f"ni{i}")
                b.op("vector", lambda e: e.tensor_mul(wm[:, :], ew[:, :], ind1[:, :]),
                     reads=("ew", "ind1"), writes=("wm",), name=f"wm{i}")
                b.op("vector", lambda e: e.reduce_max(m2[:, :], wm[:, :], axis=AX.X),
                     reads=("wm",), writes=("m2",), name=f"m2_{i}")
                b.op("vector", lambda e: e.tensor_scalar(indt[:, :], ew[:, :], m2[:, :], None, op0=ALU.is_ge),
                     reads=("ew", "m2"), writes=("indt",), name=f"it{i}")
                b.op("vector", lambda e: e.tensor_mul(cu[:, :], ew[:, :], indt[:, :]),
                     reads=("ew", "indt"), writes=("cu",), name=f"cu{i}")
                b.op("vector", lambda e: e.reduce_sum(dsum[:, :], cu[:, :], axis=AX.X),
                     reads=("cu",), writes=("dsum",), name=f"ds{i}")
                b.op("vector", lambda e: e.tensor_scalar(dsum[:, :], dsum[:, :], 1e-6, None, op0=ALU.max),
                     reads=("dsum",), writes=("dsum",), name=f"dm{i}")
                b.op("vector", lambda e: e.reciprocal(rec_[:, :], dsum[:, :]),
                     reads=("dsum",), writes=("rec_",), name=f"rc{i}")
                b.op("vector", lambda e, i=i: e.tensor_scalar(comb[:, i * 8:(i + 1) * 8], cu[:, :],
                                                              rec_[:, :], None, op0=ALU.mult),
                     reads=("cu", "rec_"), writes=("comb",), name=f"cb{i}")

                def ctp(e, i=i):
                    return e.matmul(psct[0:8, (i % 4) * 128:(i % 4) * 128 + 128],
                                    comb[:, i * 8:(i + 1) * 8], id32[:, :],
                                    start=(i % 4 == 0), stop=(i % 4 == 3), is_transpose=True)
                b.op("tensor", ctp, reads=("comb", "id32"), writes=("psct",), name=f"ct{i}")
                if i % 4 == 3:
                    b.op("vector", lambda e, i=i: e.tensor_copy(combT[0:8, (i - 3) * 128:(i + 1) * 128],
                                                                psct[0:8, :]),
                         reads=("psct",), writes=("combT",), name=f"ce{i}")

            for c in range(8):
                b.op("sync", lambda e, c=c: e.dma_start(out=xn_scr[c, :, :], in_=xnT[:, c * T:(c + 1) * T]),
                     reads=("xnT",), writes=("xn_scr",), dma=True, name=f"sp{c}")

            for fc in range(4):
                w1src = dw1[0, :, fc * 512:(fc + 1) * 512].rearrange("(c p) f -> p c f", p=128)
                w1dst = bass.AP(w1b0_pre, fc * 4096, [[4 * 4096, 128], [512, 8], [1, 512]])
                b.op("gpsimd", lambda e, d=w1dst, s=w1src: e.dma_start(out=d, in_=s),
                     writes=("w1b0",), dma=True, name=f"w1l0_{fc}")

        b.barrier()

        with ExitStack() as sc:
            sb1 = lambda n, s, d: sc.enter_context(nc.sbuf_tensor(n, s, d))
            ps1 = lambda n, s, d: sc.enter_context(nc.psum_tensor(n, s, d))
            w1b = [w1b0_pre, sb1("w1b1", [128, 4 * 4096], BF16)]
            w2h = sb1("w2h", [128, 16 * 1024], BF16)
            hdnT = [sb1(f"hdnT{i}", [128, 16 * 512], BF16) for i in range(2)]
            xnb1 = [sb1(f"xnb1_{i}", [128, 8 * 512], BF16) for i in range(2)]
            b1c = sb1("b1c", [128, 256], F32)
            b2sb = sb1("b2sb", [8, D], F32)
            gout = [sb1(f"gout{i}", [128, D], F32) for i in range(2)]
            psa = [ps1(f"psa{i}", [128, 512], F32) for i in range(2)]
            psb = [ps1(f"psb{i}", [128, 1024], F32) for i in range(2)]

            b.op("sync", lambda e: e.dma_start(out=b1c[:, :], in_=db1[:, :]), writes=("b1c",), dma=True, name="l_b1")
            b.op("sync", lambda e: e.dma_start(out=b2sb[:, :], in_=db2[:, :]), writes=("b2sb",), dma=True, name="l_b2")

            for gt in range(NT):
                go, ngo = gout[gt % 2], f"gout{gt % 2}"
                pb, npb = psb[gt % 2], f"psb{gt % 2}"

                def b2mm(e, gt=gt, pb=pb):
                    e.matmul(pb[:, 0:512], combT[0:8, gt * 128:(gt + 1) * 128],
                             b2sb[0:8, 0:512], start=True, stop=True)
                    return e.matmul(pb[:, 512:1024], combT[0:8, gt * 128:(gt + 1) * 128],
                                    b2sb[0:8, 512:1024], start=True, stop=True)
                b.op("tensor", b2mm, reads=("combT", "b2sb"), writes=(npb,), name=f"b2m{gt}")
                b.op("vector", lambda e, pb=pb, go=go: e.tensor_copy(go[:, :], pb[:, :]),
                     reads=(npb,), writes=(ngo,), name=f"b2c{gt}")
                b.op("gpsimd", lambda e, go=go, gt=gt: e.dma_start(
                        out=g_scr[gt * 128:(gt + 1) * 128, :], in_=go[:, :]),
                     reads=(ngo,), writes=(f"g{gt}",), dma=True, name=f"b2s{gt}")

            def load_w1(rnd):
                ex, fh = rnd // 2, rnd % 2
                wp, nwp = w1b[rnd % 2], f"w1b{rnd % 2}"
                for fc in range(4):
                    src = dw1[ex, :, fh * 2048 + fc * 512: fh * 2048 + (fc + 1) * 512] \
                        .rearrange("(c p) f -> p c f", p=128)
                    dst = bass.AP(wp, fc * 4096, [[4 * 4096, 128], [512, 8], [1, 512]])
                    b.op("gpsimd", lambda e, d=dst, s=src: e.dma_start(out=d, in_=s),
                         writes=(nwp,), dma=True, name=f"w1l{rnd}_{fc}")

            for rnd in range(16):
                ex, fh = rnd // 2, rnd % 2
                w1p, nw1 = w1b[rnd % 2], f"w1b{rnd % 2}"
                for fc in range(4):
                    src = dw2[ex, fh * 2048 + fc * 512: fh * 2048 + (fc + 1) * 512, :] \
                        .rearrange("(c p) h -> p c h", p=128)
                    dst = bass.AP(w2h, fc * 4 * 1024, [[16 * 1024, 128], [1024, 4], [1, 1024]])
                    b.op("gpsimd", lambda e, d=dst, s=src: e.dma_start(out=d, in_=s),
                         writes=("w2h",), dma=True, name=f"w2l{rnd}_{fc}")
                if rnd + 1 < 16:
                    load_w1(rnd + 1)

                for tg in range(8):
                    tpos = tg * 512
                    xp, nxp = xnb1[tg % 2], f"xnb1_{tg % 2}"
                    src = xn_scr[:, :, tpos:tpos + 512].rearrange("c p t -> p c t")
                    b.op("sync", lambda e, d=xp, s=src: e.dma_start(
                            out=d.ap().rearrange("p (c t) -> p c t", c=8), in_=s),
                         reads=("xn_scr",), writes=(nxp,), dma=True, name=f"xl{rnd}_{tg}")

                    for fb in range(16):
                        fc, fs = fb // 4, fb % 4
                        pa, npa = psa[fb % 2], f"psa{fb % 2}"

                        def amm(e, w1p=w1p, fc=fc, fs=fs, pa=pa, xp=xp):
                            inst = None
                            for dc in range(8):
                                inst = e.matmul(
                                    pa[:, :],
                                    w1p[:, fc * 4096 + dc * 512 + fs * 128: fc * 4096 + dc * 512 + fs * 128 + 128],
                                    xp[:, dc * 512:(dc + 1) * 512],
                                    start=(dc == 0), stop=(dc == 7))
                            return inst
                        b.op("tensor", amm, reads=(nw1, nxp), writes=(npa,), name=f"am{rnd}_{tg}_{fb}")

                        bcol = ex * 32 + fh * 16 + fb
                        hb, nhb = hdnT[tg % 2], f"hdnT{tg % 2}"
                        b.op("scalar", lambda e, pa=pa, fb=fb, bcol=bcol, hb=hb: e.activation(
                                hb[:, fb * 512:(fb + 1) * 512], pa[:, :], AF.Gelu,
                                bias=b1c[:, bcol:bcol + 1]),
                             reads=(npa, "b1c"), writes=(nhb,), name=f"ge{rnd}_{tg}_{fb}")

                    for tl in range(4):
                        gt = tg * 4 + tl
                        pb, npb = psb[tl % 2], f"psb{tl % 2}"
                        go, ngo = gout[tl % 2], f"gout{tl % 2}"

                        hb, nhb = hdnT[tg % 2], f"hdnT{tg % 2}"
                        def bmm(e, tl=tl, pb=pb, hb=hb):
                            inst = None
                            for fb in range(16):
                                lhs = hb[:, fb * 512 + tl * 128: fb * 512 + tl * 128 + 128]
                                inst = e.matmul(pb[:, 0:512], lhs, w2h[:, fb * 1024: fb * 1024 + 512],
                                                start=(fb == 0), stop=(fb == 15))
                                inst = e.matmul(pb[:, 512:1024], lhs, w2h[:, fb * 1024 + 512: fb * 1024 + 1024],
                                                start=(fb == 0), stop=(fb == 15))
                            return inst
                        b.op("tensor", bmm, reads=(nhb, "w2h"), writes=(npb,), name=f"bm{rnd}_{gt}")
                        b.op("vector", lambda e, pb=pb, go=go, gt=gt, ex=ex: e.tensor_scalar(
                                go[:, :], pb[:, :], comb[:, gt * 8 + ex: gt * 8 + ex + 1], None, op0=ALU.mult),
                             reads=(npb, "comb"), writes=(ngo,), name=f"be{rnd}_{gt}")
                        b.op("gpsimd", lambda e, go=go, gt=gt: e.dma_start(
                                out=g_scr[gt * 128:(gt + 1) * 128, :], in_=go[:, :], accum_op=ALU.add),
                             reads=(ngo,), writes=(f"g{gt}",), dma=True, name=f"ga{rnd}_{gt}")

        b.barrier()

    with ExitStack() as sc:
        sb2 = lambda n, s, d: sc.enter_context(nc.sbuf_tensor(n, s, d))
        kT = sb2("kT", [128, 8 * S], BF16)
        qT2 = sb2("qT2", [128, 8 * S], BF16)
        v_tm = sb2("v_tm", [128, 16 * 1024], BF16)
        yT = sb2("yT", [128, 8 * 1024], BF16)
        bob = sb2("bob", [128, D], F32)
        lcur = [sb2(f"lcur{i}", [128, 1], F32) for i in range(2)]
        rl2 = [sb2(f"rl2_{i}", [128, 1], F32) for i in range(2)]
        mstat = [sb2(f"mstat{i}", [128, 3], F32) for i in range(2)]
        lc2 = [sb2(f"lc2_{i}", [128, 2], F32) for i in range(2)]
        negm = [sb2(f"negm{i}", [128, 1], F32) for i in range(2)]
        stmp = sb2("stmp", [128, D], F32)
        sgb = sb2("sgb", [128, D], F32)

        b.op("sync", lambda e: e.dma_start(out=bob[:, :], in_=dbob[:, :]), writes=("bob",), dma=True, name="l_bob")

        for sq in range(SEQS):
            soff = sq * S
            with ExitStack() as sp:
                sbp = lambda n, s, d: sp.enter_context(nc.sbuf_tensor(n, s, d))
                psp = lambda n, s, d: sp.enter_context(nc.psum_tensor(n, s, d))
                wkb = sbp(f"wkb{sq}", [128, 8 * 1024], BF16)
                wqb = sbp(f"wqb{sq}", [128, 8 * 1024], BF16)
                wvb = sbp(f"wvb{sq}", [128, 8 * 1024], BF16)
                xnb = [sbp(f"xnb2_{sq}_{i}", [128, 8 * 512], BF16) for i in range(2)]
                bvb = sbp(f"bvb{sq}", [128, D], F32)
                pko = [psp(f"pko{sq}_{i}", [128, 512], F32) for i in range(2)]
                b.op("sync", lambda e, bvb=bvb: e.dma_start(out=bvb[:, :], in_=dbvb[:, :]),
                     writes=(f"bvb{sq}",), dma=True, name=f"l_bvb{sq}")
                psv = psp(f"psv{sq}", [128, 1024], F32)

                for wdst, wsrc, wname in ((wkb, dwk, f"wkb{sq}"), (wqb, dwq, f"wqb{sq}"), (wvb, dwv, f"wvb{sq}")):
                    src = wsrc[:, :].rearrange("(c p) i -> p c i", p=128)
                    b.op("gpsimd", lambda e, dd=wdst, s=src: e.dma_start(
                            out=dd.ap().rearrange("p (c i) -> p c i", c=8), in_=s),
                         writes=(wname,), dma=True, name=f"lw_{wname}")

                for tc in range(4):
                    xp, nxp = xnb[tc % 2], f"xnb2_{sq}_{tc % 2}"
                    src = xn_scr[:, :, soff + tc * 512: soff + (tc + 1) * 512].rearrange("c p t -> p c t")
                    b.op("sync", lambda e, d=xp, s=src: e.dma_start(
                            out=d.ap().rearrange("p (c t) -> p c t", c=8), in_=s),
                         reads=("xn_scr",), writes=(nxp,), dma=True, name=f"x2{sq}_{tc}")

                    for proj, wb, nwb, dstT, ndT, bias, scale in (
                            ("k", wkb, f"wkb{sq}", kT, "kT", bkc, 1.0),
                            ("q", wqb, f"wqb{sq}", qT2, "qT2", bq8, 0.125)):
                        for ic in range(8):
                            pk, npk = pko[ic % 2], f"pko{sq}_{ic % 2}"

                            def pmm(e, xp=xp, ic=ic, pk=pk, wb=wb):
                                inst = None
                                for dc in range(8):
                                    inst = e.matmul(pk[:, :],
                                                    wb[:, dc * 1024 + ic * 128: dc * 1024 + ic * 128 + 128],
                                                    xp[:, dc * 512:(dc + 1) * 512],
                                                    start=(dc == 0), stop=(dc == 7))
                                return inst
                            b.op("tensor", pmm, reads=(nwb, nxp), writes=(npk,), name=f"{proj}m{sq}_{tc}_{ic}")
                            b.op("scalar", lambda e, pk=pk, ic=ic, tc=tc, dstT=dstT, bias=bias, scale=scale:
                                    e.activation(dstT[:, ic * S + tc * 512: ic * S + (tc + 1) * 512], pk[:, :],
                                                 AF.Identity, bias=bias[:, ic:ic + 1], scale=scale),
                                 reads=(npk, "bkc", "bq8"), writes=(ndT,), name=f"{proj}e{sq}_{tc}_{ic}")

                    for tl in range(4):
                        tj = tc * 4 + tl

                        def vmm(e, xp=xp, tl=tl, wb=wvb):
                            inst = None
                            for dc in range(8):
                                lhs = xp[:, dc * 512 + tl * 128: dc * 512 + tl * 128 + 128]
                                inst = e.matmul(psv[:, 0:512], lhs, wb[:, dc * 1024: dc * 1024 + 512],
                                                start=(dc == 0), stop=(dc == 7))
                                inst = e.matmul(psv[:, 512:1024], lhs, wb[:, dc * 1024 + 512: dc * 1024 + 1024],
                                                start=(dc == 0), stop=(dc == 7))
                            return inst
                        b.op("tensor", vmm, reads=(f"wvb{sq}", nxp), writes=(f"psv{sq}",), name=f"vm{sq}_{tc}_{tl}")
                        b.op("vector", lambda e, tj=tj: e.scalar_tensor_tensor(
                                v_tm[:, tj * 1024:(tj + 1) * 1024], psv[:, :], 1.0, bvb[:, :],
                                op0=ALU.mult, op1=ALU.add),
                             reads=(f"psv{sq}", f"bvb{sq}"), writes=("v_tm",), name=f"ve{sq}_{tc}_{tl}")

            if debug and sq == 0:
                b.op("sync", lambda e: e.dma_start(out=dbg_k[:, :], in_=kT[:, :]),
                     reads=("kT",), dma=True, name="dbgk")
                b.op("sync", lambda e: e.dma_start(out=dbg_q[:, :], in_=qT2[:, :]),
                     reads=("qT2",), dma=True, name="dbgq")
                b.op("sync", lambda e: e.dma_start(out=dbg_v[:, :], in_=v_tm[:, :]),
                     reads=("v_tm",), dma=True, name="dbgv")
            b.barrier()

            with ExitStack() as sp:
                sbp = lambda n, s, d: sp.enter_context(nc.sbuf_tensor(n, s, d))
                psp = lambda n, s, d: sp.enter_context(nc.psum_tensor(n, s, d))
                wob = sbp(f"wob{sq}", [128, 8 * 1024], BF16)
                kqa = [sbp(f"kqa{sq}_{i}", [72, S], BF16) for i in range(2)]
                qaug = [sbp(f"qaug{sq}_{i}", [72, 1024], BF16) for i in range(2)]
                Ebuf = [sbp(f"Eb{sq}_{i}", [128, S], BF16) for i in range(3)]
                ETb = [sbp(f"ET{sq}_{i}", [128, 16 * 128], BF16) for i in range(2)]
                sbs = [sbp(f"sbs{sq}_{i}", [128, S], F32) for i in range(2)]
                gti = [sbp(f"gti{sq}_0", [128, D], F32)] * 2
                xti = [sbp(f"xti{sq}_0", [128, D], F32)] * 2
                pssp = [psp(f"pss{sq}_{i}", [128, 1024], F32) for i in range(2)]
                pset = psp(f"pset{sq}", [128, 1024], BF16)
                psy = psp(f"psy{sq}", [64, 128], F32)
                pso = psp(f"pso{sq}", [128, 1024], F32)

                src = dwo[:, :].rearrange("(c p) i -> p c i", p=128)
                b.op("gpsimd", lambda e, s=src: e.dma_start(
                        out=wob.ap().rearrange("p (c i) -> p c i", c=8), in_=s),
                     writes=(f"wob{sq}",), dma=True, name=f"lwo{sq}")

                gc = [0]
                for half in range(2):
                    stq = []
                    def flush_one():
                        d2, t1, t2 = stq.pop(0)
                        d2(); t1()
                        stq.append((None, None, t2))
                    for h in range(H):
                        hp = h % 2
                        kq, nkq = kqa[hp], f"kqa{sq}_{hp}"
                        qa, nqa = qaug[hp], f"qaug{sq}_{hp}"
                        b.op("sync", lambda e, kq=kq, h=h: e.dma_start(
                                out=kq[0:64, :],
                                in_=kT[(h % 2) * 64:(h % 2) * 64 + 64, (h // 2) * S:(h // 2 + 1) * S]),
                             reads=("kT",), writes=(nkq,), dma=True, name=f"bk{sq}_{half}_{h}")
                        b.op("sync", lambda e, kq=kq, h=h: e.dma_start(
                                out=kq[64:72, :], in_=ktm[8 * h:8 * h + 8, :]),
                             reads=("ktm",), writes=(nkq,), dma=True, name=f"bkt{sq}_{half}_{h}")
                        b.op("sync", lambda e, qa=qa, h=h, half=half: e.dma_start(
                                out=qa[0:64, :],
                                in_=qT2[(h % 2) * 64:(h % 2) * 64 + 64,
                                        (h // 2) * S + half * 1024:(h // 2) * S + half * 1024 + 1024]),
                             reads=("qT2",), writes=(nqa,), dma=True, name=f"bq{sq}_{half}_{h}")
                        b.op("sync", lambda e, qa=qa, h=h, half=half: e.dma_start(
                                out=qa[64:72, :], in_=qtm[8 * h:8 * h + 8, half * 1024: half * 1024 + 1024]),
                             reads=("qtm",), writes=(nqa,), dma=True, name=f"bqt{sq}_{half}_{h}")

                        for A in range(8):
                            Ag = half * 8 + A
                            W = (Ag + 1) * 128
                            ep_i = gc[0] % 3
                            et_i = gc[0] % 2
                            spar = A % 2
                            gc[0] += 1
                            Ep, nEp = Ebuf[ep_i], f"Eb{sq}_{ep_i}"
                            ETp, nETp = ETb[et_i], f"ET{sq}_{et_i}"
                            sbp_ = sbs[spar]
                            nsbs = f"sbs{sq}_{spar}"
                            nneg = f"negm{sq}_{spar}"
                            nms = f"mstat{sq}_{spar}"
                            tag = f"{sq}_{half}_{h}_{A}"
                            if Ag <= 7:
                                pieces = [(A % 2, 0, W)]
                            else:
                                pieces = [(A % 2, 0, 1024), ((A + 1) % 2, 1024, W)]
                            n512 = Ag // 4
                            rem = Ag % 4
                            chunks = [(c * 512, (c + 1) * 512, True, True) for c in range(n512)]
                            if rem:
                                chunks.append((n512 * 512, n512 * 512 + rem * 128, True, False))
                                chunks.append((W - 128, W, False, True))
                            else:
                                chunks.append((W - 128, W, True, True))

                            piece_info = []
                            for pi, (pb, lo, hi) in enumerate(pieces):
                                ps_ = pssp[pb]
                                nps = f"pss{sq}_{pb}"

                                def smm(e, qa=qa, kq=kq, A=A, ps_=ps_, lo=lo, hi=hi, chunks=tuple(chunks)):
                                    inst = None
                                    lhs = qa[0:72, A * 128:(A + 1) * 128]
                                    for gl, gh, st, sp in chunks:
                                        if gl < lo or gl >= hi:
                                            continue
                                        inst = e.matmul(ps_[:, gl - lo:gh - lo], lhs,
                                                        kq[0:72, gl:gh], start=st, stop=sp)
                                    return inst
                                b.op("tensor", smm, reads=(nqa, nkq), writes=(nps,), name=f"s{tag}_{pi}")
                                piece_info.append((pi, pb, lo, hi, ps_, nps))

                            mains = 0
                            for pi, pb, lo, hi, ps_, nps in piece_info:
                                hid = min(hi, W - 128)
                                if hid > lo:
                                    mc = mains
                                    mains += 1
                                    b.op("vector", lambda e, ps_=ps_, lo=lo, hid=hid, mc=mc, sbp_=sbp_,
                                                spar=spar: e.tensor_scalar(
                                            sbp_[:, lo:hid], ps_[:, 0:hid - lo], 1.0, None,
                                            op0=ALU.mult, op1=ALU.max,
                                            accum_out=mstat[spar][:, mc:mc + 1]),
                                         reads=(nps,), writes=(nsbs, nms), name=f"e{tag}_{pi}")
                                if hi == W:
                                    dc = mains
                                    b.op("vector", lambda e, ps_=ps_, lo=lo, W=W, sbp_=sbp_: e.scalar_tensor_tensor(
                                            sbp_[:, W - 128:W], ps_[:, W - 128 - lo:W - lo], 1.0, mneg[:, :],
                                            op0=ALU.mult, op1=ALU.add),
                                         reads=(nps, "mneg"), writes=(nsbs,), name=f"d{tag}")
                                    b.op("vector", lambda e, W=W, sbp_=sbp_, spar=spar, dc=dc: e.tensor_scalar(
                                            sbp_[:, W - 128:W], sbp_[:, W - 128:W], 1.0, None,
                                            op0=ALU.mult, op1=ALU.max,
                                            accum_out=mstat[spar][:, dc:dc + 1]),
                                         reads=(nsbs,), writes=(nsbs, nms), name=f"dm{tag}")
                            ncols = mains + 1
                            b.op("vector", lambda e, spar=spar, ncols=ncols: e.reduce_max(
                                    negm[spar][:, :], mstat[spar][:, 0:ncols], axis=AX.X, negate=True),
                                 reads=(nms,), writes=(nneg,), name=f"rm{tag}")
                            b.op("scalar", lambda e, Ep=Ep, W=W, sbp_=sbp_, spar=spar: e.activation(
                                    Ep[:, 0:W], sbp_[:, 0:W], AF.Exp, bias=negm[spar][:, :],
                                    accum_out=lcur[spar][:, :]),
                                 reads=(nsbs, nneg), writes=(nEp, f"lcur{sq}_{spar}"), name=f"x{tag}")

                            if len(stq) > 0:
                                mid = None
                                if stq and stq[0][0] is None:
                                    _, _, mid = stq.pop(0)
                                if stq:
                                    d2p, t1p, t2p = stq.pop(0)
                                    d2p()
                                    t1p(mid)
                                    stq.insert(0, (None, None, t2p))
                                elif mid is not None:
                                    mid()

                            def mk_d2(Ep=Ep, nEp=nEp, W=W, spar=spar, tag=tag):
                                def d2():
                                    b.op("vector", lambda e: e.reciprocal(rl2[spar][:, :], lcur[spar][:, :]),
                                         reads=(f"lcur{sq}_{spar}",), writes=(f"rl2_{sq}_{spar}",),
                                         name=f"rl{tag}")
                                    b.op("vector", lambda e: e.tensor_scalar(
                                            Ep[:, 0:W], Ep[:, 0:W], rl2[spar][:, :], None, op0=ALU.mult),
                                         reads=(nEp, f"rl2_{sq}_{spar}"), writes=(nEp,), name=f"en{tag}")
                                return d2

                            def mk_t1(Ep=Ep, nEp=nEp, ETp=ETp, nETp=nETp, Ag=Ag, tag=tag):
                                def t1(mid=None):
                                    ngroups = (Ag + 8) // 8
                                    for g in range(ngroups):
                                        if g == 1 and mid is not None:
                                            mid()
                                            mid = None
                                        j0, j1 = g * 8, min(g * 8 + 8, Ag + 1)

                                        def tpe(e, Ep=Ep, j0=j0, j1=j1):
                                            inst = None
                                            for j in range(j0, j1):
                                                inst = e.matmul(pset[:, (j - j0) * 128:(j - j0 + 1) * 128],
                                                                Ep[:, j * 128:(j + 1) * 128], id16[:, :],
                                                                start=(j == j0), stop=(j == j1 - 1),
                                                                is_transpose=True)
                                            return inst
                                        b.op("tensor", tpe, reads=(nEp, "id16"), writes=("pset",),
                                             name=f"t{tag}_{g}")
                                        b.op("scalar", lambda e, ETp=ETp, j0=j0, j1=j1: e.copy(
                                                ETp[:, j0 * 128:j1 * 128], pset[:, 0:(j1 - j0) * 128]),
                                             reads=("pset",), writes=(nETp,), name=f"u{tag}_{g}")
                                    if mid is not None:
                                        mid()
                                return t1

                            def mk_t2(ETp=ETp, nETp=nETp, h=h, A=A, Ag=Ag, tag=tag):
                                def t2():
                                    def avm(e, ETp=ETp, h=h, Ag=Ag):
                                        inst = None
                                        for jj in range(Ag + 1):
                                            inst = e.matmul(psy[0:64, :],
                                                            v_tm[:, jj * 1024 + h * 64: jj * 1024 + h * 64 + 64],
                                                            ETp[:, jj * 128:(jj + 1) * 128],
                                                            start=(jj == 0), stop=(jj == Ag))
                                        return inst
                                    b.op("tensor", avm, reads=(nETp, "v_tm"), writes=("psy",), name=f"a{tag}")
                                    b.op("vector", lambda e, h=h, A=A: e.tensor_copy(
                                            yT[(h % 2) * 64:(h % 2) * 64 + 64,
                                               (h // 2) * 1024 + A * 128:(h // 2) * 1024 + A * 128 + 128],
                                            psy[0:64, :]),
                                         reads=("psy",), writes=("yT",), name=f"y{tag}")
                                return t2

                            stq.append((mk_d2(), mk_t1(), mk_t2()))
                    while stq:
                        d2, t1, t2 = stq.pop(0)
                        if d2 is not None:
                            d2(); t1()
                        t2()

                    if debug and sq == 0 and half == 0:
                        b.op("sync", lambda e: e.dma_start(out=dbg_y[:, :], in_=yT[:, :]),
                             reads=("yT",), dma=True, name="dbgy")
                    for tl in range(8):
                        t0 = soff + half * 1024 + tl * 128
                        gp, ngp = gti[0], f"gti{sq}_0"
                        xp2, nxp2 = xti[0], f"xti{sq}_0"
                        ob, nob = xp2, nxp2
                        gt_glob = t0 // 128
                        b.op("sync", lambda e, gp=gp, t0=t0: e.dma_start(out=gp[:, :], in_=g_scr[t0:t0 + 128, :]),
                             reads=(f"g{gt_glob}",), writes=(ngp,), dma=True, name=f"lg{sq}_{half}_{tl}")
                        b.op("sync", lambda e, xp2=xp2, t0=t0: e.dma_start(out=xp2[:, :], in_=dx[t0:t0 + 128, :]),
                             writes=(nxp2,), dma=True, name=f"lx{sq}_{half}_{tl}")

                        def omm(e, tl=tl):
                            inst = None
                            for ic in range(8):
                                lhs = yT[:, ic * 1024 + tl * 128: ic * 1024 + tl * 128 + 128]
                                inst = e.matmul(pso[:, 0:512], lhs, wob[:, ic * 1024: ic * 1024 + 512],
                                                start=(ic == 0), stop=(ic == 7))
                                inst = e.matmul(pso[:, 512:1024], lhs, wob[:, ic * 1024 + 512: ic * 1024 + 1024],
                                                start=(ic == 0), stop=(ic == 7))
                            return inst
                        b.op("tensor", omm, reads=("yT", f"wob{sq}"), writes=(f"pso{sq}",), name=f"o{sq}_{half}_{tl}")
                        b.op("vector", lambda e: e.scalar_tensor_tensor(
                                stmp[:, :], pso[:, :], 1.0, bob[:, :], op0=ALU.mult, op1=ALU.add),
                             reads=(f"pso{sq}", "bob"), writes=("stmp",), name=f"se{sq}_{half}_{tl}")
                        b.op("scalar", lambda e, gp=gp: e.activation(gp[:, :], gp[:, :], AF.Silu),
                             reads=(ngp,), writes=(ngp,), name=f"si{sq}_{half}_{tl}")
                        b.op("vector", lambda e, gp=gp: e.tensor_mul(sgb[:, :], stmp[:, :], gp[:, :]),
                             reads=("stmp", ngp), writes=("sgb",), name=f"sg{sq}_{half}_{tl}")
                        b.op("vector", lambda e, xp2=xp2: e.tensor_add(xp2[:, :], sgb[:, :], xp2[:, :]),
                             reads=("sgb", nxp2), writes=(nxp2,), name=f"oa{sq}_{half}_{tl}")
                        b.op("sync", lambda e, ob=ob, t0=t0: e.dma_start(out=dout[t0:t0 + 128, :], in_=ob[:, :]),
                             reads=(nob,), writes=("out",), dma=True, name=f"so{sq}_{half}_{tl}")

            if sq + 1 < SEQS:
                b.barrier()

        final_out = b.writer["out"]["sync"]
        b.op("sync", lambda e: e.nop(), after=(final_out,), name="final")

    b.finalize_and_emit()
    stack.close()
    return nc


_cached = {}


def _get_program():
    if "nc" not in _cached:
        _cached["nc"] = build_program()
    return _cached["nc"]


def kernel(L0_attn_x, L0_attn_Wq, L0_attn_bq, L0_attn_Wk, L0_attn_bk,
           L0_attn_Wv, L0_attn_bv, L0_attn_Wo, L0_attn_bo,
           L0_ffn_gate_w, L0_ffn_gate_b, L0_ffn_w1, L0_ffn_b1,
           L0_ffn_w2, L0_ffn_b2):
    f32 = lambda a: np.ascontiguousarray(np.asarray(a, dtype=np.float32))
    x = f32(L0_attn_x)
    Wq, Wk, Wv, Wo = map(f32, (L0_attn_Wq, L0_attn_Wk, L0_attn_Wv, L0_attn_Wo))
    bq, bk, bv, bo = map(f32, (L0_attn_bq, L0_attn_bk, L0_attn_bv, L0_attn_bo))
    gw, gb = f32(L0_ffn_gate_w), f32(L0_ffn_gate_b)
    w1, b1, w2, b2 = map(f32, (L0_ffn_w1, L0_ffn_b1, L0_ffn_w2, L0_ffn_b2))

    id16, id32, mneg, maskx, ktmpl, qtmpl = _make_consts()
    cols = lambda v: np.ascontiguousarray(v.reshape(8, 128).T)
    common = dict(
        c_maskx=maskx,
        wqt=np.ascontiguousarray(Wq.T), wkt=np.ascontiguousarray(Wk.T),
        wvt=np.ascontiguousarray(Wv.T), wot=np.ascontiguousarray(Wo.T),
        bq_cols=cols(bq), bk_cols=cols(bk),
        bv_bc=np.ascontiguousarray(np.broadcast_to(bv, (128, D))),
        bo_bc=np.ascontiguousarray(np.broadcast_to(bo, (128, D))),
        gw_cols=np.ascontiguousarray(gw.reshape(8, 128, 8).transpose(1, 0, 2).reshape(128, 64)),
        gb_bc=np.ascontiguousarray(np.broadcast_to(gb, (128, 8))),
        w1=w1, w2=w2,
        b1_cols=np.ascontiguousarray(b1.reshape(8, 32, 128).transpose(2, 0, 1).reshape(128, 256)),
        b2=b2,
        c_id16=id16, c_id32=id32, c_mneg=mneg, c_ktmpl=ktmpl, c_qtmpl=qtmpl,
    )
    in_maps = []
    for core in range(8):
        m = dict(common)
        m["x"] = np.ascontiguousarray(x[core * SEQS:(core + 1) * SEQS].reshape(T, D))
        in_maps.append(m)

    nc = _get_program()
    res = run_bass_kernel_spmd(nc, in_maps, core_ids=list(range(8)))
    out = np.concatenate([r["out"].reshape(SEQS, S, D) for r in res.results], axis=0)
    return out


# version 10
# speedup vs baseline: 1.8384x; 1.0006x over previous
import math
from contextlib import ExitStack

import numpy as np
import ml_dtypes

import concourse.bass as bass
import concourse.mybir as mybir
from concourse.bass_utils import run_bass_kernel_spmd

F32 = mybir.dt.float32
BF16 = mybir.dt.bfloat16
AF = mybir.ActivationFunctionType
ALU = mybir.AluOpType
AX = mybir.AxisListType
BF = ml_dtypes.bfloat16

D = 1024
H = 16
DH = 64
S = 2048
NEXP = 8
FF = 4096
SEQS = 2
T = SEQS * S
NT = T // 128
NEGBIG = -3.0e38

ENGINES = ("tensor", "vector", "scalar", "gpsimd", "sync")


class Rec:
    __slots__ = ("eng", "idx", "fn", "edges", "signal", "amount", "sig_val", "n")

    def __init__(self, eng, idx, fn, amount, n):
        self.eng = eng
        self.idx = idx
        self.fn = fn
        self.edges = []
        self.signal = False
        self.amount = amount
        self.sig_val = None
        self.n = n


class Builder:
    def __init__(self, nc, stack):
        self.nc = nc
        self.stack = stack
        self.plans = {e: [] for e in ENGINES}
        self.sig = {e: stack.enter_context(nc.semaphore("sig_" + e)) for e in ENGINES}
        self.writer = {}
        self.reader = {}
        self.covered = {e: {f: -1 for f in ENGINES} for e in ENGINES}
        self.last_dma = {}

    def op(self, eng, fn, reads=(), writes=(), after=(), dma=False, name=""):
        plan = self.plans[eng]
        rec = Rec(eng, len(plan), fn, 16 if dma else 1, name)
        if dma:
            rec.signal = True
            prev = self.last_dma.get(eng)
            if prev is not None:
                after = tuple(after) + (prev,)
            self.last_dma[eng] = rec
        raw = []
        other = []
        for t in reads:
            w = self.writer.get(t)
            if w:
                raw.extend(w.values())
        for t in writes:
            w = self.writer.get(t)
            if w:
                other.extend(w.values())
            r = self.reader.get(t)
            if r:
                other.extend(r.values())
        raw.extend(after)
        cov = self.covered[eng]
        for d, is_raw in [(d, True) for d in raw] + [(d, False) for d in other]:
            if d is rec:
                continue
            if d.eng == eng and eng == "tensor" and not dma and d.amount == 1:
                continue
            if d.eng == eng and not is_raw and not dma and d.amount == 1:
                continue
            if d.idx <= cov[d.eng]:
                continue
            d.signal = True
            rec.edges.append(d)
            cov[d.eng] = max(cov[d.eng], d.idx)
        plan.append(rec)
        for t in reads:
            self.reader.setdefault(t, {})[eng] = rec
        for t in writes:
            self.writer.setdefault(t, {})[eng] = rec
            self.reader[t] = {eng: rec}
        return rec

    def barrier(self):
        lasts = [self.plans[e][-1] for e in ENGINES if self.plans[e]]
        for e in ENGINES:
            if self.plans[e]:
                self.op(e, lambda eng: eng.nop(), after=tuple(lasts), name="barrier")

    def finalize_and_emit(self):
        for e in ENGINES:
            tot = 0
            for rec in self.plans[e]:
                if rec.signal:
                    tot += rec.amount
                    rec.sig_val = tot
        nc = self.nc
        sig = self.sig
        with nc.Block() as blk:
            for ename in ENGINES:
                plan = self.plans[ename]
                if not plan:
                    continue

                def make_runner(plan=plan):
                    def run(eng):
                        for rec in plan:
                            if rec.edges:
                                waits = {}
                                for d in rec.edges:
                                    waits[d.eng] = max(waits.get(d.eng, 0), d.sig_val)
                                for e2, v in waits.items():
                                    eng.wait_ge(sig[e2], v)
                            inst = rec.fn(eng)
                            if rec.signal:
                                assert inst is not None, f"signaling rec {rec.n} has no inst"
                                inst.then_inc(sig[rec.eng], rec.amount)
                    return run

                getattr(blk, ename)(make_runner())


def _alibi_slopes(n_heads):
    start = 2 ** (-(2 ** -(math.log2(n_heads) - 3)))
    return [start * start ** i for i in range(n_heads)]


def _make_consts():
    id16 = np.eye(128, dtype=np.float32).astype(BF)
    id32 = np.eye(128, dtype=np.float32)
    j = np.arange(128)
    mneg = np.where(j[None, :] <= j[:, None], 0.0, NEGBIG).astype(np.float32)

    slopes = _alibi_slopes(H)
    ktmpl = np.zeros((128, S), dtype=np.float32)
    qtmpl = np.zeros((128, S), dtype=np.float32)
    khi = (np.arange(S) >> 8).astype(np.float32)
    klo = (np.arange(S) & 255).astype(np.float32)
    qth = (np.arange(S) >> 7).astype(np.float32)
    qtl = (np.arange(S) & 127).astype(np.float32)
    for h in range(H):
        sl = np.float64(slopes[h])
        sa = np.float32(BF(np.float32(sl)))
        sb = np.float32(BF(np.float32(sl) - sa))
        base = 8 * h
        ktmpl[base + 0] = khi
        ktmpl[base + 1] = khi
        ktmpl[base + 2] = klo
        ktmpl[base + 3] = klo
        ktmpl[base + 4] = -sa * 128.0
        ktmpl[base + 5] = -sb * 128.0
        ktmpl[base + 6] = -sa
        ktmpl[base + 7] = -sb
        qtmpl[base + 0] = sa * 256.0
        qtmpl[base + 1] = sb * 256.0
        qtmpl[base + 2] = sa
        qtmpl[base + 3] = sb
        qtmpl[base + 4] = qth
        qtmpl[base + 5] = qth
        qtmpl[base + 6] = qtl
        qtmpl[base + 7] = qtl
    maskx = np.zeros((128, 1152), dtype=np.float32)
    maskx[:, 1024:] = mneg
    return id16, id32, mneg, maskx, ktmpl.astype(BF), qtmpl.astype(BF)


def build_program(debug=False):
    stack = ExitStack()
    nc = bass.Bass("TRN2", target_bir_lowering=False)

    dx = nc.dram_tensor("x", [T, D], F32, kind="ExternalInput")
    dwq = nc.dram_tensor("wqt", [D, D], F32, kind="ExternalInput")
    dwk = nc.dram_tensor("wkt", [D, D], F32, kind="ExternalInput")
    dwv = nc.dram_tensor("wvt", [D, D], F32, kind="ExternalInput")
    dwo = nc.dram_tensor("wot", [D, D], F32, kind="ExternalInput")
    dbq = nc.dram_tensor("bq_cols", [128, 8], F32, kind="ExternalInput")
    dbk = nc.dram_tensor("bk_cols", [128, 8], F32, kind="ExternalInput")
    dbvb = nc.dram_tensor("bv_bc", [128, D], F32, kind="ExternalInput")
    dbob = nc.dram_tensor("bo_bc", [128, D], F32, kind="ExternalInput")
    dgw = nc.dram_tensor("gw_cols", [128, 64], F32, kind="ExternalInput")
    dgbb = nc.dram_tensor("gb_bc", [128, 8], F32, kind="ExternalInput")
    dw1 = nc.dram_tensor("w1", [NEXP, D, FF], F32, kind="ExternalInput")
    db1 = nc.dram_tensor("b1_cols", [128, 256], F32, kind="ExternalInput")
    dw2 = nc.dram_tensor("w2", [NEXP, FF, D], F32, kind="ExternalInput")
    db2 = nc.dram_tensor("b2", [NEXP, D], F32, kind="ExternalInput")
    did16 = nc.dram_tensor("c_id16", [128, 128], BF16, kind="ExternalInput")
    did32 = nc.dram_tensor("c_id32", [128, 128], F32, kind="ExternalInput")
    dmneg = nc.dram_tensor("c_mneg", [128, 128], F32, kind="ExternalInput")
    dmaskx = nc.dram_tensor("c_maskx", [128, 1152], F32, kind="ExternalInput")
    dktm = nc.dram_tensor("c_ktmpl", [128, S], BF16, kind="ExternalInput")
    dqtm = nc.dram_tensor("c_qtmpl", [128, S], BF16, kind="ExternalInput")

    dout = nc.dram_tensor("out", [T, D], F32, kind="ExternalOutput")
    g_scr = nc.dram_tensor("g_scr", [T, D], F32, kind="ExternalOutput" if debug else "Internal")
    xn_scr = nc.dram_tensor("xn_scr", [8, 128, T], BF16, kind="ExternalOutput" if debug else "Internal")

    if debug:
        dbg_k = nc.dram_tensor("dbg_k", [128, 8 * S], BF16, kind="ExternalOutput")
        dbg_q = nc.dram_tensor("dbg_q", [128, 8 * S], BF16, kind="ExternalOutput")
        dbg_v = nc.dram_tensor("dbg_v", [128, 16 * 1024], BF16, kind="ExternalOutput")
        dbg_y = nc.dram_tensor("dbg_y", [128, 8 * 1024], BF16, kind="ExternalOutput")

    b = Builder(nc, stack)

    id16 = stack.enter_context(nc.sbuf_tensor("id16", [128, 128], BF16))
    id32 = stack.enter_context(nc.sbuf_tensor("id32", [128, 128], F32))
    mneg = stack.enter_context(nc.sbuf_tensor("mneg", [128, 128], F32))
    ktm = stack.enter_context(nc.sbuf_tensor("ktm", [128, S], BF16))
    qtm = stack.enter_context(nc.sbuf_tensor("qtm", [128, S], BF16))
    bq8 = stack.enter_context(nc.sbuf_tensor("bq8", [128, 8], F32))
    bkc = stack.enter_context(nc.sbuf_tensor("bkc", [128, 8], F32))
    comb = stack.enter_context(nc.sbuf_tensor("comb", [128, NT * 8], F32))
    epsv = stack.enter_context(nc.sbuf_tensor("epsv", [128, 1], F32))
    b.op("vector", lambda e: e.memset(epsv[:, :], 1e-5), writes=("epsv",), name="eps")

    b.op("sync", lambda e: e.dma_start(out=id16[:, :], in_=did16[:, :]), writes=("id16",), dma=True, name="l_id16")
    b.op("sync", lambda e: e.dma_start(out=id32[:, :], in_=did32[:, :]), writes=("id32",), dma=True, name="l_id32")
    b.op("sync", lambda e: e.dma_start(out=mneg[:, :], in_=dmneg[:, :]), writes=("mneg",), dma=True, name="l_mneg")
    b.op("sync", lambda e: e.dma_start(out=ktm[:, :], in_=dktm[:, :]), writes=("ktm",), dma=True, name="l_ktm")
    b.op("sync", lambda e: e.dma_start(out=qtm[:, :], in_=dqtm[:, :]), writes=("qtm",), dma=True, name="l_qtm")
    b.op("sync", lambda e: e.dma_start(out=bkc[:, :], in_=dbk[:, :]), writes=("bkc",), dma=True, name="l_bk")
    b.op("sync", lambda e: e.dma_start(out=bq8[:, :], in_=dbq[:, :]), writes=("bq8",), dma=True, name="l_bq")
    b.op("vector", lambda e: e.tensor_scalar(bq8[:, :], bq8[:, :], 0.125, None, op0=ALU.mult),
         reads=("bq8",), writes=("bq8",), name="bq8s")

    with ExitStack() as scA:
        combT = scA.enter_context(nc.sbuf_tensor("combT", [8, NT * 128], F32))
        w1b0_pre = scA.enter_context(nc.sbuf_tensor("w1b0", [128, 4 * 4096], BF16))

        with ExitStack() as sc:
            sb0 = lambda n, s, d: sc.enter_context(nc.sbuf_tensor(n, s, d))
            ps0 = lambda n, s, d: sc.enter_context(nc.psum_tensor(n, s, d))
            xnT = sb0("xnT", [128, 8 * T], BF16)
            gw = sb0("gw", [128, 64], F32)
            gbb = sb0("gbb", [128, 8], F32)
            xb = [sb0(f"xb{i}", [128, D], F32) for i in range(2)]
            xn32 = [sb0(f"xn32_{i}", [128, D], F32) for i in range(2)]
            xn16 = [sb0(f"xn16_{i}", [128, D], BF16) for i in range(2)]
            xT32 = [sb0(f"xT32_{i}", [128, D], F32) for i in range(2)]
            junk = sb0("junk", [128, D], BF16)
            nsum = sb0("nsum", [128, 1], F32)
            negmean = sb0("negmean", [128, 1], F32)
            msq = sb0("msq", [128, 1], F32)
            var = sb0("var", [128, 1], F32)
            lnv = sb0("lnv", [128, 1], F32)
            rstd = sb0("rstd", [128, 1], F32)
            ssq = sb0("ssq", [128, 1], F32)
            logit = sb0("logit", [128, 8], F32)
            m1 = sb0("m1", [128, 1], F32)
            nm1 = sb0("nm1", [128, 1], F32)
            ew = sb0("ew", [128, 8], F32)
            ind1 = sb0("ind1", [128, 8], F32)
            wm = sb0("wm", [128, 8], F32)
            m2 = sb0("m2", [128, 1], F32)
            indt = sb0("indt", [128, 8], F32)
            cu = sb0("cu", [128, 8], F32)
            dsum = sb0("dsum", [128, 1], F32)
            rec_ = sb0("rec_", [128, 1], F32)
            pp16 = [ps0(f"pp16_{i}", [128, 1024], BF16) for i in range(2)]
            pp32 = [ps0(f"pp32_{i}", [128, 1024], F32) for i in range(2)]
            psg = ps0("psg", [128, 8], F32)
            psct = ps0("psct", [8, 512], F32)

            b.op("sync", lambda e: e.dma_start(out=gw[:, :], in_=dgw[:, :]), writes=("gw",), dma=True, name="l_gw")
            b.op("sync", lambda e: e.dma_start(out=gbb[:, :], in_=dgbb[:, :]), writes=("gbb",), dma=True, name="l_gbb")

            for i in range(NT):
                par = i % 2
                t0 = i * 128
                xbp, xn32p, xn16p, xT32p = xb[par], xn32[par], xn16[par], xT32[par]
                nx, n32, n16, nT32 = f"xb{par}", f"xn32_{par}", f"xn16_{par}", f"xT32_{par}"
                pp16p, pp32p = pp16[par], pp32[par]
                n_pp16, n_pp32 = f"pp16_{par}", f"pp32_{par}"

                b.op("sync", lambda e, d=xbp, s=t0: e.dma_start(out=d[:, :], in_=dx[s:s + 128, :]),
                     writes=(nx,), dma=True, name=f"ldx{i}")
                b.op("vector", lambda e, x=xbp: e.reduce_sum(nsum[:, :], x[:, :], axis=AX.X, negate=True),
                     reads=(nx,), writes=("nsum",), name=f"su{i}")
                b.op("scalar", lambda e, x=xbp: e.activation(junk[:, :], x[:, :], AF.Square, accum_out=ssq[:, :]),
                     reads=(nx,), writes=("junk", "ssq"), name=f"sq{i}")
                b.op("vector", lambda e: e.tensor_scalar(negmean[:, :], nsum[:, :], 1.0 / D, None, op0=ALU.mult),
                     reads=("nsum",), writes=("negmean",), name=f"nm{i}")
                b.op("vector", lambda e: e.tensor_mul(msq[:, :], negmean[:, :], negmean[:, :]),
                     reads=("negmean",), writes=("msq",), name=f"ms{i}")
                b.op("vector", lambda e: e.tensor_scalar(var[:, :], ssq[:, :], 1.0 / D, msq[:, :],
                                                         op0=ALU.mult, op1=ALU.subtract),
                     reads=("ssq", "msq"), writes=("var",), name=f"va{i}")
                b.op("scalar", lambda e: e.activation(lnv[:, :], var[:, :], AF.Ln, bias=epsv[:, :]),
                     reads=("var", "epsv"), writes=("lnv",), name=f"ln{i}")
                b.op("scalar", lambda e: e.activation(rstd[:, :], lnv[:, :], AF.Exp, scale=-0.5),
                     reads=("lnv",), writes=("rstd",), name=f"rs{i}")
                b.op("vector", lambda e, x=xbp, o=xn32p: e.tensor_scalar(
                        o[:, :], x[:, :], negmean[:, :], rstd[:, :], op0=ALU.add, op1=ALU.mult),
                     reads=(nx, "negmean", "rstd"), writes=(n32,), name=f"xn{i}")
                b.op("vector", lambda e, s=xn32p, o=xn16p: e.tensor_copy(o[:, :], s[:, :]),
                     reads=(n32,), writes=(n16,), name=f"cst{i}")

                def tp16(e, src=xn16p, dst=pp16p):
                    inst = None
                    for c in range(8):
                        inst = e.matmul(dst[:, c * 128:(c + 1) * 128], src[:, c * 128:(c + 1) * 128],
                                        id16[:, :], start=(c == 0), stop=(c == 7), is_transpose=True)
                    return inst
                b.op("tensor", tp16, reads=(n16, "id16"), writes=(n_pp16,), name=f"t16_{i}")

                def tp32(e, src=xn32p, dst=pp32p):
                    inst = None
                    for c in range(8):
                        inst = e.matmul(dst[:, c * 128:(c + 1) * 128], src[:, c * 128:(c + 1) * 128],
                                        id32[:, :], start=(c in (0, 4)), stop=(c in (3, 7)), is_transpose=True)
                    return inst
                b.op("tensor", tp32, reads=(n32, "id32"), writes=(n_pp32,), name=f"t32_{i}")

                def ev16(e, src=pp16p, t0=t0):
                    dst = bass.AP(xnT, t0, [[8 * T, 128], [T, 8], [1, 128]])
                    return e.tensor_copy(dst, src[:, :])
                b.op("vector", ev16, reads=(n_pp16,), writes=("xnT",), name=f"e16_{i}")
                b.op("vector", lambda e, src=pp32p, dst=xT32p: e.tensor_copy(dst[:, :], src[:, :]),
                     reads=(n_pp32,), writes=(nT32,), name=f"e32_{i}")

                def gmm(e, src=xT32p):
                    inst = None
                    for c in range(8):
                        inst = e.matmul(psg[:, :], src[:, c * 128:(c + 1) * 128],
                                        gw[:, c * 8:(c + 1) * 8], start=(c == 0), stop=(c == 7))
                    return inst
                b.op("tensor", gmm, reads=(nT32, "gw"), writes=("psg",), name=f"gm{i}")
                b.op("vector", lambda e: e.tensor_add(logit[:, :], psg[:, :], gbb[:, :]),
                     reads=("psg", "gbb"), writes=("logit",), name=f"lg{i}")
                b.op("vector", lambda e: e.reduce_max(m1[:, :], logit[:, :], axis=AX.X),
                     reads=("logit",), writes=("m1",), name=f"m1_{i}")
                b.op("vector", lambda e: e.tensor_scalar(nm1[:, :], m1[:, :], -1.0, None, op0=ALU.mult),
                     reads=("m1",), writes=("nm1",), name=f"nm1_{i}")
                b.op("scalar", lambda e: e.activation(ew[:, :], logit[:, :], AF.Exp, bias=nm1[:, :]),
                     reads=("logit", "nm1"), writes=("ew",), name=f"ew{i}")
                b.op("vector", lambda e: e.tensor_scalar(ind1[:, :], logit[:, :], m1[:, :], None, op0=ALU.is_ge),
                     reads=("logit", "m1"), writes=("ind1",), name=f"i1_{i}")
                b.op("vector", lambda e: e.tensor_scalar(ind1[:, :], ind1[:, :], -1.0, 1.0,
                                                         op0=ALU.mult, op1=ALU.add),
                     reads=("ind1",), writes=("ind1",), name=f"ni{i}")
                b.op("vector", lambda e: e.tensor_mul(wm[:, :], ew[:, :], ind1[:, :]),
                     reads=("ew", "ind1"), writes=("wm",), name=f"wm{i}")
                b.op("vector", lambda e: e.reduce_max(m2[:, :], wm[:, :], axis=AX.X),
                     reads=("wm",), writes=("m2",), name=f"m2_{i}")
                b.op("vector", lambda e: e.tensor_scalar(indt[:, :], ew[:, :], m2[:, :], None, op0=ALU.is_ge),
                     reads=("ew", "m2"), writes=("indt",), name=f"it{i}")
                b.op("vector", lambda e: e.tensor_mul(cu[:, :], ew[:, :], indt[:, :]),
                     reads=("ew", "indt"), writes=("cu",), name=f"cu{i}")
                b.op("vector", lambda e: e.reduce_sum(dsum[:, :], cu[:, :], axis=AX.X),
                     reads=("cu",), writes=("dsum",), name=f"ds{i}")
                b.op("vector", lambda e: e.tensor_scalar(dsum[:, :], dsum[:, :], 1e-6, None, op0=ALU.max),
                     reads=("dsum",), writes=("dsum",), name=f"dm{i}")
                b.op("vector", lambda e: e.reciprocal(rec_[:, :], dsum[:, :]),
                     reads=("dsum",), writes=("rec_",), name=f"rc{i}")
                b.op("vector", lambda e, i=i: e.tensor_scalar(comb[:, i * 8:(i + 1) * 8], cu[:, :],
                                                              rec_[:, :], None, op0=ALU.mult),
                     reads=("cu", "rec_"), writes=("comb",), name=f"cb{i}")

                def ctp(e, i=i):
                    return e.matmul(psct[0:8, (i % 4) * 128:(i % 4) * 128 + 128],
                                    comb[:, i * 8:(i + 1) * 8], id32[:, :],
                                    start=(i % 4 == 0), stop=(i % 4 == 3), is_transpose=True)
                b.op("tensor", ctp, reads=("comb", "id32"), writes=("psct",), name=f"ct{i}")
                if i % 4 == 3:
                    b.op("vector", lambda e, i=i: e.tensor_copy(combT[0:8, (i - 3) * 128:(i + 1) * 128],
                                                                psct[0:8, :]),
                         reads=("psct",), writes=("combT",), name=f"ce{i}")

            for c in range(8):
                b.op("sync", lambda e, c=c: e.dma_start(out=xn_scr[c, :, :], in_=xnT[:, c * T:(c + 1) * T]),
                     reads=("xnT",), writes=("xn_scr",), dma=True, name=f"sp{c}")

            for fc in range(4):
                w1src = dw1[0, :, fc * 512:(fc + 1) * 512].rearrange("(c p) f -> p c f", p=128)
                w1dst = bass.AP(w1b0_pre, fc * 4096, [[4 * 4096, 128], [512, 8], [1, 512]])
                b.op("gpsimd", lambda e, d=w1dst, s=w1src: e.dma_start(out=d, in_=s),
                     writes=("w1b0",), dma=True, name=f"w1l0_{fc}")

        b.barrier()

        with ExitStack() as sc:
            sb1 = lambda n, s, d: sc.enter_context(nc.sbuf_tensor(n, s, d))
            ps1 = lambda n, s, d: sc.enter_context(nc.psum_tensor(n, s, d))
            w1b = [w1b0_pre, sb1("w1b1", [128, 4 * 4096], BF16)]
            w2h = sb1("w2h", [128, 16 * 1024], BF16)
            hdnT = [sb1(f"hdnT{i}", [128, 16 * 512], BF16) for i in range(2)]
            xnb1 = [sb1(f"xnb1_{i}", [128, 8 * 512], BF16) for i in range(2)]
            b1c = sb1("b1c", [128, 256], F32)
            b2sb = sb1("b2sb", [8, D], F32)
            gout = [sb1(f"gout{i}", [128, D], F32) for i in range(2)]
            psa = [ps1(f"psa{i}", [128, 512], F32) for i in range(2)]
            psb = [ps1(f"psb{i}", [128, 1024], F32) for i in range(2)]

            b.op("sync", lambda e: e.dma_start(out=b1c[:, :], in_=db1[:, :]), writes=("b1c",), dma=True, name="l_b1")
            b.op("sync", lambda e: e.dma_start(out=b2sb[:, :], in_=db2[:, :]), writes=("b2sb",), dma=True, name="l_b2")

            for gt in range(NT):
                go, ngo = gout[gt % 2], f"gout{gt % 2}"
                pb, npb = psb[gt % 2], f"psb{gt % 2}"

                def b2mm(e, gt=gt, pb=pb):
                    e.matmul(pb[:, 0:512], combT[0:8, gt * 128:(gt + 1) * 128],
                             b2sb[0:8, 0:512], start=True, stop=True)
                    return e.matmul(pb[:, 512:1024], combT[0:8, gt * 128:(gt + 1) * 128],
                                    b2sb[0:8, 512:1024], start=True, stop=True)
                b.op("tensor", b2mm, reads=("combT", "b2sb"), writes=(npb,), name=f"b2m{gt}")
                b.op("vector", lambda e, pb=pb, go=go: e.tensor_copy(go[:, :], pb[:, :]),
                     reads=(npb,), writes=(ngo,), name=f"b2c{gt}")
                b.op("gpsimd", lambda e, go=go, gt=gt: e.dma_start(
                        out=g_scr[gt * 128:(gt + 1) * 128, :], in_=go[:, :]),
                     reads=(ngo,), writes=(f"g{gt}",), dma=True, name=f"b2s{gt}")

            def load_w1(rnd):
                ex, fh = rnd // 2, rnd % 2
                wp, nwp = w1b[rnd % 2], f"w1b{rnd % 2}"
                for fc in range(4):
                    src = dw1[ex, :, fh * 2048 + fc * 512: fh * 2048 + (fc + 1) * 512] \
                        .rearrange("(c p) f -> p c f", p=128)
                    dst = bass.AP(wp, fc * 4096, [[4 * 4096, 128], [512, 8], [1, 512]])
                    b.op("gpsimd", lambda e, d=dst, s=src: e.dma_start(out=d, in_=s),
                         writes=(nwp,), dma=True, name=f"w1l{rnd}_{fc}")

            for rnd in range(16):
                ex, fh = rnd // 2, rnd % 2
                w1p, nw1 = w1b[rnd % 2], f"w1b{rnd % 2}"
                for fc in range(4):
                    src = dw2[ex, fh * 2048 + fc * 512: fh * 2048 + (fc + 1) * 512, :] \
                        .rearrange("(c p) h -> p c h", p=128)
                    dst = bass.AP(w2h, fc * 4 * 1024, [[16 * 1024, 128], [1024, 4], [1, 1024]])
                    b.op("gpsimd", lambda e, d=dst, s=src: e.dma_start(out=d, in_=s),
                         writes=("w2h",), dma=True, name=f"w2l{rnd}_{fc}")
                if rnd + 1 < 16:
                    load_w1(rnd + 1)

                for tg in range(8):
                    tpos = tg * 512
                    xp, nxp = xnb1[tg % 2], f"xnb1_{tg % 2}"
                    src = xn_scr[:, :, tpos:tpos + 512].rearrange("c p t -> p c t")
                    b.op("sync", lambda e, d=xp, s=src: e.dma_start(
                            out=d.ap().rearrange("p (c t) -> p c t", c=8), in_=s),
                         reads=("xn_scr",), writes=(nxp,), dma=True, name=f"xl{rnd}_{tg}")

                    for fb in range(16):
                        fc, fs = fb // 4, fb % 4
                        pa, npa = psa[fb % 2], f"psa{fb % 2}"

                        def amm(e, w1p=w1p, fc=fc, fs=fs, pa=pa, xp=xp):
                            inst = None
                            for dc in range(8):
                                inst = e.matmul(
                                    pa[:, :],
                                    w1p[:, fc * 4096 + dc * 512 + fs * 128: fc * 4096 + dc * 512 + fs * 128 + 128],
                                    xp[:, dc * 512:(dc + 1) * 512],
                                    start=(dc == 0), stop=(dc == 7))
                            return inst
                        b.op("tensor", amm, reads=(nw1, nxp), writes=(npa,), name=f"am{rnd}_{tg}_{fb}")

                        bcol = ex * 32 + fh * 16 + fb
                        hb, nhb = hdnT[tg % 2], f"hdnT{tg % 2}"
                        b.op("scalar", lambda e, pa=pa, fb=fb, bcol=bcol, hb=hb: e.activation(
                                hb[:, fb * 512:(fb + 1) * 512], pa[:, :], AF.Gelu,
                                bias=b1c[:, bcol:bcol + 1]),
                             reads=(npa, "b1c"), writes=(nhb,), name=f"ge{rnd}_{tg}_{fb}")

                    for tl in range(4):
                        gt = tg * 4 + tl
                        pb, npb = psb[tl % 2], f"psb{tl % 2}"
                        go, ngo = gout[tl % 2], f"gout{tl % 2}"

                        hb, nhb = hdnT[tg % 2], f"hdnT{tg % 2}"
                        def bmm(e, tl=tl, pb=pb, hb=hb):
                            inst = None
                            for fb in range(16):
                                lhs = hb[:, fb * 512 + tl * 128: fb * 512 + tl * 128 + 128]
                                inst = e.matmul(pb[:, 0:512], lhs, w2h[:, fb * 1024: fb * 1024 + 512],
                                                start=(fb == 0), stop=(fb == 15))
                                inst = e.matmul(pb[:, 512:1024], lhs, w2h[:, fb * 1024 + 512: fb * 1024 + 1024],
                                                start=(fb == 0), stop=(fb == 15))
                            return inst
                        b.op("tensor", bmm, reads=(nhb, "w2h"), writes=(npb,), name=f"bm{rnd}_{gt}")
                        b.op("vector", lambda e, pb=pb, go=go, gt=gt, ex=ex: e.tensor_scalar(
                                go[:, :], pb[:, :], comb[:, gt * 8 + ex: gt * 8 + ex + 1], None, op0=ALU.mult),
                             reads=(npb, "comb"), writes=(ngo,), name=f"be{rnd}_{gt}")
                        b.op("gpsimd", lambda e, go=go, gt=gt: e.dma_start(
                                out=g_scr[gt * 128:(gt + 1) * 128, :], in_=go[:, :], accum_op=ALU.add),
                             reads=(ngo,), writes=(f"g{gt}",), dma=True, name=f"ga{rnd}_{gt}")

        b.barrier()

    with ExitStack() as sc:
        sb2 = lambda n, s, d: sc.enter_context(nc.sbuf_tensor(n, s, d))
        kT = sb2("kT", [128, 8 * S], BF16)
        qT2 = sb2("qT2", [128, 8 * S], BF16)
        v_tm = sb2("v_tm", [128, 16 * 1024], BF16)
        yT = sb2("yT", [128, 8 * 1024], BF16)
        bob = sb2("bob", [128, D], F32)
        lcur = [sb2(f"lcur{i}", [128, 1], F32) for i in range(2)]
        rl2 = [sb2(f"rl2_{i}", [128, 1], F32) for i in range(2)]
        mstat = [sb2(f"mstat{i}", [128, 3], F32) for i in range(2)]
        lc2 = [sb2(f"lc2_{i}", [128, 2], F32) for i in range(2)]
        negm = [sb2(f"negm{i}", [128, 1], F32) for i in range(2)]

        b.op("sync", lambda e: e.dma_start(out=bob[:, :], in_=dbob[:, :]), writes=("bob",), dma=True, name="l_bob")

        for sq in range(SEQS):
            soff = sq * S
            with ExitStack() as sp:
                sbp = lambda n, s, d: sp.enter_context(nc.sbuf_tensor(n, s, d))
                psp = lambda n, s, d: sp.enter_context(nc.psum_tensor(n, s, d))
                wkb = sbp(f"wkb{sq}", [128, 8 * 1024], BF16)
                wqb = sbp(f"wqb{sq}", [128, 8 * 1024], BF16)
                wvb = sbp(f"wvb{sq}", [128, 8 * 1024], BF16)
                xnb = [sbp(f"xnb2_{sq}_{i}", [128, 8 * 512], BF16) for i in range(2)]
                bvb = sbp(f"bvb{sq}", [128, D], F32)
                pko = [psp(f"pko{sq}_{i}", [128, 512], F32) for i in range(2)]
                b.op("sync", lambda e, bvb=bvb: e.dma_start(out=bvb[:, :], in_=dbvb[:, :]),
                     writes=(f"bvb{sq}",), dma=True, name=f"l_bvb{sq}")
                psv = psp(f"psv{sq}", [128, 1024], F32)

                for wdst, wsrc, wname in ((wkb, dwk, f"wkb{sq}"), (wqb, dwq, f"wqb{sq}"), (wvb, dwv, f"wvb{sq}")):
                    src = wsrc[:, :].rearrange("(c p) i -> p c i", p=128)
                    b.op("gpsimd", lambda e, dd=wdst, s=src: e.dma_start(
                            out=dd.ap().rearrange("p (c i) -> p c i", c=8), in_=s),
                         writes=(wname,), dma=True, name=f"lw_{wname}")

                for tc in range(4):
                    xp, nxp = xnb[tc % 2], f"xnb2_{sq}_{tc % 2}"
                    src = xn_scr[:, :, soff + tc * 512: soff + (tc + 1) * 512].rearrange("c p t -> p c t")
                    b.op("sync", lambda e, d=xp, s=src: e.dma_start(
                            out=d.ap().rearrange("p (c t) -> p c t", c=8), in_=s),
                         reads=("xn_scr",), writes=(nxp,), dma=True, name=f"x2{sq}_{tc}")

                    for proj, wb, nwb, dstT, ndT, bias, scale in (
                            ("k", wkb, f"wkb{sq}", kT, "kT", bkc, 1.0),
                            ("q", wqb, f"wqb{sq}", qT2, "qT2", bq8, 0.125)):
                        for ic in range(8):
                            pk, npk = pko[ic % 2], f"pko{sq}_{ic % 2}"

                            def pmm(e, xp=xp, ic=ic, pk=pk, wb=wb):
                                inst = None
                                for dc in range(8):
                                    inst = e.matmul(pk[:, :],
                                                    wb[:, dc * 1024 + ic * 128: dc * 1024 + ic * 128 + 128],
                                                    xp[:, dc * 512:(dc + 1) * 512],
                                                    start=(dc == 0), stop=(dc == 7))
                                return inst
                            b.op("tensor", pmm, reads=(nwb, nxp), writes=(npk,), name=f"{proj}m{sq}_{tc}_{ic}")
                            b.op("scalar", lambda e, pk=pk, ic=ic, tc=tc, dstT=dstT, bias=bias, scale=scale:
                                    e.activation(dstT[:, ic * S + tc * 512: ic * S + (tc + 1) * 512], pk[:, :],
                                                 AF.Identity, bias=bias[:, ic:ic + 1], scale=scale),
                                 reads=(npk, "bkc", "bq8"), writes=(ndT,), name=f"{proj}e{sq}_{tc}_{ic}")

                    for tl in range(4):
                        tj = tc * 4 + tl

                        def vmm(e, xp=xp, tl=tl, wb=wvb):
                            inst = None
                            for dc in range(8):
                                lhs = xp[:, dc * 512 + tl * 128: dc * 512 + tl * 128 + 128]
                                inst = e.matmul(psv[:, 0:512], lhs, wb[:, dc * 1024: dc * 1024 + 512],
                                                start=(dc == 0), stop=(dc == 7))
                                inst = e.matmul(psv[:, 512:1024], lhs, wb[:, dc * 1024 + 512: dc * 1024 + 1024],
                                                start=(dc == 0), stop=(dc == 7))
                            return inst
                        b.op("tensor", vmm, reads=(f"wvb{sq}", nxp), writes=(f"psv{sq}",), name=f"vm{sq}_{tc}_{tl}")
                        b.op("vector", lambda e, tj=tj: e.scalar_tensor_tensor(
                                v_tm[:, tj * 1024:(tj + 1) * 1024], psv[:, :], 1.0, bvb[:, :],
                                op0=ALU.mult, op1=ALU.add),
                             reads=(f"psv{sq}", f"bvb{sq}"), writes=("v_tm",), name=f"ve{sq}_{tc}_{tl}")

            if debug and sq == 0:
                b.op("sync", lambda e: e.dma_start(out=dbg_k[:, :], in_=kT[:, :]),
                     reads=("kT",), dma=True, name="dbgk")
                b.op("sync", lambda e: e.dma_start(out=dbg_q[:, :], in_=qT2[:, :]),
                     reads=("qT2",), dma=True, name="dbgq")
                b.op("sync", lambda e: e.dma_start(out=dbg_v[:, :], in_=v_tm[:, :]),
                     reads=("v_tm",), dma=True, name="dbgv")
            b.barrier()

            with ExitStack() as sp:
                sbp = lambda n, s, d: sp.enter_context(nc.sbuf_tensor(n, s, d))
                psp = lambda n, s, d: sp.enter_context(nc.psum_tensor(n, s, d))
                wob = sbp(f"wob{sq}", [128, 8 * 1024], BF16)

                src = dwo[:, :].rearrange("(c p) i -> p c i", p=128)
                b.op("gpsimd", lambda e, s=src: e.dma_start(
                        out=wob.ap().rearrange("p (c i) -> p c i", c=8), in_=s),
                     writes=(f"wob{sq}",), dma=True, name=f"lwo{sq}")

                gc = [0]
                for half in range(2):
                    hsc = ExitStack()
                    hps = lambda n, s, d: hsc.enter_context(nc.psum_tensor(n, s, d))
                    hsb = lambda n, s, d: hsc.enter_context(nc.sbuf_tensor(n, s, d))
                    kqa = [hsb(f"kqa{sq}_{half}_{i}", [72, S], BF16) for i in range(2)]
                    qaug = [hsb(f"qaug{sq}_{half}_{i}", [72, 1024], BF16) for i in range(2)]
                    Ebuf = [hsb(f"Eb{sq}_{half}_{i}", [128, S], BF16) for i in range(3)]
                    ETb = [hsb(f"ET{sq}_{half}_{i}", [128, 16 * 128], BF16) for i in range(2)]
                    sbs = [hsb(f"sbs{sq}_{half}_{i}", [128, S], F32) for i in range(2)]
                    pssp = [hps(f"pss{sq}_{half}_{i}", [128, 1024], F32) for i in range(2)]
                    pset = [hps(f"pset{sq}_{half}_{i}", [128, 1024], BF16) for i in range(2)]
                    psy = hps(f"psy{sq}_{half}", [64, 128], F32)
                    stq = []
                    for h in range(H):
                        hp = h % 2
                        kq, nkq = kqa[hp], f"kqa{sq}_{half}_{hp}"
                        qa, nqa = qaug[hp], f"qaug{sq}_{half}_{hp}"
                        b.op("sync", lambda e, kq=kq, h=h: e.dma_start(
                                out=kq[0:64, :],
                                in_=kT[(h % 2) * 64:(h % 2) * 64 + 64, (h // 2) * S:(h // 2 + 1) * S]),
                             reads=("kT",), writes=(nkq,), dma=True, name=f"bk{sq}_{half}_{h}")
                        b.op("sync", lambda e, kq=kq, h=h: e.dma_start(
                                out=kq[64:72, :], in_=ktm[8 * h:8 * h + 8, :]),
                             reads=("ktm",), writes=(nkq,), dma=True, name=f"bkt{sq}_{half}_{h}")
                        b.op("sync", lambda e, qa=qa, h=h, half=half: e.dma_start(
                                out=qa[0:64, :],
                                in_=qT2[(h % 2) * 64:(h % 2) * 64 + 64,
                                        (h // 2) * S + half * 1024:(h // 2) * S + half * 1024 + 1024]),
                             reads=("qT2",), writes=(nqa,), dma=True, name=f"bq{sq}_{half}_{h}")
                        b.op("sync", lambda e, qa=qa, h=h, half=half: e.dma_start(
                                out=qa[64:72, :], in_=qtm[8 * h:8 * h + 8, half * 1024: half * 1024 + 1024]),
                             reads=("qtm",), writes=(nqa,), dma=True, name=f"bqt{sq}_{half}_{h}")

                        for A in range(8):
                            Ag = half * 8 + A
                            W = (Ag + 1) * 128
                            ep_i = gc[0] % 3
                            et_i = gc[0] % 2
                            spar = A % 2
                            gc[0] += 1
                            Ep, nEp = Ebuf[ep_i], f"Eb{sq}_{half}_{ep_i}"
                            ETp, nETp = ETb[et_i], f"ET{sq}_{half}_{et_i}"
                            sbp_ = sbs[spar]
                            nsbs = f"sbs{sq}_{half}_{spar}"
                            nneg = f"negm{sq}_{spar}"
                            nms = f"mstat{sq}_{spar}"
                            tag = f"{sq}_{half}_{h}_{A}"
                            if Ag <= 7:
                                pieces = [(A % 2, 0, W)]
                            else:
                                pieces = [(A % 2, 0, 1024), ((A + 1) % 2, 1024, W)]
                            n512 = Ag // 4
                            rem = Ag % 4
                            chunks = [(c * 512, (c + 1) * 512, True, True) for c in range(n512)]
                            if rem:
                                chunks.append((n512 * 512, n512 * 512 + rem * 128, True, False))
                                chunks.append((W - 128, W, False, True))
                            else:
                                chunks.append((W - 128, W, True, True))

                            piece_info = []
                            for pi, (pb, lo, hi) in enumerate(pieces):
                                ps_ = pssp[pb]
                                nps = f"pss{sq}_{half}_{pb}"

                                def smm(e, qa=qa, kq=kq, A=A, ps_=ps_, lo=lo, hi=hi, chunks=tuple(chunks)):
                                    inst = None
                                    lhs = qa[0:72, A * 128:(A + 1) * 128]
                                    for gl, gh, st, sp in chunks:
                                        if gl < lo or gl >= hi:
                                            continue
                                        inst = e.matmul(ps_[:, gl - lo:gh - lo], lhs,
                                                        kq[0:72, gl:gh], start=st, stop=sp)
                                    return inst
                                b.op("tensor", smm, reads=(nqa, nkq), writes=(nps,), name=f"s{tag}_{pi}")
                                piece_info.append((pi, pb, lo, hi, ps_, nps))

                            mains = 0
                            for pi, pb, lo, hi, ps_, nps in piece_info:
                                hid = min(hi, W - 128)
                                if hid > lo:
                                    mc = mains
                                    mains += 1
                                    b.op("vector", lambda e, ps_=ps_, lo=lo, hid=hid, mc=mc, sbp_=sbp_,
                                                spar=spar: e.tensor_scalar(
                                            sbp_[:, lo:hid], ps_[:, 0:hid - lo], 1.0, None,
                                            op0=ALU.mult, op1=ALU.max,
                                            accum_out=mstat[spar][:, mc:mc + 1]),
                                         reads=(nps,), writes=(nsbs, nms), name=f"e{tag}_{pi}")
                                if hi == W:
                                    dc = mains
                                    b.op("vector", lambda e, ps_=ps_, lo=lo, W=W, sbp_=sbp_: e.scalar_tensor_tensor(
                                            sbp_[:, W - 128:W], ps_[:, W - 128 - lo:W - lo], 1.0, mneg[:, :],
                                            op0=ALU.mult, op1=ALU.add),
                                         reads=(nps, "mneg"), writes=(nsbs,), name=f"d{tag}")
                                    b.op("vector", lambda e, W=W, sbp_=sbp_, spar=spar, dc=dc: e.tensor_scalar(
                                            sbp_[:, W - 128:W], sbp_[:, W - 128:W], 1.0, None,
                                            op0=ALU.mult, op1=ALU.max,
                                            accum_out=mstat[spar][:, dc:dc + 1]),
                                         reads=(nsbs,), writes=(nsbs, nms), name=f"dm{tag}")
                            ncols = mains + 1
                            b.op("vector", lambda e, spar=spar, ncols=ncols: e.reduce_max(
                                    negm[spar][:, :], mstat[spar][:, 0:ncols], axis=AX.X, negate=True),
                                 reads=(nms,), writes=(nneg,), name=f"rm{tag}")
                            b.op("scalar", lambda e, Ep=Ep, W=W, sbp_=sbp_, spar=spar: e.activation(
                                    Ep[:, 0:W], sbp_[:, 0:W], AF.Exp, bias=negm[spar][:, :],
                                    accum_out=lcur[spar][:, :]),
                                 reads=(nsbs, nneg), writes=(nEp, f"lcur{sq}_{spar}"), name=f"x{tag}")

                            if len(stq) > 0:
                                mid = None
                                if stq and stq[0][0] is None:
                                    _, _, mid = stq.pop(0)
                                if stq:
                                    d2p, t1p, t2p = stq.pop(0)
                                    d2p()
                                    t1p(mid)
                                    stq.insert(0, (None, None, t2p))
                                elif mid is not None:
                                    mid()

                            def mk_d2(Ep=Ep, nEp=nEp, W=W, spar=spar, tag=tag):
                                def d2():
                                    b.op("vector", lambda e: e.reciprocal(rl2[spar][:, :], lcur[spar][:, :]),
                                         reads=(f"lcur{sq}_{spar}",), writes=(f"rl2_{sq}_{spar}",),
                                         name=f"rl{tag}")
                                    b.op("vector", lambda e: e.tensor_scalar(
                                            Ep[:, 0:W], Ep[:, 0:W], rl2[spar][:, :], None, op0=ALU.mult),
                                         reads=(nEp, f"rl2_{sq}_{spar}"), writes=(nEp,), name=f"en{tag}")
                                return d2

                            def mk_t1(Ep=Ep, nEp=nEp, ETp=ETp, nETp=nETp, Ag=Ag, tag=tag,
                                      pset=pset, half=half):
                                def t1(mid=None):
                                    ngroups = (Ag + 8) // 8
                                    for g in range(ngroups):
                                        if g == 1 and mid is not None:
                                            mid()
                                            mid = None
                                        j0, j1 = g * 8, min(g * 8 + 8, Ag + 1)
                                        psetg = pset[g % 2]
                                        npset = f"pset{sq}_{half}_{g % 2}"

                                        def tpe(e, Ep=Ep, j0=j0, j1=j1, psetg=psetg):
                                            inst = None
                                            for j in range(j0, j1):
                                                inst = e.matmul(psetg[:, (j - j0) * 128:(j - j0 + 1) * 128],
                                                                Ep[:, j * 128:(j + 1) * 128], id16[:, :],
                                                                start=(j == j0), stop=(j == j1 - 1),
                                                                is_transpose=True)
                                            return inst
                                        b.op("tensor", tpe, reads=(nEp, "id16"), writes=(npset,),
                                             name=f"t{tag}_{g}")
                                        b.op("scalar", lambda e, ETp=ETp, j0=j0, j1=j1, psetg=psetg: e.copy(
                                                ETp[:, j0 * 128:j1 * 128], psetg[:, 0:(j1 - j0) * 128]),
                                             reads=(npset,), writes=(nETp,), name=f"u{tag}_{g}")
                                    if mid is not None:
                                        mid()
                                return t1

                            def mk_t2(ETp=ETp, nETp=nETp, h=h, A=A, Ag=Ag, tag=tag,
                                      psy=psy, half=half):
                                npsy = f"psy{sq}_{half}"
                                def t2():
                                    def avm(e, ETp=ETp, h=h, Ag=Ag):
                                        inst = None
                                        for jj in range(Ag + 1):
                                            inst = e.matmul(psy[0:64, :],
                                                            v_tm[:, jj * 1024 + h * 64: jj * 1024 + h * 64 + 64],
                                                            ETp[:, jj * 128:(jj + 1) * 128],
                                                            start=(jj == 0), stop=(jj == Ag))
                                        return inst
                                    b.op("tensor", avm, reads=(nETp, "v_tm"), writes=(npsy,), name=f"a{tag}")
                                    b.op("vector", lambda e, h=h, A=A: e.tensor_copy(
                                            yT[(h % 2) * 64:(h % 2) * 64 + 64,
                                               (h // 2) * 1024 + A * 128:(h // 2) * 1024 + A * 128 + 128],
                                            psy[0:64, :]),
                                         reads=(npsy,), writes=("yT",), name=f"y{tag}")
                                return t2

                            stq.append((mk_d2(), mk_t1(), mk_t2()))
                    while stq:
                        d2, t1, t2 = stq.pop(0)
                        if d2 is not None:
                            d2(); t1()
                        t2()
                    hsc.close()
                    b.barrier()
                    osc = ExitStack()
                    ops_ = lambda n, s, d: osc.enter_context(nc.psum_tensor(n, s, d))
                    osb = lambda n, s, d: osc.enter_context(nc.sbuf_tensor(n, s, d))
                    pso = ops_(f"pso{sq}_{half}", [128, 1024], F32)
                    stmp = osb(f"stmp{sq}_{half}", [128, D], F32)
                    sgb = osb(f"sgb{sq}_{half}", [128, D], F32)
                    gti = [osb(f"gti{sq}_{half}_{i}", [128, D], F32) for i in range(2)]
                    xti = [osb(f"xti{sq}_{half}_{i}", [128, D], F32) for i in range(2)]

                    if debug and sq == 0 and half == 0:
                        b.op("sync", lambda e: e.dma_start(out=dbg_y[:, :], in_=yT[:, :]),
                             reads=("yT",), dma=True, name="dbgy")
                    for tl in range(8):
                        t0 = soff + half * 1024 + tl * 128
                        gp, ngp = gti[tl % 2], f"gti{sq}_{half}_{tl % 2}"
                        xp2, nxp2 = xti[tl % 2], f"xti{sq}_{half}_{tl % 2}"
                        ob, nob = xp2, nxp2
                        gt_glob = t0 // 128
                        b.op("sync", lambda e, gp=gp, t0=t0: e.dma_start(out=gp[:, :], in_=g_scr[t0:t0 + 128, :]),
                             reads=(f"g{gt_glob}",), writes=(ngp,), dma=True, name=f"lg{sq}_{half}_{tl}")
                        b.op("sync", lambda e, xp2=xp2, t0=t0: e.dma_start(out=xp2[:, :], in_=dx[t0:t0 + 128, :]),
                             writes=(nxp2,), dma=True, name=f"lx{sq}_{half}_{tl}")

                        def omm(e, tl=tl):
                            inst = None
                            for ic in range(8):
                                lhs = yT[:, ic * 1024 + tl * 128: ic * 1024 + tl * 128 + 128]
                                inst = e.matmul(pso[:, 0:512], lhs, wob[:, ic * 1024: ic * 1024 + 512],
                                                start=(ic == 0), stop=(ic == 7))
                                inst = e.matmul(pso[:, 512:1024], lhs, wob[:, ic * 1024 + 512: ic * 1024 + 1024],
                                                start=(ic == 0), stop=(ic == 7))
                            return inst
                        b.op("tensor", omm, reads=("yT", f"wob{sq}"), writes=(f"pso{sq}_{half}",), name=f"o{sq}_{half}_{tl}")
                        b.op("vector", lambda e: e.scalar_tensor_tensor(
                                stmp[:, :], pso[:, :], 1.0, bob[:, :], op0=ALU.mult, op1=ALU.add),
                             reads=(f"pso{sq}_{half}", "bob"), writes=("stmp",), name=f"se{sq}_{half}_{tl}")
                        b.op("scalar", lambda e, gp=gp: e.activation(gp[:, :], gp[:, :], AF.Silu),
                             reads=(ngp,), writes=(ngp,), name=f"si{sq}_{half}_{tl}")
                        b.op("vector", lambda e, gp=gp: e.tensor_mul(sgb[:, :], stmp[:, :], gp[:, :]),
                             reads=("stmp", ngp), writes=("sgb",), name=f"sg{sq}_{half}_{tl}")
                        b.op("vector", lambda e, xp2=xp2: e.tensor_add(xp2[:, :], sgb[:, :], xp2[:, :]),
                             reads=("sgb", nxp2), writes=(nxp2,), name=f"oa{sq}_{half}_{tl}")
                        b.op("sync", lambda e, ob=ob, t0=t0: e.dma_start(out=dout[t0:t0 + 128, :], in_=ob[:, :]),
                             reads=(nob,), writes=("out",), dma=True, name=f"so{sq}_{half}_{tl}")
                    osc.close()
                    if half == 0:
                        b.barrier()

            if sq + 1 < SEQS:
                b.barrier()

        final_out = b.writer["out"]["sync"]
        b.op("sync", lambda e: e.nop(), after=(final_out,), name="final")

    b.finalize_and_emit()
    stack.close()
    return nc


_cached = {}


def _get_program():
    if "nc" not in _cached:
        _cached["nc"] = build_program()
    return _cached["nc"]


def kernel(L0_attn_x, L0_attn_Wq, L0_attn_bq, L0_attn_Wk, L0_attn_bk,
           L0_attn_Wv, L0_attn_bv, L0_attn_Wo, L0_attn_bo,
           L0_ffn_gate_w, L0_ffn_gate_b, L0_ffn_w1, L0_ffn_b1,
           L0_ffn_w2, L0_ffn_b2):
    f32 = lambda a: np.ascontiguousarray(np.asarray(a, dtype=np.float32))
    x = f32(L0_attn_x)
    Wq, Wk, Wv, Wo = map(f32, (L0_attn_Wq, L0_attn_Wk, L0_attn_Wv, L0_attn_Wo))
    bq, bk, bv, bo = map(f32, (L0_attn_bq, L0_attn_bk, L0_attn_bv, L0_attn_bo))
    gw, gb = f32(L0_ffn_gate_w), f32(L0_ffn_gate_b)
    w1, b1, w2, b2 = map(f32, (L0_ffn_w1, L0_ffn_b1, L0_ffn_w2, L0_ffn_b2))

    id16, id32, mneg, maskx, ktmpl, qtmpl = _make_consts()
    cols = lambda v: np.ascontiguousarray(v.reshape(8, 128).T)
    common = dict(
        c_maskx=maskx,
        wqt=np.ascontiguousarray(Wq.T), wkt=np.ascontiguousarray(Wk.T),
        wvt=np.ascontiguousarray(Wv.T), wot=np.ascontiguousarray(Wo.T),
        bq_cols=cols(bq), bk_cols=cols(bk),
        bv_bc=np.ascontiguousarray(np.broadcast_to(bv, (128, D))),
        bo_bc=np.ascontiguousarray(np.broadcast_to(bo, (128, D))),
        gw_cols=np.ascontiguousarray(gw.reshape(8, 128, 8).transpose(1, 0, 2).reshape(128, 64)),
        gb_bc=np.ascontiguousarray(np.broadcast_to(gb, (128, 8))),
        w1=w1, w2=w2,
        b1_cols=np.ascontiguousarray(b1.reshape(8, 32, 128).transpose(2, 0, 1).reshape(128, 256)),
        b2=b2,
        c_id16=id16, c_id32=id32, c_mneg=mneg, c_ktmpl=ktmpl, c_qtmpl=qtmpl,
    )
    in_maps = []
    for core in range(8):
        m = dict(common)
        m["x"] = np.ascontiguousarray(x[core * SEQS:(core + 1) * SEQS].reshape(T, D))
        in_maps.append(m)

    nc = _get_program()
    res = run_bass_kernel_spmd(nc, in_maps, core_ids=list(range(8)))
    out = np.concatenate([r["out"].reshape(SEQS, S, D) for r in res.results], axis=0)
    return out


# version 11
# speedup vs baseline: 1.8513x; 1.0071x over previous
import math
from contextlib import ExitStack

import numpy as np
import ml_dtypes

import concourse.bass as bass
import concourse.mybir as mybir
from concourse.bass_utils import run_bass_kernel_spmd

F32 = mybir.dt.float32
BF16 = mybir.dt.bfloat16
AF = mybir.ActivationFunctionType
ALU = mybir.AluOpType
AX = mybir.AxisListType
BF = ml_dtypes.bfloat16

D = 1024
H = 16
DH = 64
S = 2048
NEXP = 8
FF = 4096
SEQS = 2
T = SEQS * S
NT = T // 128
NEGBIG = -3.0e38

ENGINES = ("tensor", "vector", "scalar", "gpsimd", "sync")


class Rec:
    __slots__ = ("eng", "idx", "fn", "edges", "signal", "amount", "sig_val", "n")

    def __init__(self, eng, idx, fn, amount, n):
        self.eng = eng
        self.idx = idx
        self.fn = fn
        self.edges = []
        self.signal = False
        self.amount = amount
        self.sig_val = None
        self.n = n


class Builder:
    def __init__(self, nc, stack):
        self.nc = nc
        self.stack = stack
        self.plans = {e: [] for e in ENGINES}
        self.sig = {e: stack.enter_context(nc.semaphore("sig_" + e)) for e in ENGINES}
        self.writer = {}
        self.reader = {}
        self.covered = {e: {f: -1 for f in ENGINES} for e in ENGINES}
        self.last_dma = {}

    def op(self, eng, fn, reads=(), writes=(), after=(), dma=False, name=""):
        plan = self.plans[eng]
        rec = Rec(eng, len(plan), fn, 16 if dma else 1, name)
        if dma:
            rec.signal = True
            prev = self.last_dma.get(eng)
            if prev is not None:
                after = tuple(after) + (prev,)
            self.last_dma[eng] = rec
        raw = []
        other = []
        for t in reads:
            w = self.writer.get(t)
            if w:
                raw.extend(w.values())
        for t in writes:
            w = self.writer.get(t)
            if w:
                other.extend(w.values())
            r = self.reader.get(t)
            if r:
                other.extend(r.values())
        raw.extend(after)
        cov = self.covered[eng]
        for d, is_raw in [(d, True) for d in raw] + [(d, False) for d in other]:
            if d is rec:
                continue
            if d.eng == eng and eng == "tensor" and not dma and d.amount == 1:
                continue
            if d.eng == eng and not is_raw and not dma and d.amount == 1:
                continue
            if d.idx <= cov[d.eng]:
                continue
            d.signal = True
            rec.edges.append(d)
            cov[d.eng] = max(cov[d.eng], d.idx)
        plan.append(rec)
        for t in reads:
            self.reader.setdefault(t, {})[eng] = rec
        for t in writes:
            self.writer.setdefault(t, {})[eng] = rec
            self.reader[t] = {eng: rec}
        return rec

    def barrier(self):
        lasts = [self.plans[e][-1] for e in ENGINES if self.plans[e]]
        for e in ENGINES:
            if self.plans[e]:
                self.op(e, lambda eng: eng.nop(), after=tuple(lasts), name="barrier")

    def finalize_and_emit(self):
        for e in ENGINES:
            tot = 0
            for rec in self.plans[e]:
                if rec.signal:
                    tot += rec.amount
                    rec.sig_val = tot
        nc = self.nc
        sig = self.sig
        with nc.Block() as blk:
            for ename in ENGINES:
                plan = self.plans[ename]
                if not plan:
                    continue

                def make_runner(plan=plan):
                    def run(eng):
                        for rec in plan:
                            if rec.edges:
                                waits = {}
                                for d in rec.edges:
                                    waits[d.eng] = max(waits.get(d.eng, 0), d.sig_val)
                                for e2, v in waits.items():
                                    eng.wait_ge(sig[e2], v)
                            inst = rec.fn(eng)
                            if rec.signal:
                                assert inst is not None, f"signaling rec {rec.n} has no inst"
                                inst.then_inc(sig[rec.eng], rec.amount)
                    return run

                getattr(blk, ename)(make_runner())


def _alibi_slopes(n_heads):
    start = 2 ** (-(2 ** -(math.log2(n_heads) - 3)))
    return [start * start ** i for i in range(n_heads)]


def _make_consts():
    id16 = np.eye(128, dtype=np.float32).astype(BF)
    id32 = np.eye(128, dtype=np.float32)
    j = np.arange(128)
    mneg = np.where(j[None, :] <= j[:, None], 0.0, NEGBIG).astype(np.float32)

    slopes = _alibi_slopes(H)
    ktmpl = np.zeros((128, S), dtype=np.float32)
    qtmpl = np.zeros((128, S), dtype=np.float32)
    khi = (np.arange(S) >> 8).astype(np.float32)
    klo = (np.arange(S) & 255).astype(np.float32)
    qth = (np.arange(S) >> 7).astype(np.float32)
    qtl = (np.arange(S) & 127).astype(np.float32)
    for h in range(H):
        sl = np.float64(slopes[h])
        sa = np.float32(BF(np.float32(sl)))
        sb = np.float32(BF(np.float32(sl) - sa))
        base = 8 * h
        ktmpl[base + 0] = khi
        ktmpl[base + 1] = khi
        ktmpl[base + 2] = klo
        ktmpl[base + 3] = klo
        ktmpl[base + 4] = -sa * 128.0
        ktmpl[base + 5] = -sb * 128.0
        ktmpl[base + 6] = -sa
        ktmpl[base + 7] = -sb
        qtmpl[base + 0] = sa * 256.0
        qtmpl[base + 1] = sb * 256.0
        qtmpl[base + 2] = sa
        qtmpl[base + 3] = sb
        qtmpl[base + 4] = qth
        qtmpl[base + 5] = qth
        qtmpl[base + 6] = qtl
        qtmpl[base + 7] = qtl
    maskx = np.zeros((128, 1152), dtype=np.float32)
    maskx[:, 1024:] = mneg
    return id16, id32, mneg, maskx, ktmpl.astype(BF), qtmpl.astype(BF)


def build_program(debug=False):
    stack = ExitStack()
    nc = bass.Bass("TRN2", target_bir_lowering=False)

    dx = nc.dram_tensor("x", [T, D], F32, kind="ExternalInput")
    dwq = nc.dram_tensor("wqt", [D, D], F32, kind="ExternalInput")
    dwk = nc.dram_tensor("wkt", [D, D], F32, kind="ExternalInput")
    dwv = nc.dram_tensor("wvt", [D, D], F32, kind="ExternalInput")
    dwo = nc.dram_tensor("wot", [D, D], F32, kind="ExternalInput")
    dbq = nc.dram_tensor("bq_cols", [128, 8], F32, kind="ExternalInput")
    dbk = nc.dram_tensor("bk_cols", [128, 8], F32, kind="ExternalInput")
    dbvb = nc.dram_tensor("bv_bc", [128, D], F32, kind="ExternalInput")
    dbob = nc.dram_tensor("bo_bc", [128, D], F32, kind="ExternalInput")
    dgw = nc.dram_tensor("gw_cols", [128, 64], F32, kind="ExternalInput")
    dgbb = nc.dram_tensor("gb_bc", [128, 8], F32, kind="ExternalInput")
    dw1 = nc.dram_tensor("w1", [NEXP, D, FF], F32, kind="ExternalInput")
    db1 = nc.dram_tensor("b1_cols", [128, 256], F32, kind="ExternalInput")
    dw2 = nc.dram_tensor("w2", [NEXP, FF, D], F32, kind="ExternalInput")
    db2 = nc.dram_tensor("b2", [NEXP, D], F32, kind="ExternalInput")
    did16 = nc.dram_tensor("c_id16", [128, 128], BF16, kind="ExternalInput")
    did32 = nc.dram_tensor("c_id32", [128, 128], F32, kind="ExternalInput")
    dmneg = nc.dram_tensor("c_mneg", [128, 128], F32, kind="ExternalInput")
    dmaskx = nc.dram_tensor("c_maskx", [128, 1152], F32, kind="ExternalInput")
    dktm = nc.dram_tensor("c_ktmpl", [128, S], BF16, kind="ExternalInput")
    dqtm = nc.dram_tensor("c_qtmpl", [128, S], BF16, kind="ExternalInput")

    dout = nc.dram_tensor("out", [T, D], F32, kind="ExternalOutput")
    g_scr = nc.dram_tensor("g_scr", [T, D], F32, kind="ExternalOutput" if debug else "Internal")
    xn_scr = nc.dram_tensor("xn_scr", [8, 128, T], BF16, kind="ExternalOutput" if debug else "Internal")

    if debug:
        dbg_k = nc.dram_tensor("dbg_k", [128, 8 * S], BF16, kind="ExternalOutput")
        dbg_q = nc.dram_tensor("dbg_q", [128, 8 * S], BF16, kind="ExternalOutput")
        dbg_v = nc.dram_tensor("dbg_v", [128, 16 * 1024], BF16, kind="ExternalOutput")
        dbg_y = nc.dram_tensor("dbg_y", [128, 8 * 1024], BF16, kind="ExternalOutput")

    b = Builder(nc, stack)

    id16 = stack.enter_context(nc.sbuf_tensor("id16", [128, 128], BF16))
    id32 = stack.enter_context(nc.sbuf_tensor("id32", [128, 128], F32))
    mneg = stack.enter_context(nc.sbuf_tensor("mneg", [128, 128], F32))
    ktm = stack.enter_context(nc.sbuf_tensor("ktm", [128, S], BF16))
    qtm = stack.enter_context(nc.sbuf_tensor("qtm", [128, S], BF16))
    bq8 = stack.enter_context(nc.sbuf_tensor("bq8", [128, 8], F32))
    bkc = stack.enter_context(nc.sbuf_tensor("bkc", [128, 8], F32))
    comb = stack.enter_context(nc.sbuf_tensor("comb", [128, NT * 8], F32))
    epsv = stack.enter_context(nc.sbuf_tensor("epsv", [128, 1], F32))
    b.op("vector", lambda e: e.memset(epsv[:, :], 1e-5), writes=("epsv",), name="eps")

    b.op("sync", lambda e: e.dma_start(out=id16[:, :], in_=did16[:, :]), writes=("id16",), dma=True, name="l_id16")
    b.op("sync", lambda e: e.dma_start(out=id32[:, :], in_=did32[:, :]), writes=("id32",), dma=True, name="l_id32")
    b.op("sync", lambda e: e.dma_start(out=mneg[:, :], in_=dmneg[:, :]), writes=("mneg",), dma=True, name="l_mneg")
    b.op("sync", lambda e: e.dma_start(out=ktm[:, :], in_=dktm[:, :]), writes=("ktm",), dma=True, name="l_ktm")
    b.op("sync", lambda e: e.dma_start(out=qtm[:, :], in_=dqtm[:, :]), writes=("qtm",), dma=True, name="l_qtm")
    b.op("sync", lambda e: e.dma_start(out=bkc[:, :], in_=dbk[:, :]), writes=("bkc",), dma=True, name="l_bk")
    b.op("sync", lambda e: e.dma_start(out=bq8[:, :], in_=dbq[:, :]), writes=("bq8",), dma=True, name="l_bq")
    b.op("vector", lambda e: e.tensor_scalar(bq8[:, :], bq8[:, :], 0.125, None, op0=ALU.mult),
         reads=("bq8",), writes=("bq8",), name="bq8s")

    with ExitStack() as scA:
        combT = scA.enter_context(nc.sbuf_tensor("combT", [8, NT * 128], F32))
        w1b0_pre = scA.enter_context(nc.sbuf_tensor("w1b0", [128, 4 * 4096], BF16))

        with ExitStack() as sc:
            sb0 = lambda n, s, d: sc.enter_context(nc.sbuf_tensor(n, s, d))
            ps0 = lambda n, s, d: sc.enter_context(nc.psum_tensor(n, s, d))
            xnT = sb0("xnT", [128, 8 * T], BF16)
            gw = sb0("gw", [128, 64], F32)
            gbb = sb0("gbb", [128, 8], F32)
            xb = [sb0(f"xb{i}", [128, D], F32) for i in range(2)]
            xn32 = [sb0(f"xn32_{i}", [128, D], F32) for i in range(2)]
            xn16 = [sb0(f"xn16_{i}", [128, D], BF16) for i in range(2)]
            xT32 = [sb0(f"xT32_{i}", [128, D], F32) for i in range(2)]
            junk = sb0("junk", [128, D], BF16)
            nsum = sb0("nsum", [128, 1], F32)
            negmean = sb0("negmean", [128, 1], F32)
            msq = sb0("msq", [128, 1], F32)
            var = sb0("var", [128, 1], F32)
            lnv = sb0("lnv", [128, 1], F32)
            rstd = sb0("rstd", [128, 1], F32)
            ssq = sb0("ssq", [128, 1], F32)
            logit = sb0("logit", [128, 8], F32)
            m1 = sb0("m1", [128, 1], F32)
            nm1 = sb0("nm1", [128, 1], F32)
            ew = sb0("ew", [128, 8], F32)
            ind1 = sb0("ind1", [128, 8], F32)
            wm = sb0("wm", [128, 8], F32)
            m2 = sb0("m2", [128, 1], F32)
            indt = sb0("indt", [128, 8], F32)
            cu = sb0("cu", [128, 8], F32)
            dsum = sb0("dsum", [128, 1], F32)
            rec_ = sb0("rec_", [128, 1], F32)
            pp16 = [ps0(f"pp16_{i}", [128, 1024], BF16) for i in range(2)]
            pp32 = [ps0(f"pp32_{i}", [128, 1024], F32) for i in range(2)]
            psg = ps0("psg", [128, 8], F32)
            psct = ps0("psct", [8, 512], F32)

            b.op("sync", lambda e: e.dma_start(out=gw[:, :], in_=dgw[:, :]), writes=("gw",), dma=True, name="l_gw")
            b.op("sync", lambda e: e.dma_start(out=gbb[:, :], in_=dgbb[:, :]), writes=("gbb",), dma=True, name="l_gbb")

            for i in range(NT):
                par = i % 2
                t0 = i * 128
                xbp, xn32p, xn16p, xT32p = xb[par], xn32[par], xn16[par], xT32[par]
                nx, n32, n16, nT32 = f"xb{par}", f"xn32_{par}", f"xn16_{par}", f"xT32_{par}"
                pp16p, pp32p = pp16[par], pp32[par]
                n_pp16, n_pp32 = f"pp16_{par}", f"pp32_{par}"

                b.op("sync", lambda e, d=xbp, s=t0: e.dma_start(out=d[:, :], in_=dx[s:s + 128, :]),
                     writes=(nx,), dma=True, name=f"ldx{i}")
                b.op("vector", lambda e, x=xbp: e.reduce_sum(nsum[:, :], x[:, :], axis=AX.X, negate=True),
                     reads=(nx,), writes=("nsum",), name=f"su{i}")
                b.op("scalar", lambda e, x=xbp: e.activation(junk[:, :], x[:, :], AF.Square, accum_out=ssq[:, :]),
                     reads=(nx,), writes=("junk", "ssq"), name=f"sq{i}")
                b.op("vector", lambda e: e.tensor_scalar(negmean[:, :], nsum[:, :], 1.0 / D, None, op0=ALU.mult),
                     reads=("nsum",), writes=("negmean",), name=f"nm{i}")
                b.op("vector", lambda e: e.tensor_mul(msq[:, :], negmean[:, :], negmean[:, :]),
                     reads=("negmean",), writes=("msq",), name=f"ms{i}")
                b.op("vector", lambda e: e.tensor_scalar(var[:, :], ssq[:, :], 1.0 / D, msq[:, :],
                                                         op0=ALU.mult, op1=ALU.subtract),
                     reads=("ssq", "msq"), writes=("var",), name=f"va{i}")
                b.op("scalar", lambda e: e.activation(lnv[:, :], var[:, :], AF.Ln, bias=epsv[:, :]),
                     reads=("var", "epsv"), writes=("lnv",), name=f"ln{i}")
                b.op("scalar", lambda e: e.activation(rstd[:, :], lnv[:, :], AF.Exp, scale=-0.5),
                     reads=("lnv",), writes=("rstd",), name=f"rs{i}")
                b.op("vector", lambda e, x=xbp, o=xn32p: e.tensor_scalar(
                        o[:, :], x[:, :], negmean[:, :], rstd[:, :], op0=ALU.add, op1=ALU.mult),
                     reads=(nx, "negmean", "rstd"), writes=(n32,), name=f"xn{i}")
                b.op("vector", lambda e, s=xn32p, o=xn16p: e.tensor_copy(o[:, :], s[:, :]),
                     reads=(n32,), writes=(n16,), name=f"cst{i}")

                def tp16(e, src=xn16p, dst=pp16p):
                    inst = None
                    for c in range(8):
                        inst = e.matmul(dst[:, c * 128:(c + 1) * 128], src[:, c * 128:(c + 1) * 128],
                                        id16[:, :], start=(c == 0), stop=(c == 7), is_transpose=True)
                    return inst
                b.op("tensor", tp16, reads=(n16, "id16"), writes=(n_pp16,), name=f"t16_{i}")

                def tp32(e, src=xn32p, dst=pp32p):
                    inst = None
                    for c in range(8):
                        inst = e.matmul(dst[:, c * 128:(c + 1) * 128], src[:, c * 128:(c + 1) * 128],
                                        id32[:, :], start=(c in (0, 4)), stop=(c in (3, 7)), is_transpose=True)
                    return inst
                b.op("tensor", tp32, reads=(n32, "id32"), writes=(n_pp32,), name=f"t32_{i}")

                def ev16(e, src=pp16p, t0=t0):
                    dst = bass.AP(xnT, t0, [[8 * T, 128], [T, 8], [1, 128]])
                    return e.tensor_copy(dst, src[:, :])
                b.op("vector", ev16, reads=(n_pp16,), writes=("xnT",), name=f"e16_{i}")
                b.op("vector", lambda e, src=pp32p, dst=xT32p: e.tensor_copy(dst[:, :], src[:, :]),
                     reads=(n_pp32,), writes=(nT32,), name=f"e32_{i}")

                def gmm(e, src=xT32p):
                    inst = None
                    for c in range(8):
                        inst = e.matmul(psg[:, :], src[:, c * 128:(c + 1) * 128],
                                        gw[:, c * 8:(c + 1) * 8], start=(c == 0), stop=(c == 7))
                    return inst
                b.op("tensor", gmm, reads=(nT32, "gw"), writes=("psg",), name=f"gm{i}")
                b.op("vector", lambda e: e.tensor_add(logit[:, :], psg[:, :], gbb[:, :]),
                     reads=("psg", "gbb"), writes=("logit",), name=f"lg{i}")
                b.op("vector", lambda e: e.reduce_max(m1[:, :], logit[:, :], axis=AX.X),
                     reads=("logit",), writes=("m1",), name=f"m1_{i}")
                b.op("vector", lambda e: e.tensor_scalar(nm1[:, :], m1[:, :], -1.0, None, op0=ALU.mult),
                     reads=("m1",), writes=("nm1",), name=f"nm1_{i}")
                b.op("scalar", lambda e: e.activation(ew[:, :], logit[:, :], AF.Exp, bias=nm1[:, :]),
                     reads=("logit", "nm1"), writes=("ew",), name=f"ew{i}")
                b.op("vector", lambda e: e.tensor_scalar(ind1[:, :], logit[:, :], m1[:, :], None, op0=ALU.is_ge),
                     reads=("logit", "m1"), writes=("ind1",), name=f"i1_{i}")
                b.op("vector", lambda e: e.tensor_scalar(ind1[:, :], ind1[:, :], -1.0, 1.0,
                                                         op0=ALU.mult, op1=ALU.add),
                     reads=("ind1",), writes=("ind1",), name=f"ni{i}")
                b.op("vector", lambda e: e.tensor_mul(wm[:, :], ew[:, :], ind1[:, :]),
                     reads=("ew", "ind1"), writes=("wm",), name=f"wm{i}")
                b.op("vector", lambda e: e.reduce_max(m2[:, :], wm[:, :], axis=AX.X),
                     reads=("wm",), writes=("m2",), name=f"m2_{i}")
                b.op("vector", lambda e: e.tensor_scalar(indt[:, :], ew[:, :], m2[:, :], None, op0=ALU.is_ge),
                     reads=("ew", "m2"), writes=("indt",), name=f"it{i}")
                b.op("vector", lambda e: e.tensor_mul(cu[:, :], ew[:, :], indt[:, :]),
                     reads=("ew", "indt"), writes=("cu",), name=f"cu{i}")
                b.op("vector", lambda e: e.reduce_sum(dsum[:, :], cu[:, :], axis=AX.X),
                     reads=("cu",), writes=("dsum",), name=f"ds{i}")
                b.op("vector", lambda e: e.tensor_scalar(dsum[:, :], dsum[:, :], 1e-6, None, op0=ALU.max),
                     reads=("dsum",), writes=("dsum",), name=f"dm{i}")
                b.op("vector", lambda e: e.reciprocal(rec_[:, :], dsum[:, :]),
                     reads=("dsum",), writes=("rec_",), name=f"rc{i}")
                b.op("vector", lambda e, i=i: e.tensor_scalar(comb[:, i * 8:(i + 1) * 8], cu[:, :],
                                                              rec_[:, :], None, op0=ALU.mult),
                     reads=("cu", "rec_"), writes=("comb",), name=f"cb{i}")

                def ctp(e, i=i):
                    return e.matmul(psct[0:8, (i % 4) * 128:(i % 4) * 128 + 128],
                                    comb[:, i * 8:(i + 1) * 8], id32[:, :],
                                    start=(i % 4 == 0), stop=(i % 4 == 3), is_transpose=True)
                b.op("tensor", ctp, reads=("comb", "id32"), writes=("psct",), name=f"ct{i}")
                if i % 4 == 3:
                    b.op("vector", lambda e, i=i: e.tensor_copy(combT[0:8, (i - 3) * 128:(i + 1) * 128],
                                                                psct[0:8, :]),
                         reads=("psct",), writes=("combT",), name=f"ce{i}")

            for c in range(8):
                b.op("sync", lambda e, c=c: e.dma_start(out=xn_scr[c, :, :], in_=xnT[:, c * T:(c + 1) * T]),
                     reads=("xnT",), writes=("xn_scr",), dma=True, name=f"sp{c}")

            for fc in range(4):
                w1src = dw1[0, :, fc * 512:(fc + 1) * 512].rearrange("(c p) f -> p c f", p=128)
                w1dst = bass.AP(w1b0_pre, fc * 4096, [[4 * 4096, 128], [512, 8], [1, 512]])
                b.op("gpsimd", lambda e, d=w1dst, s=w1src: e.dma_start(out=d, in_=s),
                     writes=("w1b0",), dma=True, name=f"w1l0_{fc}")

        b.barrier()

        with ExitStack() as sc:
            sb1 = lambda n, s, d: sc.enter_context(nc.sbuf_tensor(n, s, d))
            ps1 = lambda n, s, d: sc.enter_context(nc.psum_tensor(n, s, d))
            w1b = [w1b0_pre, sb1("w1b1", [128, 4 * 4096], BF16)]
            w2h = sb1("w2h", [128, 16 * 1024], BF16)
            hdnT = [sb1(f"hdnT{i}", [128, 16 * 512], BF16) for i in range(2)]
            xnb1 = [sb1(f"xnb1_{i}", [128, 8 * 512], BF16) for i in range(2)]
            b1c = sb1("b1c", [128, 256], F32)
            b2sb = sb1("b2sb", [8, D], F32)
            gout = [sb1(f"gout{i}", [128, D], F32) for i in range(2)]
            psa = [ps1(f"psa{i}", [128, 512], F32) for i in range(2)]
            psb = [ps1(f"psb{i}", [128, 1024], F32) for i in range(2)]

            b.op("sync", lambda e: e.dma_start(out=b1c[:, :], in_=db1[:, :]), writes=("b1c",), dma=True, name="l_b1")
            b.op("sync", lambda e: e.dma_start(out=b2sb[:, :], in_=db2[:, :]), writes=("b2sb",), dma=True, name="l_b2")

            for gt in range(NT):
                go, ngo = gout[gt % 2], f"gout{gt % 2}"
                pb, npb = psb[gt % 2], f"psb{gt % 2}"

                def b2mm(e, gt=gt, pb=pb):
                    e.matmul(pb[:, 0:512], combT[0:8, gt * 128:(gt + 1) * 128],
                             b2sb[0:8, 0:512], start=True, stop=True)
                    return e.matmul(pb[:, 512:1024], combT[0:8, gt * 128:(gt + 1) * 128],
                                    b2sb[0:8, 512:1024], start=True, stop=True)
                b.op("tensor", b2mm, reads=("combT", "b2sb"), writes=(npb,), name=f"b2m{gt}")
                b.op("vector", lambda e, pb=pb, go=go: e.tensor_copy(go[:, :], pb[:, :]),
                     reads=(npb,), writes=(ngo,), name=f"b2c{gt}")
                b.op("gpsimd", lambda e, go=go, gt=gt: e.dma_start(
                        out=g_scr[gt * 128:(gt + 1) * 128, :], in_=go[:, :]),
                     reads=(ngo,), writes=(f"g{gt}",), dma=True, name=f"b2s{gt}")

            def load_w1(rnd):
                ex, fh = rnd // 2, rnd % 2
                wp, nwp = w1b[rnd % 2], f"w1b{rnd % 2}"
                for fc in range(4):
                    src = dw1[ex, :, fh * 2048 + fc * 512: fh * 2048 + (fc + 1) * 512] \
                        .rearrange("(c p) f -> p c f", p=128)
                    dst = bass.AP(wp, fc * 4096, [[4 * 4096, 128], [512, 8], [1, 512]])
                    b.op("gpsimd", lambda e, d=dst, s=src: e.dma_start(out=d, in_=s),
                         writes=(nwp,), dma=True, name=f"w1l{rnd}_{fc}")

            for rnd in range(16):
                ex, fh = rnd // 2, rnd % 2
                w1p, nw1 = w1b[rnd % 2], f"w1b{rnd % 2}"
                for fc in range(4):
                    src = dw2[ex, fh * 2048 + fc * 512: fh * 2048 + (fc + 1) * 512, :] \
                        .rearrange("(c p) h -> p c h", p=128)
                    dst = bass.AP(w2h, fc * 4 * 1024, [[16 * 1024, 128], [1024, 4], [1, 1024]])
                    b.op("gpsimd", lambda e, d=dst, s=src: e.dma_start(out=d, in_=s),
                         writes=("w2h",), dma=True, name=f"w2l{rnd}_{fc}")
                if rnd + 1 < 16:
                    load_w1(rnd + 1)

                for tg in range(8):
                    tpos = tg * 512
                    xp, nxp = xnb1[tg % 2], f"xnb1_{tg % 2}"
                    src = xn_scr[:, :, tpos:tpos + 512].rearrange("c p t -> p c t")
                    b.op("sync", lambda e, d=xp, s=src: e.dma_start(
                            out=d.ap().rearrange("p (c t) -> p c t", c=8), in_=s),
                         reads=("xn_scr",), writes=(nxp,), dma=True, name=f"xl{rnd}_{tg}")

                    for fb in range(16):
                        fc, fs = fb // 4, fb % 4
                        pa, npa = psa[fb % 2], f"psa{fb % 2}"

                        def amm(e, w1p=w1p, fc=fc, fs=fs, pa=pa, xp=xp):
                            inst = None
                            for dc in range(8):
                                inst = e.matmul(
                                    pa[:, :],
                                    w1p[:, fc * 4096 + dc * 512 + fs * 128: fc * 4096 + dc * 512 + fs * 128 + 128],
                                    xp[:, dc * 512:(dc + 1) * 512],
                                    start=(dc == 0), stop=(dc == 7))
                            return inst
                        b.op("tensor", amm, reads=(nw1, nxp), writes=(npa,), name=f"am{rnd}_{tg}_{fb}")

                        bcol = ex * 32 + fh * 16 + fb
                        hb, nhb = hdnT[tg % 2], f"hdnT{tg % 2}"
                        b.op("scalar", lambda e, pa=pa, fb=fb, bcol=bcol, hb=hb: e.activation(
                                hb[:, fb * 512:(fb + 1) * 512], pa[:, :], AF.Gelu,
                                bias=b1c[:, bcol:bcol + 1]),
                             reads=(npa, "b1c"), writes=(nhb,), name=f"ge{rnd}_{tg}_{fb}")

                    for tl in range(4):
                        gt = tg * 4 + tl
                        pb, npb = psb[tl % 2], f"psb{tl % 2}"
                        go, ngo = gout[tl % 2], f"gout{tl % 2}"

                        hb, nhb = hdnT[tg % 2], f"hdnT{tg % 2}"
                        def bmm(e, tl=tl, pb=pb, hb=hb):
                            inst = None
                            for fb in range(16):
                                lhs = hb[:, fb * 512 + tl * 128: fb * 512 + tl * 128 + 128]
                                inst = e.matmul(pb[:, 0:512], lhs, w2h[:, fb * 1024: fb * 1024 + 512],
                                                start=(fb == 0), stop=(fb == 15))
                                inst = e.matmul(pb[:, 512:1024], lhs, w2h[:, fb * 1024 + 512: fb * 1024 + 1024],
                                                start=(fb == 0), stop=(fb == 15))
                            return inst
                        b.op("tensor", bmm, reads=(nhb, "w2h"), writes=(npb,), name=f"bm{rnd}_{gt}")
                        b.op("vector", lambda e, pb=pb, go=go, gt=gt, ex=ex: e.tensor_scalar(
                                go[:, :], pb[:, :], comb[:, gt * 8 + ex: gt * 8 + ex + 1], None, op0=ALU.mult),
                             reads=(npb, "comb"), writes=(ngo,), name=f"be{rnd}_{gt}")
                        b.op("gpsimd", lambda e, go=go, gt=gt: e.dma_start(
                                out=g_scr[gt * 128:(gt + 1) * 128, :], in_=go[:, :], accum_op=ALU.add),
                             reads=(ngo,), writes=(f"g{gt}",), dma=True, name=f"ga{rnd}_{gt}")

        b.barrier()

    with ExitStack() as sc:
        sb2 = lambda n, s, d: sc.enter_context(nc.sbuf_tensor(n, s, d))
        kT = sb2("kT", [128, 8 * S], BF16)
        qT2 = sb2("qT2", [128, 8 * S], BF16)
        v_tm = sb2("v_tm", [128, 16 * 1024], BF16)
        yT = sb2("yT", [128, 8 * 1024], BF16)
        bob = sb2("bob", [128, D], F32)
        lcur = [sb2(f"lcur{i}", [128, 1], F32) for i in range(2)]
        rl2 = [sb2(f"rl2_{i}", [128, 1], F32) for i in range(2)]
        mstat = [sb2(f"mstat{i}", [128, 3], F32) for i in range(2)]
        lc2 = [sb2(f"lc2_{i}", [128, 2], F32) for i in range(2)]
        negm = [sb2(f"negm{i}", [128, 1], F32) for i in range(2)]
        stmp = sb2("stmp", [128, D], F32)
        sgb = sb2("sgb", [128, D], F32)

        b.op("sync", lambda e: e.dma_start(out=bob[:, :], in_=dbob[:, :]), writes=("bob",), dma=True, name="l_bob")

        for sq in range(SEQS):
            soff = sq * S
            with ExitStack() as sp:
                sbp = lambda n, s, d: sp.enter_context(nc.sbuf_tensor(n, s, d))
                psp = lambda n, s, d: sp.enter_context(nc.psum_tensor(n, s, d))
                wkb = sbp(f"wkb{sq}", [128, 8 * 1024], BF16)
                wqb = sbp(f"wqb{sq}", [128, 8 * 1024], BF16)
                wvb = sbp(f"wvb{sq}", [128, 8 * 1024], BF16)
                xnb = [sbp(f"xnb2_{sq}_{i}", [128, 8 * 512], BF16) for i in range(2)]
                bvb = sbp(f"bvb{sq}", [128, D], F32)
                pko = [psp(f"pko{sq}_{i}", [128, 512], F32) for i in range(2)]
                b.op("sync", lambda e, bvb=bvb: e.dma_start(out=bvb[:, :], in_=dbvb[:, :]),
                     writes=(f"bvb{sq}",), dma=True, name=f"l_bvb{sq}")
                psv = psp(f"psv{sq}", [128, 1024], F32)

                for wdst, wsrc, wname in ((wkb, dwk, f"wkb{sq}"), (wqb, dwq, f"wqb{sq}"), (wvb, dwv, f"wvb{sq}")):
                    src = wsrc[:, :].rearrange("(c p) i -> p c i", p=128)
                    b.op("gpsimd", lambda e, dd=wdst, s=src: e.dma_start(
                            out=dd.ap().rearrange("p (c i) -> p c i", c=8), in_=s),
                         writes=(wname,), dma=True, name=f"lw_{wname}")

                for tc in range(4):
                    xp, nxp = xnb[tc % 2], f"xnb2_{sq}_{tc % 2}"
                    src = xn_scr[:, :, soff + tc * 512: soff + (tc + 1) * 512].rearrange("c p t -> p c t")
                    b.op("sync", lambda e, d=xp, s=src: e.dma_start(
                            out=d.ap().rearrange("p (c t) -> p c t", c=8), in_=s),
                         reads=("xn_scr",), writes=(nxp,), dma=True, name=f"x2{sq}_{tc}")

                    for proj, wb, nwb, dstT, ndT, bias, scale in (
                            ("k", wkb, f"wkb{sq}", kT, "kT", bkc, 1.0),
                            ("q", wqb, f"wqb{sq}", qT2, "qT2", bq8, 0.125)):
                        for ic in range(8):
                            pk, npk = pko[ic % 2], f"pko{sq}_{ic % 2}"

                            def pmm(e, xp=xp, ic=ic, pk=pk, wb=wb):
                                inst = None
                                for dc in range(8):
                                    inst = e.matmul(pk[:, :],
                                                    wb[:, dc * 1024 + ic * 128: dc * 1024 + ic * 128 + 128],
                                                    xp[:, dc * 512:(dc + 1) * 512],
                                                    start=(dc == 0), stop=(dc == 7))
                                return inst
                            b.op("tensor", pmm, reads=(nwb, nxp), writes=(npk,), name=f"{proj}m{sq}_{tc}_{ic}")
                            b.op("scalar", lambda e, pk=pk, ic=ic, tc=tc, dstT=dstT, bias=bias, scale=scale:
                                    e.activation(dstT[:, ic * S + tc * 512: ic * S + (tc + 1) * 512], pk[:, :],
                                                 AF.Identity, bias=bias[:, ic:ic + 1], scale=scale),
                                 reads=(npk, "bkc", "bq8"), writes=(ndT,), name=f"{proj}e{sq}_{tc}_{ic}")

                    for tl in range(4):
                        tj = tc * 4 + tl

                        def vmm(e, xp=xp, tl=tl, wb=wvb):
                            inst = None
                            for dc in range(8):
                                lhs = xp[:, dc * 512 + tl * 128: dc * 512 + tl * 128 + 128]
                                inst = e.matmul(psv[:, 0:512], lhs, wb[:, dc * 1024: dc * 1024 + 512],
                                                start=(dc == 0), stop=(dc == 7))
                                inst = e.matmul(psv[:, 512:1024], lhs, wb[:, dc * 1024 + 512: dc * 1024 + 1024],
                                                start=(dc == 0), stop=(dc == 7))
                            return inst
                        b.op("tensor", vmm, reads=(f"wvb{sq}", nxp), writes=(f"psv{sq}",), name=f"vm{sq}_{tc}_{tl}")
                        b.op("vector", lambda e, tj=tj: e.scalar_tensor_tensor(
                                v_tm[:, tj * 1024:(tj + 1) * 1024], psv[:, :], 1.0, bvb[:, :],
                                op0=ALU.mult, op1=ALU.add),
                             reads=(f"psv{sq}", f"bvb{sq}"), writes=("v_tm",), name=f"ve{sq}_{tc}_{tl}")

            if debug and sq == 0:
                b.op("sync", lambda e: e.dma_start(out=dbg_k[:, :], in_=kT[:, :]),
                     reads=("kT",), dma=True, name="dbgk")
                b.op("sync", lambda e: e.dma_start(out=dbg_q[:, :], in_=qT2[:, :]),
                     reads=("qT2",), dma=True, name="dbgq")
                b.op("sync", lambda e: e.dma_start(out=dbg_v[:, :], in_=v_tm[:, :]),
                     reads=("v_tm",), dma=True, name="dbgv")
            b.barrier()

            with ExitStack() as sp:
                sbp = lambda n, s, d: sp.enter_context(nc.sbuf_tensor(n, s, d))
                psp = lambda n, s, d: sp.enter_context(nc.psum_tensor(n, s, d))
                wob = sbp(f"wob{sq}", [128, 8 * 1024], BF16)
                kqa = [sbp(f"kqa{sq}_{i}", [72, S], BF16) for i in range(2)]
                qaug = [sbp(f"qaug{sq}_{i}", [72, 1024], BF16) for i in range(2)]
                Ebuf = [sbp(f"Eb{sq}_{i}", [128, S], BF16) for i in range(3)]
                ETb = [sbp(f"ET{sq}_{i}", [128, 16 * 128], BF16) for i in range(2)]
                sbs = [sbp(f"sbs{sq}_{i}", [128, S], F32) for i in range(2)]
                gti = [sbp(f"gti{sq}_0", [128, D], F32)] * 2
                xti = [sbp(f"xti{sq}_0", [128, D], F32)] * 2
                pssp = [psp(f"pss{sq}_{i}", [128, 1024], F32) for i in range(2)]
                pset0 = psp(f"pset{sq}", [128, 1024], BF16)
                psy = psp(f"psy{sq}", [64, 128], F32)
                pso = psp(f"pso{sq}", [128, 1024], F32)
                pset1 = pso.ap().bitcast(BF16)[:, 0:1024]
                pset = [pset0, pset1]
                npsets = [f"pset{sq}", f"pso{sq}"]

                src = dwo[:, :].rearrange("(c p) i -> p c i", p=128)
                b.op("gpsimd", lambda e, s=src: e.dma_start(
                        out=wob.ap().rearrange("p (c i) -> p c i", c=8), in_=s),
                     writes=(f"wob{sq}",), dma=True, name=f"lwo{sq}")

                gc = [0]
                for half in range(2):
                    stq = []
                    def flush_one():
                        d2, t1, t2 = stq.pop(0)
                        d2(); t1()
                        stq.append((None, None, t2))
                    for h in range(H):
                        hp = h % 2
                        kq, nkq = kqa[hp], f"kqa{sq}_{hp}"
                        qa, nqa = qaug[hp], f"qaug{sq}_{hp}"
                        b.op("sync", lambda e, kq=kq, h=h: e.dma_start(
                                out=kq[0:64, :],
                                in_=kT[(h % 2) * 64:(h % 2) * 64 + 64, (h // 2) * S:(h // 2 + 1) * S]),
                             reads=("kT",), writes=(nkq,), dma=True, name=f"bk{sq}_{half}_{h}")
                        b.op("sync", lambda e, kq=kq, h=h: e.dma_start(
                                out=kq[64:72, :], in_=ktm[8 * h:8 * h + 8, :]),
                             reads=("ktm",), writes=(nkq,), dma=True, name=f"bkt{sq}_{half}_{h}")
                        b.op("sync", lambda e, qa=qa, h=h, half=half: e.dma_start(
                                out=qa[0:64, :],
                                in_=qT2[(h % 2) * 64:(h % 2) * 64 + 64,
                                        (h // 2) * S + half * 1024:(h // 2) * S + half * 1024 + 1024]),
                             reads=("qT2",), writes=(nqa,), dma=True, name=f"bq{sq}_{half}_{h}")
                        b.op("sync", lambda e, qa=qa, h=h, half=half: e.dma_start(
                                out=qa[64:72, :], in_=qtm[8 * h:8 * h + 8, half * 1024: half * 1024 + 1024]),
                             reads=("qtm",), writes=(nqa,), dma=True, name=f"bqt{sq}_{half}_{h}")

                        for A in range(8):
                            Ag = half * 8 + A
                            W = (Ag + 1) * 128
                            ep_i = gc[0] % 3
                            et_i = gc[0] % 2
                            spar = A % 2
                            gc[0] += 1
                            Ep, nEp = Ebuf[ep_i], f"Eb{sq}_{ep_i}"
                            ETp, nETp = ETb[et_i], f"ET{sq}_{et_i}"
                            sbp_ = sbs[spar]
                            nsbs = f"sbs{sq}_{spar}"
                            nneg = f"negm{sq}_{spar}"
                            nms = f"mstat{sq}_{spar}"
                            tag = f"{sq}_{half}_{h}_{A}"
                            if Ag <= 7:
                                pieces = [(A % 2, 0, W)]
                            else:
                                pieces = [(A % 2, 0, 1024), ((A + 1) % 2, 1024, W)]
                            n512 = Ag // 4
                            rem = Ag % 4
                            chunks = [(c * 512, (c + 1) * 512, True, True) for c in range(n512)]
                            if rem:
                                chunks.append((n512 * 512, n512 * 512 + rem * 128, True, False))
                                chunks.append((W - 128, W, False, True))
                            else:
                                chunks.append((W - 128, W, True, True))

                            piece_info = []
                            for pi, (pb, lo, hi) in enumerate(pieces):
                                ps_ = pssp[pb]
                                nps = f"pss{sq}_{pb}"

                                def smm(e, qa=qa, kq=kq, A=A, ps_=ps_, lo=lo, hi=hi, chunks=tuple(chunks)):
                                    inst = None
                                    lhs = qa[0:72, A * 128:(A + 1) * 128]
                                    for gl, gh, st, sp in chunks:
                                        if gl < lo or gl >= hi:
                                            continue
                                        inst = e.matmul(ps_[:, gl - lo:gh - lo], lhs,
                                                        kq[0:72, gl:gh], start=st, stop=sp)
                                    return inst
                                b.op("tensor", smm, reads=(nqa, nkq), writes=(nps,), name=f"s{tag}_{pi}")
                                piece_info.append((pi, pb, lo, hi, ps_, nps))

                            mains = 0
                            for pi, pb, lo, hi, ps_, nps in piece_info:
                                hid = min(hi, W - 128)
                                if hid > lo:
                                    mc = mains
                                    mains += 1
                                    b.op("vector", lambda e, ps_=ps_, lo=lo, hid=hid, mc=mc, sbp_=sbp_,
                                                spar=spar: e.tensor_scalar(
                                            sbp_[:, lo:hid], ps_[:, 0:hid - lo], 1.0, None,
                                            op0=ALU.mult, op1=ALU.max,
                                            accum_out=mstat[spar][:, mc:mc + 1]),
                                         reads=(nps,), writes=(nsbs, nms), name=f"e{tag}_{pi}")
                                if hi == W:
                                    dc = mains
                                    b.op("vector", lambda e, ps_=ps_, lo=lo, W=W, sbp_=sbp_: e.scalar_tensor_tensor(
                                            sbp_[:, W - 128:W], ps_[:, W - 128 - lo:W - lo], 1.0, mneg[:, :],
                                            op0=ALU.mult, op1=ALU.add),
                                         reads=(nps, "mneg"), writes=(nsbs,), name=f"d{tag}")
                                    b.op("vector", lambda e, W=W, sbp_=sbp_, spar=spar, dc=dc: e.tensor_scalar(
                                            sbp_[:, W - 128:W], sbp_[:, W - 128:W], 1.0, None,
                                            op0=ALU.mult, op1=ALU.max,
                                            accum_out=mstat[spar][:, dc:dc + 1]),
                                         reads=(nsbs,), writes=(nsbs, nms), name=f"dm{tag}")
                            ncols = mains + 1
                            b.op("vector", lambda e, spar=spar, ncols=ncols: e.reduce_max(
                                    negm[spar][:, :], mstat[spar][:, 0:ncols], axis=AX.X, negate=True),
                                 reads=(nms,), writes=(nneg,), name=f"rm{tag}")
                            b.op("scalar", lambda e, Ep=Ep, W=W, sbp_=sbp_, spar=spar: e.activation(
                                    Ep[:, 0:W], sbp_[:, 0:W], AF.Exp, bias=negm[spar][:, :],
                                    accum_out=lcur[spar][:, :]),
                                 reads=(nsbs, nneg), writes=(nEp, f"lcur{sq}_{spar}"), name=f"x{tag}")

                            if len(stq) > 0:
                                mid = None
                                if stq and stq[0][0] is None:
                                    _, _, mid = stq.pop(0)
                                if stq:
                                    d2p, t1p, t2p = stq.pop(0)
                                    d2p()
                                    t1p(mid)
                                    stq.insert(0, (None, None, t2p))
                                elif mid is not None:
                                    mid()

                            def mk_d2(Ep=Ep, nEp=nEp, W=W, spar=spar, tag=tag):
                                def d2():
                                    b.op("vector", lambda e: e.reciprocal(rl2[spar][:, :], lcur[spar][:, :]),
                                         reads=(f"lcur{sq}_{spar}",), writes=(f"rl2_{sq}_{spar}",),
                                         name=f"rl{tag}")
                                    b.op("vector", lambda e: e.tensor_scalar(
                                            Ep[:, 0:W], Ep[:, 0:W], rl2[spar][:, :], None, op0=ALU.mult),
                                         reads=(nEp, f"rl2_{sq}_{spar}"), writes=(nEp,), name=f"en{tag}")
                                return d2

                            def mk_t1(Ep=Ep, nEp=nEp, ETp=ETp, nETp=nETp, Ag=Ag, tag=tag):
                                def t1(mid=None):
                                    ngroups = (Ag + 8) // 8
                                    for g in range(ngroups):
                                        if g == 1 and mid is not None:
                                            mid()
                                            mid = None
                                        j0, j1 = g * 8, min(g * 8 + 8, Ag + 1)
                                        psetg = pset[g % 2]
                                        npset = npsets[g % 2]

                                        def tpe(e, Ep=Ep, j0=j0, j1=j1, psetg=psetg):
                                            inst = None
                                            for j in range(j0, j1):
                                                inst = e.matmul(psetg[:, (j - j0) * 128:(j - j0 + 1) * 128],
                                                                Ep[:, j * 128:(j + 1) * 128], id16[:, :],
                                                                start=(j == j0), stop=(j == j1 - 1),
                                                                is_transpose=True)
                                            return inst
                                        b.op("tensor", tpe, reads=(nEp, "id16"), writes=(npset,),
                                             name=f"t{tag}_{g}")
                                        b.op("scalar", lambda e, ETp=ETp, j0=j0, j1=j1, psetg=psetg: e.copy(
                                                ETp[:, j0 * 128:j1 * 128], psetg[:, 0:(j1 - j0) * 128]),
                                             reads=(npset,), writes=(nETp,), name=f"u{tag}_{g}")
                                    if mid is not None:
                                        mid()
                                return t1

                            def mk_t2(ETp=ETp, nETp=nETp, h=h, A=A, Ag=Ag, tag=tag):
                                def t2():
                                    def avm(e, ETp=ETp, h=h, Ag=Ag):
                                        inst = None
                                        for jj in range(Ag + 1):
                                            inst = e.matmul(psy[0:64, :],
                                                            v_tm[:, jj * 1024 + h * 64: jj * 1024 + h * 64 + 64],
                                                            ETp[:, jj * 128:(jj + 1) * 128],
                                                            start=(jj == 0), stop=(jj == Ag))
                                        return inst
                                    b.op("tensor", avm, reads=(nETp, "v_tm"), writes=("psy",), name=f"a{tag}")
                                    b.op("vector", lambda e, h=h, A=A: e.tensor_copy(
                                            yT[(h % 2) * 64:(h % 2) * 64 + 64,
                                               (h // 2) * 1024 + A * 128:(h // 2) * 1024 + A * 128 + 128],
                                            psy[0:64, :]),
                                         reads=("psy",), writes=("yT",), name=f"y{tag}")
                                return t2

                            stq.append((mk_d2(), mk_t1(), mk_t2()))
                    while stq:
                        d2, t1, t2 = stq.pop(0)
                        if d2 is not None:
                            d2(); t1()
                        t2()

                    if debug and sq == 0 and half == 0:
                        b.op("sync", lambda e: e.dma_start(out=dbg_y[:, :], in_=yT[:, :]),
                             reads=("yT",), dma=True, name="dbgy")
                    for tl in range(8):
                        t0 = soff + half * 1024 + tl * 128
                        gp, ngp = gti[0], f"gti{sq}_0"
                        xp2, nxp2 = xti[0], f"xti{sq}_0"
                        ob, nob = xp2, nxp2
                        gt_glob = t0 // 128
                        b.op("sync", lambda e, gp=gp, t0=t0: e.dma_start(out=gp[:, :], in_=g_scr[t0:t0 + 128, :]),
                             reads=(f"g{gt_glob}",), writes=(ngp,), dma=True, name=f"lg{sq}_{half}_{tl}")
                        b.op("sync", lambda e, xp2=xp2, t0=t0: e.dma_start(out=xp2[:, :], in_=dx[t0:t0 + 128, :]),
                             writes=(nxp2,), dma=True, name=f"lx{sq}_{half}_{tl}")

                        def omm(e, tl=tl):
                            inst = None
                            for ic in range(8):
                                lhs = yT[:, ic * 1024 + tl * 128: ic * 1024 + tl * 128 + 128]
                                inst = e.matmul(pso[:, 0:512], lhs, wob[:, ic * 1024: ic * 1024 + 512],
                                                start=(ic == 0), stop=(ic == 7))
                                inst = e.matmul(pso[:, 512:1024], lhs, wob[:, ic * 1024 + 512: ic * 1024 + 1024],
                                                start=(ic == 0), stop=(ic == 7))
                            return inst
                        b.op("tensor", omm, reads=("yT", f"wob{sq}"), writes=(f"pso{sq}",), name=f"o{sq}_{half}_{tl}")
                        b.op("vector", lambda e: e.scalar_tensor_tensor(
                                stmp[:, :], pso[:, :], 1.0, bob[:, :], op0=ALU.mult, op1=ALU.add),
                             reads=(f"pso{sq}", "bob"), writes=("stmp",), name=f"se{sq}_{half}_{tl}")
                        b.op("scalar", lambda e, gp=gp: e.activation(gp[:, :], gp[:, :], AF.Silu),
                             reads=(ngp,), writes=(ngp,), name=f"si{sq}_{half}_{tl}")
                        b.op("vector", lambda e, gp=gp: e.tensor_mul(sgb[:, :], stmp[:, :], gp[:, :]),
                             reads=("stmp", ngp), writes=("sgb",), name=f"sg{sq}_{half}_{tl}")
                        b.op("vector", lambda e, xp2=xp2: e.tensor_add(xp2[:, :], sgb[:, :], xp2[:, :]),
                             reads=("sgb", nxp2), writes=(nxp2,), name=f"oa{sq}_{half}_{tl}")
                        b.op("sync", lambda e, ob=ob, t0=t0: e.dma_start(out=dout[t0:t0 + 128, :], in_=ob[:, :]),
                             reads=(nob,), writes=("out",), dma=True, name=f"so{sq}_{half}_{tl}")

            if sq + 1 < SEQS:
                b.barrier()

        final_out = b.writer["out"]["sync"]
        b.op("sync", lambda e: e.nop(), after=(final_out,), name="final")

    b.finalize_and_emit()
    stack.close()
    return nc


_cached = {}


def _get_program():
    if "nc" not in _cached:
        _cached["nc"] = build_program()
    return _cached["nc"]


def kernel(L0_attn_x, L0_attn_Wq, L0_attn_bq, L0_attn_Wk, L0_attn_bk,
           L0_attn_Wv, L0_attn_bv, L0_attn_Wo, L0_attn_bo,
           L0_ffn_gate_w, L0_ffn_gate_b, L0_ffn_w1, L0_ffn_b1,
           L0_ffn_w2, L0_ffn_b2):
    f32 = lambda a: np.ascontiguousarray(np.asarray(a, dtype=np.float32))
    x = f32(L0_attn_x)
    Wq, Wk, Wv, Wo = map(f32, (L0_attn_Wq, L0_attn_Wk, L0_attn_Wv, L0_attn_Wo))
    bq, bk, bv, bo = map(f32, (L0_attn_bq, L0_attn_bk, L0_attn_bv, L0_attn_bo))
    gw, gb = f32(L0_ffn_gate_w), f32(L0_ffn_gate_b)
    w1, b1, w2, b2 = map(f32, (L0_ffn_w1, L0_ffn_b1, L0_ffn_w2, L0_ffn_b2))

    id16, id32, mneg, maskx, ktmpl, qtmpl = _make_consts()
    cols = lambda v: np.ascontiguousarray(v.reshape(8, 128).T)
    common = dict(
        c_maskx=maskx,
        wqt=np.ascontiguousarray(Wq.T), wkt=np.ascontiguousarray(Wk.T),
        wvt=np.ascontiguousarray(Wv.T), wot=np.ascontiguousarray(Wo.T),
        bq_cols=cols(bq), bk_cols=cols(bk),
        bv_bc=np.ascontiguousarray(np.broadcast_to(bv, (128, D))),
        bo_bc=np.ascontiguousarray(np.broadcast_to(bo, (128, D))),
        gw_cols=np.ascontiguousarray(gw.reshape(8, 128, 8).transpose(1, 0, 2).reshape(128, 64)),
        gb_bc=np.ascontiguousarray(np.broadcast_to(gb, (128, 8))),
        w1=w1, w2=w2,
        b1_cols=np.ascontiguousarray(b1.reshape(8, 32, 128).transpose(2, 0, 1).reshape(128, 256)),
        b2=b2,
        c_id16=id16, c_id32=id32, c_mneg=mneg, c_ktmpl=ktmpl, c_qtmpl=qtmpl,
    )
    in_maps = []
    for core in range(8):
        m = dict(common)
        m["x"] = np.ascontiguousarray(x[core * SEQS:(core + 1) * SEQS].reshape(T, D))
        in_maps.append(m)

    nc = _get_program()
    res = run_bass_kernel_spmd(nc, in_maps, core_ids=list(range(8)))
    out = np.concatenate([r["out"].reshape(SEQS, S, D) for r in res.results], axis=0)
    return out
